# version 01
# baseline (speedup 1.0000x reference)
import numpy as np
from contextlib import ExitStack
import concourse.bass as bass
import concourse.mybir as mybir
from concourse.bass_utils import run_bass_kernel_spmd

F32 = mybir.dt.float32
BF16 = mybir.dt.bfloat16
I32 = mybir.dt.int32
AF = mybir.ActivationFunctionType
ALU = mybir.AluOpType
AX = mybir.AxisListType

NCORE = 8
B, T, D, II = 8, 4096, 1024, 3072
CPC = 128
QMAX = 32767.0
NTILE = T // 128

_CORES = list(range(NCORE))
ENGS = ("sync", "vector", "scalar", "tensor", "gpsimd")


def _ap(t, offset, pattern):
    return bass.AP(tensor=t.tensor, offset=offset, ap=pattern)


class Slots:

    def __init__(self, nc, st, name, n):
        self.sems = [st.enter_context(nc.semaphore(f"{name}{k}")) for k in range(n)]
        self.cnt = [0] * n
        self.n = n

    def nxt(self, i):
        s = i % self.n
        self.cnt[s] += 16
        return self.sems[s], self.cnt[s]

    def drains(self):
        return [(self.sems[k], self.cnt[k]) for k in range(self.n) if self.cnt[k]]


class Prog:

    def __init__(self, nc, st):
        self.nc = nc
        self.entries = {e: [] for e in ENGS}
        self.sems = {e: st.enter_context(nc.semaphore("ch_" + e)) for e in ENGS}
        self.counts = {e: 0 for e in ENGS}
        self.labels = {}

    def op(self, eng, fname, *a, waits=(), label=None, **kw):
        self.counts[eng] += 1
        if label is not None:
            assert label not in self.labels, label
            self.labels[label] = (eng, self.counts[eng])
        self.entries[eng].append(("op", tuple(waits), fname, a, kw))

    def dma(self, eng, slot, *a, waits=(), label=None, **kw):
        if label is not None:
            assert label not in self.labels, label
            self.labels[label] = ("dma", slot)
        self.entries[eng].append(("dma", tuple(waits), slot, a, kw))

    def wait_of(self, w):
        if isinstance(w, str):
            tgt = self.labels[w]
            if tgt[0] == "dma":
                return tgt[1]
            eng, cnt = tgt
            return self.sems[eng], cnt
        return w

    def emit(self, block):
        for eng in ENGS:
            if not self.entries[eng]:
                continue
            entries = self.entries[eng]
            sem = self.sems[eng]
            wait_of = self.wait_of

            def body(raw, entries=entries, sem=sem):
                chain = 0
                for kind, waits, f_or_s, a, kw in entries:
                    for w in waits:
                        ws, wv = wait_of(w)
                        raw.wait_ge(ws, wv)
                    if kind == "op":
                        if chain:
                            raw.wait_ge(sem, chain)
                        inst = getattr(raw, f_or_s)(*a, **kw)
                        inst.then_inc(sem, 1)
                        chain += 1
                    else:
                        ssem, _ = f_or_s
                        raw.dma_start(*a, **kw).then_inc(ssem, 16)
            getattr(block, eng)(body)



def build_L1():
    nc = bass.Bass("TRN2", target_bir_lowering=False)
    x = nc.dram_tensor("x", [T, D], F32, kind="ExternalInput").ap()
    wup_s = nc.dram_tensor("wup_s", [384, D], F32, kind="ExternalInput").ap()
    wg_s = nc.dram_tensor("wg_s", [384, D], F32, kind="ExternalInput").ap()
    wd_s = nc.dram_tensor("wd_s", [128, II], F32, kind="ExternalInput").ap()
    af = nc.dram_tensor("af", [CPC, 64], F32, kind="ExternalInput").ap()
    bsl = nc.dram_tensor("bsl", [CPC, 64], F32, kind="ExternalInput").ap()

    xn = nc.dram_tensor("xn", [T, D], F32, kind="ExternalOutput").ap()
    mx = nc.dram_tensor("mx", [128, 1], F32, kind="ExternalOutput").ap()
    wupq = nc.dram_tensor("wupq", [384, D], BF16, kind="ExternalOutput").ap()
    wgq = nc.dram_tensor("wgq", [384, D], BF16, kind="ExternalOutput").ap()
    wdq = nc.dram_tensor("wdq", [128, II], BF16, kind="ExternalOutput").ap()
    cvec = nc.dram_tensor("cvec", [CPC, 1024], BF16, kind="ExternalOutput").ap()
    psiT = nc.dram_tensor("psiT", [CPC, 64 * 128], BF16, kind="ExternalOutput").ap()

    NB = 3

    with ExitStack() as st:
        sb = lambda n, s, d: st.enter_context(nc.sbuf_tensor(n, s, d))
        xbufA = sb("xbufA", [128, NB * D], F32)
        xbufC = sb("xbufC", [128, NB * D], F32)
        sums = sb("sums", [128, NTILE], F32)
        sumsq = sb("sumsq", [128, NTILE], F32)
        sqscr = sb("sqscr", [128, D], F32)
        m32 = sb("m32", [128, NTILE], F32)
        ex2 = sb("ex2", [128, NTILE], F32)
        msq = sb("msq", [128, NTILE], F32)
        var0 = sb("var0", [128, NTILE], F32)
        std32 = sb("std32", [128, NTILE], F32)
        rstd = sb("rstd", [128, NTILE], F32)
        epst = sb("epst", [128, 1], F32)
        xnout = sb("xnout", [128, NB * D], F32)
        mxcols = sb("mxcols", [128, NTILE], F32)
        mxfin = sb("mxfin", [128, 1], F32)
        wbuf = sb("wbuf", [128, 2 * II], F32)
        wmax = sb("wmax", [128, 2 * (II // 32)], F32)
        wsc = sb("wsc", [128, 2 * (II // 32)], F32)
        winv = sb("winv", [128, 2 * (II // 32)], F32)
        wi32 = sb("wi32", [128, 2 * II], I32)
        wqb = sb("wqb", [128, 2 * II], BF16)
        aft = sb("aft", [CPC, 64], F32)
        bt = sb("bt", [CPC, 64], F32)
        nega = sb("nega", [CPC, 64], F32)
        hbuf = sb("hbuf", [CPC, 192], F32)
        hdot = sb("hdot", [CPC, 64], F32)
        cvt = sb("cvt", [CPC, 1024], BF16)
        thtg = sb("thtg", [128, 2 * 8 * 128], BF16)
        eg = sb("eg", [128, 2 * 8 * 64], BF16)
        thtr = sb("thtr", [128, 8 * 128], BF16)
        egr = sb("egr", [128, 8 * 64], BF16)
        psout = sb("psout", [64, 2 * 512], BF16)
        pspsi = st.enter_context(nc.psum_tensor("pspsi", [64, 2 * 512], F32))
        ldA = Slots(nc, st, "ldA", NB)
        ldC = Slots(nc, st, "ldC", NB)
        ldW = Slots(nc, st, "ldW", 2)
        ldS = Slots(nc, st, "ldS", 2)
        stC = Slots(nc, st, "stC", NB)
        stM = Slots(nc, st, "stM", 1)
        stW = Slots(nc, st, "stW", 2)
        stV = Slots(nc, st, "stV", 1)
        stP = Slots(nc, st, "stP", 2)
        wdm = Slots(nc, st, "wdm", 4)
        block = st.enter_context(nc.Block())

        p = Prog(nc, st)

        wtiles = [(wup_s[r * 128:(r + 1) * 128, :], wupq[r * 128:(r + 1) * 128, :], D)
                  for r in range(3)]
        wtiles += [(wg_s[r * 128:(r + 1) * 128, :], wgq[r * 128:(r + 1) * 128, :], D)
                   for r in range(3)]
        wtiles.append((wd_s, wdq, II))

        for i in range(NTILE):
            w = [f"A_red{i - NB}", f"A_sq{i - NB}"] if i >= NB else []
            p.dma("sync", ldA.nxt(i), out=xbufA[:, (i % NB) * D:(i % NB + 1) * D],
                  in_=x[i * 128:(i + 1) * 128, :], waits=w, label=f"ldA{i}")
        for i in range(NTILE):
            bufA = xbufA[:, (i % NB) * D:(i % NB + 1) * D]
            p.op("vector", "reduce_sum", out=sums[:, i:i + 1], in_=bufA, axis=AX.X,
                 waits=[f"ldA{i}"], label=f"A_red{i}")
            p.op("scalar", "activation", out=sqscr[:], in_=bufA, func=AF.Square,
                 accum_out=sumsq[:, i:i + 1], waits=[f"ldA{i}"], label=f"A_sq{i}")
        p.op("vector", "memset", epst[:], 1e-5)
        p.op("vector", "tensor_scalar", out=m32[:], in0=sums[:], scalar1=1.0 / D,
             scalar2=None, op0=ALU.mult)
        p.op("vector", "tensor_scalar", out=ex2[:], in0=sumsq[:], scalar1=1.0 / D,
             scalar2=None, op0=ALU.mult, waits=[f"A_sq{NTILE - 1}"])
        p.op("vector", "tensor_tensor", out=msq[:], in0=m32[:], in1=m32[:], op=ALU.mult)
        p.op("vector", "scalar_tensor_tensor", out=var0[:], in0=msq[:], scalar=-1.0,
             in1=ex2[:], op0=ALU.mult, op1=ALU.add, label="B_var")
        p.op("scalar", "activation", out=std32[:], in_=var0[:], func=AF.Sqrt,
             bias=epst[:, 0:1], waits=["B_var"], label="B_std")
        p.op("vector", "reciprocal", out=rstd[:], in_=std32[:], waits=["B_std"],
             label="B_rstd")
        for i in range(NTILE):
            w = [f"C_norm{i - NB}"] if i >= NB else []
            p.dma("sync", ldC.nxt(i), out=xbufC[:, (i % NB) * D:(i % NB + 1) * D],
                  in_=x[i * 128:(i + 1) * 128, :], waits=w, label=f"ldC{i}")
        for i in range(NTILE):
            w = [f"ldC{i}"]
            if i >= NB:
                w.append(f"stC{i - NB}")
            p.op("vector", "tensor_scalar",
                 out=xnout[:, (i % NB) * D:(i % NB + 1) * D],
                 in0=xbufC[:, (i % NB) * D:(i % NB + 1) * D],
                 scalar1=m32[:, i:i + 1], scalar2=rstd[:, i:i + 1],
                 op0=ALU.subtract, op1=ALU.mult, waits=w, label=f"C_norm{i}")
            p.op("vector", "tensor_reduce", out=mxcols[:, i:i + 1],
                 in_=xnout[:, (i % NB) * D:(i % NB + 1) * D], axis=AX.X, op=ALU.max,
                 apply_absolute_value=True)
            p.dma("scalar", stC.nxt(i), out=xn[i * 128:(i + 1) * 128, :],
                  in_=xnout[:, (i % NB) * D:(i % NB + 1) * D],
                  waits=[f"C_norm{i}"], label=f"stC{i}")
        p.op("vector", "tensor_reduce", out=mxfin[:], in_=mxcols[:], axis=AX.X,
             op=ALU.max, apply_absolute_value=True, label="C_mx")
        p.dma("scalar", stM.nxt(0), out=mx, in_=mxfin[:], waits=["C_mx"])
        for wi, (win, wout, flen) in enumerate(wtiles):
            half = wi % 2
            hoff = half * II
            hblk = half * (II // 32)
            nblk = flen // 32
            w = [f"W_q{wi - 2}"] if wi >= 2 else []
            p.dma("sync", ldW.nxt(wi), out=wbuf[:, hoff:hoff + flen], in_=win,
                  waits=w, label=f"ldW{wi}")
            wv = wbuf[:, hoff:hoff + flen].rearrange("q (n c) -> q n c", c=32)
            p.op("vector", "tensor_reduce", out=wmax[:, hblk:hblk + nblk], in_=wv,
                 axis=AX.X, op=ALU.max, apply_absolute_value=True,
                 waits=[f"ldW{wi}"])
            p.op("vector", "tensor_scalar", out=wsc[:, hblk:hblk + nblk],
                 in0=wmax[:, hblk:hblk + nblk], scalar1=1.0 / 7.0, scalar2=1e-12,
                 op0=ALU.mult, op1=ALU.max)
            p.op("vector", "reciprocal", out=winv[:, hblk:hblk + nblk],
                 in_=wsc[:, hblk:hblk + nblk])
            wiv = wi32[:, hoff:hoff + flen].rearrange("q (n c) -> q n c", c=32)
            p.op("vector", "tensor_tensor", out=wiv, in0=wv,
                 in1=winv[:, hblk:hblk + nblk].unsqueeze(2).broadcast_to([128, nblk, 32]),
                 op=ALU.mult)
            wq_w = [f"stW{wi - 2}"] if wi >= 2 else []
            wqv = wqb[:, hoff:hoff + flen].rearrange("q (n c) -> q n c", c=32)
            p.op("vector", "tensor_tensor", out=wqv, in0=wiv,
                 in1=wsc[:, hblk:hblk + nblk].unsqueeze(2).broadcast_to([128, nblk, 32]),
                 op=ALU.mult, waits=wq_w, label=f"W_q{wi}")
            p.dma("scalar", stW.nxt(wi), out=wout, in_=wqb[:, hoff:hoff + flen],
                  waits=[f"W_q{wi}"], label=f"stW{wi}")
        p.dma("sync", ldS.nxt(0), out=aft[:], in_=af, label="ldaf")
        p.dma("sync", ldS.nxt(1), out=bt[:], in_=bsl, label="ldb")
        p.op("vector", "tensor_scalar", out=nega[:], in0=aft[:], scalar1=-1.0,
             scalar2=None, op0=ALU.mult, waits=["ldaf"])
        p.op("vector", "memset", hbuf[:, 0:64], 0.0)
        p.op("vector", "memset", hbuf[:, 64:65], 1.0)
        p.op("vector", "memset", hbuf[:, 65:192], 0.0)
        for t in range(1, 128):
            p.op("vector", "tensor_tensor", out=hdot[:], in0=nega[:],
                 in1=hbuf[:, t:t + 64], op=ALU.mult)
            p.op("vector", "reduce_sum", out=hbuf[:, 64 + t:65 + t], in_=hdot[:],
                 axis=AX.X)
        p.op("vector", "memset", cvt[:], 0.0)
        hrev_in = _ap(hbuf[:], 191, [[192, CPC], [-1, 128]])
        p.op("vector", "tensor_copy", out=cvt[:, 0:128], in_=hrev_in)
        p.op("vector", "tensor_copy", out=cvt[:, 319:383], in_=bt[:], waits=["ldb"])
        p.op("vector", "tensor_copy", out=cvt[:, 703:767], in_=bt[:])
        nega_rev = _ap(nega[:], 63, [[64, CPC], [-1, 64]])
        p.op("vector", "tensor_copy", out=cvt[:, 768:832], in_=nega_rev,
             label="cv_done")
        p.dma("scalar", stV.nxt(0), out=cvec, in_=cvt[:], waits=["cv_done"],
              label="stV")
        for gi in range(16):
            ch0 = gi * 8
            gb = (gi % 2) * 8
            w = ["stV"]
            if gi >= 2:
                w.append(f"psi_rv{gi - 2}")
            in_th = _ap(cvec, ch0 * 1024, [[1, 128], [1024, 8], [1, 128]])
            p.dma("gpsimd", wdm.nxt(2 * gi), out=thtg[:, gb * 128:(gb + 8) * 128]
                  .rearrange("q (c i) -> q c i", c=8), in_=in_th, waits=w,
                  label=f"wdm_t{gi}")
            in_e = _ap(cvec, ch0 * 1024 + 768, [[1, 128], [1024, 8], [1, 64]])
            p.dma("gpsimd", wdm.nxt(2 * gi + 1), out=eg[:, gb * 64:(gb + 8) * 64]
                  .rearrange("q (c i) -> q c i", c=8), in_=in_e, waits=w,
                  label=f"wdm_e{gi}")
            th_rev_in = _ap(thtg[:], gb * 128 + 127,
                            [[2 * 8 * 128, 128], [128, 8], [-1, 128]])
            p.op("vector", "tensor_copy",
                 out=thtr[:].rearrange("q (c i) -> q c i", c=8),
                 in_=th_rev_in, waits=[f"wdm_t{gi}"] +
                 ([f"psi_mm{2 * (gi - 1) + 1}"] if gi >= 1 else []))
            e_rev_in = _ap(eg[:], gb * 64 + 63,
                           [[2 * 8 * 64, 128], [64, 8], [-1, 64]])
            p.op("vector", "tensor_copy",
                 out=egr[:].rearrange("q (c i) -> q c i", c=8),
                 in_=e_rev_in, waits=[f"wdm_e{gi}"], label=f"psi_rv{gi}")
            for half in range(2):
                k = 2 * gi + half
                w2 = [f"psi_rv{gi}"]
                if k >= 2:
                    w2.append(f"psi_ev{k - 2}")
                for c4 in range(4):
                    ch = half * 4 + c4
                    p.op("tensor", "matmul",
                         out=pspsi[:, half * 512 + c4 * 128: half * 512 + (c4 + 1) * 128],
                         lhsT=egr[:, ch * 64:(ch + 1) * 64],
                         rhs=thtr[:, ch * 128:(ch + 1) * 128],
                         start=True, stop=True,
                         waits=(w2 if c4 == 0 else []),
                         label=(f"psi_mm{k}" if c4 == 3 else None))
                w3 = [f"psi_mm{k}"]
                if k >= 2:
                    w3.append(f"stP{k - 2}")
                p.op("vector", "tensor_copy", out=psout[:, half * 512:(half + 1) * 512],
                     in_=pspsi[:, half * 512:(half + 1) * 512], waits=w3,
                     label=f"psi_ev{k}")
                for c4 in range(4):
                    ch4 = k * 4 + c4
                    out_ap = _ap(psiT, ch4 * 8192, [[128, 64], [1, 128]])
                    p.dma("scalar", stP.nxt(k), out=out_ap,
                          in_=psout[:, half * 512 + c4 * 128: half * 512 + (c4 + 1) * 128],
                          waits=[f"psi_ev{k}"],
                          label=(f"stP{k}" if c4 == 3 else None))
        drains = []
        for slots in (stC, stM, stW, stV, stP):
            drains += slots.drains()
        p.op("gpsimd", "engine_nop", waits=drains)

        p.emit(block)
    return nc





def build_L2():
    nc = bass.Bass("TRN2", target_bir_lowering=False)
    xnT = nc.dram_tensor("xnT", [D, T], F32, kind="ExternalInput").ap()
    wuT = nc.dram_tensor("wuT", [D, II], BF16, kind="ExternalInput").ap()
    wgT = nc.dram_tensor("wgT", [D, II], BF16, kind="ExternalInput").ap()
    mx8 = nc.dram_tensor("mx8", [1, NCORE * 128], F32, kind="ExternalInput").ap()
    up = nc.dram_tensor("up", [II, T], BF16, kind="ExternalOutput").ap()
    gate = nc.dram_tensor("gate", [II, T], BF16, kind="ExternalOutput").ap()
    mup = nc.dram_tensor("mup", [128, 1], F32, kind="ExternalOutput").ap()
    mgate = nc.dram_tensor("mgate", [128, 1], F32, kind="ExternalOutput").ap()

    NJ = II // 128

    with ExitStack() as st:
        sb = lambda n, s, d: st.enter_context(nc.sbuf_tensor(n, s, d))
        mxt = sb("mxt", [1, NCORE * 128], F32)
        gmax = sb("gmax", [1, 1], F32)
        scl = sb("scl", [1, 2], F32)
        ones1 = sb("ones1", [1, 128], F32)
        sxb = sb("sxb", [128, 2], F32)
        xbuf = sb("xbuf", [128, 2 * T], F32)
        xi32 = sb("xi32", [128, 2 * T], I32)
        xq = sb("xq", [128, 8 * T], BF16)
        wb = sb("wb", [128, 2 * 2 * 1024], BF16)
        stage = sb("stage", [128, 2 * T], BF16)
        mcols = sb("mcols", [128, 2 * NJ], F32)
        mfin = sb("mfin", [128, 2], F32)
        ps = st.enter_context(nc.psum_tensor("ps", [128, 8 * 512], F32))
        ldM = Slots(nc, st, "ldM", 1)
        ldX = Slots(nc, st, "ldX", 2)
        ldW2 = Slots(nc, st, "ldW2", 4)
        stU = Slots(nc, st, "stU", 2)
        stMx = Slots(nc, st, "stMx", 2)
        block = st.enter_context(nc.Block())
        p = Prog(nc, st)

        p.dma("sync", ldM.nxt(0), out=mxt[:], in_=mx8, label="ldM")
        p.op("vector", "tensor_reduce", out=gmax[:], in_=mxt[:], axis=AX.X,
             op=ALU.max, waits=["ldM"])
        p.op("vector", "tensor_scalar", out=scl[:, 0:1], in0=gmax[:],
             scalar1=1.0 / QMAX, scalar2=1e-12, op0=ALU.mult, op1=ALU.max)
        p.op("vector", "reciprocal", out=scl[:, 1:2], in_=scl[:, 0:1])
        p.op("vector", "memset", ones1[:], 1.0, label="sc_ready")
        p.op("tensor", "matmul", out=ps[:, 0:2], lhsT=ones1[:], rhs=scl[:],
             start=True, stop=True, waits=["sc_ready"], label="sc_mm")
        p.op("vector", "tensor_copy", out=sxb[:], in_=ps[:, 0:2], waits=["sc_mm"],
             label="sc_bcast")

        for kb in range(8):
            w = [f"xq{kb - 2}"] if kb >= 2 else []
            p.dma("sync", ldX.nxt(kb), out=xbuf[:, (kb % 2) * T:(kb % 2 + 1) * T],
                  in_=xnT[kb * 128:(kb + 1) * 128, :], waits=w, label=f"ldX{kb}")
            p.op("vector", "tensor_scalar", out=xi32[:, (kb % 2) * T:(kb % 2 + 1) * T],
                 in0=xbuf[:, (kb % 2) * T:(kb % 2 + 1) * T],
                 scalar1=sxb[:, 1:2], scalar2=None, op0=ALU.mult,
                 waits=[f"ldX{kb}", "sc_bcast"])
            p.op("vector", "tensor_scalar", out=xq[:, kb * T:(kb + 1) * T],
                 in0=xi32[:, (kb % 2) * T:(kb % 2 + 1) * T],
                 scalar1=sxb[:, 0:1], scalar2=None, op0=ALU.mult, label=f"xq{kb}")

        for mi, (wsrc, dst, mret) in enumerate(((wuT, up, mup), (wgT, gate, mgate))):
            for j in range(NJ):
                js = j % 2
                woff = (mi * 2 + js) * 1024
                w = [f"mm{mi}_{j - 2}_l"] if j >= 2 else []
                win = _ap(wsrc, j * 128, [[II, 128], [128 * II, 8], [1, 128]])
                p.dma("sync", ldW2.nxt(mi * 2 + js), out=wb[:, woff:woff + 1024]
                      .rearrange("q (k c) -> q k c", k=8), in_=win, waits=w,
                      label=f"ldw{mi}_{j}")
                for k in range(8):
                    for tt in range(8):
                        wv = [f"ldw{mi}_{j}", f"xq{k}"] if tt == 0 else []
                        if k == 0 and j >= 1 and tt < 8:
                            wv.append(f"ev{mi}_{j - 1}_{tt}")
                        if k == 0 and mi == 1 and j == 0:
                            wv.append(f"ev0_{NJ - 1}_{tt}")
                        if k == 0 and mi == 0 and j == 0 and tt == 0:
                            wv.append("sc_bcast")
                        lastmm = (k == 7 and tt == 7)
                        p.op("tensor", "matmul", out=ps[:, tt * 512:(tt + 1) * 512],
                             lhsT=wb[:, woff + k * 128:woff + (k + 1) * 128],
                             rhs=xq[:, k * T + tt * 512:k * T + (tt + 1) * 512],
                             start=(k == 0), stop=(k == 7), waits=wv,
                             label=(f"mm{mi}_{j}_l" if lastmm else
                                    (f"mm{mi}_{j}_{tt}" if k == 7 else None)))
                    if k == 7:
                        pass
                for tt in range(8):
                    wv = [f"mm{mi}_{j}_{tt}" if tt < 7 else f"mm{mi}_{j}_l"]
                    if j >= 2:
                        wv.append(f"st{mi}_{j - 2}")
                        wv.append(f"red{mi}_{j - 2}")
                    elif mi == 1:
                        wv.append(f"st0_{NJ - 2 + j}")
                        wv.append(f"red0_{NJ - 2 + j}")
                    p.op("vector", "tensor_copy",
                         out=stage[:, js * T + tt * 512:js * T + (tt + 1) * 512],
                         in_=ps[:, tt * 512:(tt + 1) * 512], waits=wv,
                         label=f"ev{mi}_{j}_{tt}")

                p.op("vector", "tensor_reduce",
                     out=mcols[:, mi * NJ + j:mi * NJ + j + 1],
                     in_=stage[:, js * T:(js + 1) * T],
                     axis=AX.X, op=ALU.max, apply_absolute_value=True,
                     label=f"red{mi}_{j}")
                p.dma("scalar", stU.nxt(mi * 2 + js),
                      out=dst[j * 128:(j + 1) * 128, :],
                      in_=stage[:, js * T:(js + 1) * T],
                      waits=[f"ev{mi}_{j}_{tt2}" for tt2 in range(8)],
                      label=f"st{mi}_{j}")
            p.op("vector", "tensor_reduce", out=mfin[:, mi:mi + 1],
                 in_=mcols[:, mi * NJ:(mi + 1) * NJ], axis=AX.X, op=ALU.max,
                 label=f"mfin{mi}")
            p.dma("scalar", stMx.nxt(mi), out=mret, in_=mfin[:, mi:mi + 1],
                  waits=[f"mfin{mi}"])

        drains = []
        for slots in (stU, stMx):
            drains += slots.drains()
        p.op("gpsimd", "engine_nop", waits=drains)
        p.emit(block)
    return nc


def l2_prep(l1res, ab_unused=None):
    import ml_dtypes
    wuT = np.ascontiguousarray(
        np.concatenate([np.asarray(r["wupq"]) for r in l1res], axis=0).T)
    wgT = np.ascontiguousarray(
        np.concatenate([np.asarray(r["wgq"]) for r in l1res], axis=0).T)
    mx8 = np.ascontiguousarray(
        np.stack([np.asarray(r["mx"]).ravel() for r in l1res]).reshape(1, -1))
    in_maps = []
    for c in range(NCORE):
        xnT = np.ascontiguousarray(np.asarray(l1res[c]["xn"]).T)
        in_maps.append({"xnT": xnT, "wuT": wuT, "wgT": wgT, "mx8": _f32(mx8)})
    return in_maps


def run_L2(l1res):
    in_maps = l2_prep(l1res)
    nc = build_L2()
    res = run_bass_kernel_spmd(nc, in_maps, core_ids=_CORES)
    return res.results





DBG_L34 = {'gate': True, 'ssm': True, 'chain': True, 'far': True, 'far_stt': True, 'far_st': True, 'far_mm': True, 'far_ev': True}


def build_L34(part):
    is4 = (part == 1)
    nc = bass.Bass("TRN2", target_bir_lowering=False)
    gate = nc.dram_tensor("gate", [II, T], BF16, kind="ExternalInput").ap()
    mg8 = nc.dram_tensor("mg8", [1, NCORE * 128], F32, kind="ExternalInput").ap()
    xsl = nc.dram_tensor("xsl", [128, 8 * 32 * 64], F32, kind="ExternalInput").ap()
    cvec = nc.dram_tensor("cvec", [CPC, 1024], BF16, kind="ExternalInput").ap()
    psiT = nc.dram_tensor("psiT", [CPC, 64 * 128], BF16, kind="ExternalInput").ap()
    h0r = nc.dram_tensor("h0r", [1, 64], F32, kind="ExternalInput").ap()
    if is4:
        up = nc.dram_tensor("up", [II, T], BF16, kind="ExternalInput").ap()
        mu8 = nc.dram_tensor("mu8", [1, NCORE * 128], F32, kind="ExternalInput").ap()
        m48 = nc.dram_tensor("m48", [1, NCORE * 128], F32, kind="ExternalInput").ap()
        prod = nc.dram_tensor("prod", [II, T], BF16, kind="ExternalOutput").ap()
        mout = nc.dram_tensor("m5", [128, 1], F32, kind="ExternalOutput").ap()
    else:
        mout = nc.dram_tensor("m4", [128, 1], F32, kind="ExternalOutput").ap()
    sa = nc.dram_tensor("sa", [64, 128 * 256], F32, kind="ExternalOutput").ap()

    I16 = mybir.dt.int16
    U16 = mybir.dt.uint16
    ch0 = part * 64
    NK = II // 128
    NH = 2 * NK
    HW = 2048

    with ExitStack() as st:
        sb = lambda n, s, d: st.enter_context(nc.sbuf_tensor(n, s, d))
        mrow = sb("mrow", [1, NCORE * 128], F32)
        sc1 = sb("sc1", [1, 8], F32)
        ones1 = sb("ones1", [1, 128], F32)
        sxb = sb("sxb", [128, 8], F32)
        h0row = sb("h0row", [1, 64], F32)
        h0bc = sb("h0bc", [128, 64], F32)
        c65536 = sb("c65536", [128, 1], F32)
        gbuf = sb("gbuf", [128, 2 * HW], BF16)
        gi16 = sb("gi16", [128, 2 * HW], I16)
        sgf = sb("sgf", [128, 2 * HW], F32)
        su16 = sb("su16", [128, 2 * HW], U16)
        p65 = sb("p65", [128, 2 * HW], F32)
        mcol = sb("mcol", [128, NH], F32)
        mfin = sb("mfin", [128, 1], F32)
        if is4:
            ubuf = sb("ubuf", [128, 2 * HW], BF16)
            ui16 = sb("ui16", [128, 2 * HW], I16)
            ti16 = sb("ti16", [128, 2 * HW], I16)
            prodb = sb("prodb", [128, 2 * HW], BF16)
            m5col = sb("m5col", [128, NH], F32)
        xf32 = sb("xf32", [128, 2048], F32)
        XSL = sb("XSL", [128, 8 * 33 * 64], BF16)
        hk = sb("hk", [128, 8 * 128 * 3], BF16)
        tbm = sb("tbm", [128, 8 * 128 * 3], BF16)
        psib = sb("psib", [128, 16 * 128], BF16)
        psibZ = sb("psibZ", [128, 32 * 128], BF16)
        bd = sb("bd", [128, 16 * 128], BF16)
        ubx = sb("ubx", [128, 2 * 256], BF16)
        vinb = sb("vinb", [128, 32 * 256], BF16)
        qbuf = sb("qbuf", [128, 32 * 128], BF16)
        zbuf = sb("zbuf", [128, 32 * 128], BF16)
        t1s = sb("t1s", [128, 2 * 256], F32)
        sstg = sb("sstg", [128, 2 * 256], F32)
        ps_u = st.enter_context(nc.psum_tensor("ps_u", [128, 1024], F32))
        ps_v = st.enter_context(nc.psum_tensor("ps_v", [128, 1024], F32))
        ps_q = st.enter_context(nc.psum_tensor("ps_q", [128, 1024], F32))
        ps_f = st.enter_context(nc.psum_tensor("ps_f", [128, 1024], F32))
        ldM = Slots(nc, st, "ldM", 3)
        ldG = Slots(nc, st, "ldG", 2)
        ldU = Slots(nc, st, "ldU", 2)
        ldX8 = Slots(nc, st, "ldX8", 1)
        ldPS = Slots(nc, st, "ldPS", 2)
        ldH = Slots(nc, st, "ldH", 1)
        wdm = Slots(nc, st, "wdm", 3)
        stS = Slots(nc, st, "stS", 2)
        stMx = Slots(nc, st, "stMx", 1)
        stP4 = Slots(nc, st, "stP4", 2)
        block = st.enter_context(nc.Block())
        p = Prog(nc, st)

        p.dma("sync", ldM.nxt(0), out=mrow[:], in_=mg8, label="ldmg")
        p.op("vector", "memset", sc1[:], 0.0)
        p.op("vector", "tensor_reduce", out=sc1[:, 6:7], in_=mrow[:],
             axis=AX.X, op=ALU.max, waits=["ldmg"], label="redmg")
        p.op("vector", "tensor_scalar", out=sc1[:, 0:1], in0=sc1[:, 6:7],
             scalar1=1.0 / QMAX, scalar2=1e-12, op0=ALU.mult, op1=ALU.max)
        p.op("vector", "reciprocal", out=sc1[:, 1:2], in_=sc1[:, 0:1])
        if is4:
            p.dma("sync", ldM.nxt(1), out=mrow[:], in_=mu8, waits=["redmg"],
                  label="ldmu")
            p.op("vector", "tensor_reduce", out=sc1[:, 6:7], in_=mrow[:],
                 axis=AX.X, op=ALU.max, waits=["ldmu"], label="redmu")
            p.op("vector", "tensor_scalar", out=sc1[:, 4:5], in0=sc1[:, 6:7],
                 scalar1=1.0 / QMAX, scalar2=1e-12, op0=ALU.mult, op1=ALU.max)
            p.op("vector", "reciprocal", out=sc1[:, 5:6], in_=sc1[:, 4:5])
            p.dma("sync", ldM.nxt(2), out=mrow[:], in_=m48, waits=["redmu"],
                  label="ldm4")
            p.op("vector", "tensor_reduce", out=sc1[:, 7:8], in_=mrow[:],
                 axis=AX.X, op=ALU.max, waits=["ldm4"])
            p.op("vector", "tensor_scalar", out=sc1[:, 7:8], in0=sc1[:, 7:8],
                 scalar1=1.0, scalar2=1e-30, op0=ALU.mult, op1=ALU.max)
            p.op("vector", "reciprocal", out=sc1[:, 2:3], in_=sc1[:, 7:8])
            p.op("vector", "tensor_scalar", out=sc1[:, 2:3], in0=sc1[:, 2:3],
                 scalar1=QMAX, scalar2=None, op0=ALU.mult)
            p.op("vector", "tensor_tensor", out=sc1[:, 3:4], in0=sc1[:, 7:8],
                 in1=sc1[:, 0:1], op=ALU.mult)
            p.op("vector", "tensor_tensor", out=sc1[:, 3:4], in0=sc1[:, 3:4],
                 in1=sc1[:, 4:5], op=ALU.mult)
            p.op("vector", "tensor_scalar", out=sc1[:, 3:4], in0=sc1[:, 3:4],
                 scalar1=1.0 / (65536.0 * QMAX), scalar2=None, op0=ALU.mult)
        p.dma("sync", ldH.nxt(0), out=h0row[:], in_=h0r, label="ldh0")
        p.op("vector", "memset", c65536[:], 65536.0)
        p.op("vector", "memset", ones1[:], 1.0, label="sc_ready")
        p.op("tensor", "matmul", out=ps_q[:, 384:392], lhsT=ones1[:], rhs=sc1[:],
             start=True, stop=True, waits=["sc_ready"], label="sc_mm")
        p.op("tensor", "matmul", out=ps_q[:, 392:456], lhsT=ones1[:], rhs=h0row[:],
             start=True, stop=True, waits=["ldh0"], label="h0_mm")
        p.op("vector", "tensor_copy", out=sxb[:], in_=ps_q[:, 384:392], waits=["sc_mm"],
             label="sc_bcast")
        p.op("vector", "tensor_copy", out=h0bc[:], in_=ps_q[:, 392:456],
             waits=["h0_mm"], label="h0_bcast")

        for b in range(8):
            p.dma("sync", ldX8.nxt(b), out=xf32[:],
                  in_=xsl[:, b * 2048:(b + 1) * 2048],
                  waits=([f"xc{b - 1}"] if b >= 1 else []), label=f"ldx{b}")
            p.op("vector", "tensor_copy",
                 out=XSL[:, (b * 33 + 1) * 64:(b * 33 + 33) * 64],
                 in_=xf32[:], waits=[f"ldx{b}"], label=f"xc{b}")
        zsl = _ap(XSL[:], 0, [[8 * 33 * 64, 128], [33 * 64, 8], [1, 64]])
        p.op("vector", "memset", zsl, 0.0, label="xz")

        def emit_gate_half(n):
            k, h = n // 2, n % 2
            ns = n % 2
            w = [f"gq{n - 2}"] if n >= 2 else []
            p.dma("sync", ldG.nxt(n), out=gbuf[:, ns * HW:(ns + 1) * HW],
                  in_=gate[k * 128:(k + 1) * 128, h * HW:(h + 1) * HW],
                  waits=w, label=f"ldg{n}")
            gv = gbuf[:, ns * HW:(ns + 1) * HW]
            w2 = [f"ldg{n}", "sc_bcast"]
            if n >= 2:
                w2.append(f"sg{n - 2}")
                w2.append(f"p65_{n - 2}")
            p.op("vector", "tensor_scalar", out=gi16[:, ns * HW:(ns + 1) * HW],
                 in0=gv, scalar1=sxb[:, 1:2], scalar2=None, op0=ALU.mult,
                 waits=w2, label=f"gq{n}")
            p.op("scalar", "activation", out=sgf[:, ns * HW:(ns + 1) * HW],
                 in_=gi16[:, ns * HW:(ns + 1) * HW], func=AF.Sigmoid,
                 scale=sxb[:, 0:1], waits=[f"gq{n}"])
            p.op("scalar", "activation", out=su16[:, ns * HW:(ns + 1) * HW],
                 in_=sgf[:, ns * HW:(ns + 1) * HW], func=AF.Copy,
                 scale=c65536[:, 0:1], label=f"sg{n}")
            p.op("gpsimd", "tensor_tensor", out=p65[:, ns * HW:(ns + 1) * HW],
                 in0=su16[:, ns * HW:(ns + 1) * HW],
                 in1=gi16[:, ns * HW:(ns + 1) * HW], op=ALU.mult,
                 waits=[f"sg{n}"], label=f"p65_{n}")
            if not is4:
                p.op("vector", "tensor_reduce", out=mcol[:, n:n + 1],
                     in_=p65[:, ns * HW:(ns + 1) * HW], axis=AX.X, op=ALU.max,
                     apply_absolute_value=True, waits=[f"p65_{n}"],
                     label=f"gdone{n}")
            else:
                wu_ = [f"uq{n - 2}"] if n >= 2 else []
                p.dma("sync", ldU.nxt(n), out=ubuf[:, ns * HW:(ns + 1) * HW],
                      in_=up[k * 128:(k + 1) * 128, h * HW:(h + 1) * HW],
                      waits=wu_, label=f"ldu{n}")
                p.op("vector", "tensor_scalar", out=ti16[:, ns * HW:(ns + 1) * HW],
                     in0=p65[:, ns * HW:(ns + 1) * HW], scalar1=sxb[:, 2:3],
                     scalar2=None, op0=ALU.mult, waits=[f"p65_{n}"],
                     label=f"tq{n}")
                p.op("vector", "tensor_scalar", out=ui16[:, ns * HW:(ns + 1) * HW],
                     in0=ubuf[:, ns * HW:(ns + 1) * HW], scalar1=sxb[:, 5:6],
                     scalar2=None, op0=ALU.mult, waits=[f"ldu{n}"],
                     label=f"uq{n}")
                wpr = [f"stp{n - 2}"] if n >= 2 else []
                p.op("vector", "scalar_tensor_tensor",
                     out=prodb[:, ns * HW:(ns + 1) * HW],
                     in0=ti16[:, ns * HW:(ns + 1) * HW], scalar=sxb[:, 3:4],
                     in1=ui16[:, ns * HW:(ns + 1) * HW], op0=ALU.mult,
                     op1=ALU.mult, waits=wpr, label=f"pr{n}")
                p.op("vector", "tensor_reduce", out=m5col[:, n:n + 1],
                     in_=prodb[:, ns * HW:(ns + 1) * HW], axis=AX.X, op=ALU.max,
                     apply_absolute_value=True, label=f"gdone{n}")
                p.dma("scalar", stP4.nxt(n),
                      out=prod[k * 128:(k + 1) * 128, h * HW:(h + 1) * HW],
                      in_=prodb[:, ns * HW:(ns + 1) * HW], waits=[f"pr{n}"],
                      label=f"stp{n}")

        def emit_group(wv, g):
            gch = wv * 32 + g * 8
            gid = wv * 4 + g
            for mi2, off in ((0, 256), (1, 512), (2, 0)):
                in_w = _ap(cvec, (ch0 + gch) * 1024 + off,
                           [[1, 128], [1024, 8], [1, 128]])
                w = ["xz"] if gid == 0 and mi2 == 0 else []
                if gid >= 1:
                    w.append(f"grp_mm{gid - 1}")
                p.dma("gpsimd", wdm.nxt(3 * gid + mi2),
                      out=hk[:, mi2 * 1024:(mi2 + 1) * 1024]
                      .rearrange("q (c i) -> q c i", c=8), in_=in_w, waits=w,
                      label=f"wd{gid}_{mi2}")
            for mi2 in range(3):
                rev_in = _ap(hk[:], mi2 * 1024 + 127,
                             [[3 * 1024, 128], [128, 8], [-1, 128]])
                w = [f"wd{gid}_{mi2}"]
                if gid >= 1:
                    w.append(f"grp_mm{gid - 1}")
                p.op("vector", "tensor_copy",
                     out=tbm[:, mi2 * 1024:(mi2 + 1) * 1024]
                     .rearrange("q (c i) -> q c i", c=8),
                     in_=rev_in, waits=w, label=f"rv{gid}_{mi2}")
            for ci in range(8):
                lch = gch + ci
                wch = lch % 32
                par = wch % 2
                pr = wch // 2
                cs = wch % 2
                xb0 = _ap(XSL[:], 64 + lch, [[8 * 33 * 64, 128], [33 * 64, 8], [64, 32]])
                xb1 = _ap(XSL[:], lch, [[8 * 33 * 64, 128], [33 * 64, 8], [64, 32]])
                wv_ = ["xz", f"rv{gid}_0"]
                if gid * 8 + ci >= 2:
                    wv_.append(f"uev{gch + ci - 2}")
                p.op("tensor", "matmul", out=ps_u[:, cs * 512:cs * 512 + 256],
                     lhsT=tbm[:, ci * 128:(ci + 1) * 128],
                     rhs=xb0, start=True, stop=False, waits=wv_)
                p.op("tensor", "matmul", out=ps_u[:, cs * 512:cs * 512 + 256],
                     lhsT=tbm[:, 1024 + ci * 128:1024 + (ci + 1) * 128],
                     rhs=xb1, start=False, stop=True, waits=[f"rv{gid}_1"],
                     label=f"umm{lch}")
                p.op("vector", "tensor_copy", out=ubx[:, cs * 256:(cs + 1) * 256],
                     in_=ps_u[:, cs * 512:cs * 512 + 256], waits=[f"umm{lch}"],
                     label=f"uev{lch}")
                wv2 = [f"uev{lch}", f"rv{gid}_2"]
                if gid * 8 + ci >= 2:
                    wv2.append(f"vev{gch + ci - 2}")
                p.op("tensor", "matmul", out=ps_v[:, cs * 512:cs * 512 + 256],
                     lhsT=tbm[:, 2048 + ci * 128:2048 + (ci + 1) * 128],
                     rhs=ubx[:, cs * 256:(cs + 1) * 256], start=True, stop=True,
                     waits=wv2, label=f"vmm{lch}")
                wq = []
                if gid == 0 and ci < 2:
                    wq += ["sc_bcast", "h0_bcast"]
                if pr >= 2 and par == 0:
                    wq.append(f"qev{wv}_{pr - 2}")
                p.op("tensor", "matmul",
                     out=ps_q[64 * par:64 * par + 64, (pr % 2) * 512:(pr % 2) * 512 + 256],
                     lhsT=tbm[:, 2048 + ci * 128 + 64:2048 + (ci + 1) * 128],
                     rhs=ubx[:, cs * 256:(cs + 1) * 256], start=True, stop=True,
                     tile_position=(0, 64 * par), waits=wq, label=f"qmm{lch}")
                p.op("vector", "tensor_copy",
                     out=vinb[:, wch * 256:(wch + 1) * 256],
                     in_=ps_v[:, cs * 512:cs * 512 + 256], waits=[f"vmm{lch}"],
                     label=f"vev{lch}")
                if par == 1:
                    qo = _ap(qbuf[:], pr * 8, [[32 * 128, 128], [1, 8], [128, 32]])
                    p.op("vector", "tensor_copy", out=qo,
                         in_=ps_q[:, (pr % 2) * 512:(pr % 2) * 512 + 256]
                         .rearrange("q (b c) -> q b c", b=8),
                         waits=[f"qmm{lch - 1}", f"qmm{lch}"], label=f"qev{wv}_{pr}")
                if ci == 7:
                    p.op("gpsimd", "engine_nop", waits=[f"vmm{lch}", f"qmm{lch}"],
                         label=f"grp_mm{gid}")

        def emit_psib_bd(wv):
            in_e = _ap(psiT, (ch0 + 32 * wv) * 8192,
                       [[128, 64], [2 * 8192, 16], [1, 128]])
            wld = [f"fmm{wv - 1}_31"] if (wv >= 1 and DBG_L34['far']) else []
            p.dma("sync", ldPS.nxt(2 * wv),
                  out=psib[0:64, :].rearrange("q (r i) -> q r i", r=16), in_=in_e,
                  waits=wld, label=f"ldpe{wv}")
            in_o = _ap(psiT, (ch0 + 32 * wv + 1) * 8192,
                       [[128, 64], [2 * 8192, 16], [1, 128]])
            p.dma("sync", ldPS.nxt(2 * wv + 1),
                  out=psib[64:128, :].rearrange("q (r i) -> q r i", r=16), in_=in_o,
                  waits=wld, label=f"ldpo{wv}")
            wbd = [f"chaindone{wv - 1}"] if (wv >= 1 and DBG_L34['chain']) else []
            p.op("vector", "memset", bd[:], 0.0, waits=wbd)
            bde_o = _ap(bd[:], 0, [[16 * 128, 64], [128, 16], [1, 64]])
            bde_i = _ap(psib[:], 64, [[16 * 128, 64], [128, 16], [1, 64]])
            p.op("vector", "tensor_copy", out=bde_o, in_=bde_i, waits=[f"ldpe{wv}"])
            bdo_o = _ap(bd[:], 64 * 16 * 128 + 64, [[16 * 128, 64], [128, 16], [1, 64]])
            bdo_i = _ap(psib[:], 64 * 16 * 128 + 64, [[16 * 128, 64], [128, 16], [1, 64]])
            p.op("vector", "tensor_copy", out=bdo_o, in_=bdo_i, waits=[f"ldpo{wv}"],
                 label=f"bd_done{wv}")
            wz = ([f"t1_{wv * 32 - 1}"]
                  if (wv >= 1 and DBG_L34['far'] and DBG_L34.get('far_ev') != 'none')
                  else [])
            p.op("vector", "memset", psibZ[:], 0.0, waits=wz)
            ze_o = _ap(psibZ[:], 0, [[32 * 128, 64], [256, 16], [1, 128]])
            ze_i = _ap(psib[:], 0, [[16 * 128, 64], [128, 16], [1, 128]])
            p.op("vector", "tensor_copy", out=ze_o, in_=ze_i, waits=[f"ldpe{wv}"])
            zo_o = _ap(psibZ[:], 64 * 32 * 128 + 128, [[32 * 128, 64], [256, 16], [1, 128]])
            zo_i = _ap(psib[:], 64 * 16 * 128, [[16 * 128, 64], [128, 16], [1, 128]])
            p.op("vector", "tensor_copy", out=zo_o, in_=zo_i, waits=[f"ldpo{wv}"],
                 label=f"psz_done{wv}")

        def emit_chain(wv):
            p.op("vector", "memset", zbuf[:, 0:128], 0.0,
                 waits=[f"bd_done{wv}", f"qev{wv}_15"], label=f"z{wv}_0")
            for c in range(1, 32):
                for pr in range(16):
                    w = []
                    if pr == 0:
                        w.append(f"z{wv}_{c - 1}")
                    p.op("tensor", "matmul", out=ps_q[:, 256 + pr * 8:256 + (pr + 1) * 8],
                         lhsT=bd[:, pr * 128:(pr + 1) * 128],
                         rhs=zbuf[:, (c - 1) * 128 + pr * 8:(c - 1) * 128 + (pr + 1) * 8],
                         start=True, stop=True, waits=w,
                         label=(f"cmm{wv}_{c}" if pr == 15 else None))
                p.op("vector", "tensor_tensor", out=zbuf[:, c * 128:(c + 1) * 128],
                     in0=ps_q[:, 256:384], in1=qbuf[:, (c - 1) * 128:c * 128], op=ALU.add,
                     waits=[f"cmm{wv}_{c}"], label=f"z{wv}_{c}")
            p.op("gpsimd", "engine_nop", waits=[f"z{wv}_31"], label=f"chaindone{wv}")

        def emit_far(wv):
            for wch in range(32):
                lch = wv * 32 + wch
                par = wch % 2
                pr = wch // 2
                cs = wch % 2
                w = [f"z{wv}_31"] if wch == 0 else []
                if wch >= 2 and DBG_L34['far_ev'] != 'none':
                    w.append(f"t1_{lch - 2}")
                rhs_far = _ap(zbuf[:], pr * 8,
                              [[32 * 128, 128], [1, 8], [128, 32]])
                if wch == 0:
                    w.append(f"psz_done{wv}")
                if DBG_L34['far_mm']:
                    p.op("tensor", "matmul", out=ps_f[:, cs * 512:cs * 512 + 256],
                         lhsT=psibZ[:, wch * 128:(wch + 1) * 128],
                         rhs=rhs_far, start=True, stop=True,
                         waits=w, label=f"fmm{wv}_{wch}")
                else:
                    p.op("tensor", "matmul", out=ps_f[:, cs * 512:cs * 512 + 256],
                         lhsT=psibZ[:, wch * 128:(wch + 1) * 128],
                         rhs=ubx[:, 0:256].rearrange("q (b c) -> q b c", b=8),
                         start=True, stop=True, waits=w, label=f"fmm{wv}_{wch}")
                if DBG_L34['far_ev'] == 'none':
                    continue
                if not DBG_L34['far_ev']:
                    p.op("vector", "tensor_copy",
                         out=t1s[:, cs * 256:(cs + 1) * 256],
                         in_=ps_f[:, cs * 512:cs * 512 + 256],
                         waits=[f"fmm{wv}_{wch}"], label=f"t1_{lch}")
                    p.op("vector", "tensor_copy",
                         out=sstg[:, cs * 256:(cs + 1) * 256],
                         in_=t1s[:, cs * 256:(cs + 1) * 256],
                         waits=[f"t1_{lch}"], label=f"sf{lch}")
                    continue
                xb0 = _ap(XSL[:], 64 + lch,
                          [[8 * 33 * 64, 128], [33 * 64, 8], [64, 32]])
                if DBG_L34['far_stt']:
                    p.op("vector", "scalar_tensor_tensor",
                         out=t1s[:, cs * 256:(cs + 1) * 256],
                         in0=xb0, scalar=h0bc[:, lch:lch + 1],
                         in1=ps_f[:, cs * 512:cs * 512 + 256],
                         op0=ALU.mult, op1=ALU.add,
                         waits=[f"fmm{wv}_{wch}", "h0_bcast"], label=f"t1_{lch}")
                else:
                    p.op("vector", "tensor_copy",
                         out=t1s[:, cs * 256:(cs + 1) * 256],
                         in_=ps_f[:, cs * 512:cs * 512 + 256],
                         waits=[f"fmm{wv}_{wch}", "h0_bcast"], label=f"t1_{lch}")
                wst = [f"t1_{lch}"]
                if lch >= 2 and DBG_L34['far_st']:
                    wst.append(f"sst{lch - 2}")
                p.op("vector", "tensor_tensor",
                     out=sstg[:, cs * 256:(cs + 1) * 256],
                     in0=t1s[:, cs * 256:(cs + 1) * 256],
                     in1=vinb[:, wch * 256:(wch + 1) * 256], op=ALU.add,
                     waits=wst, label=f"sf{lch}")
                if DBG_L34['far_st']:
                    out_sa = _ap(sa, lch * 32768, [[256, 128], [1, 256]])
                    p.dma("scalar", stS.nxt(lch), out=out_sa,
                          in_=sstg[:, cs * 256:(cs + 1) * 256], waits=[f"sf{lch}"],
                          label=f"sst{lch}")

        gate_n = [0]

        def emit_some_gate(upto):
            while gate_n[0] < min(NH, upto):
                emit_gate_half(gate_n[0])
                gate_n[0] += 1

        if DBG_L34['ssm']:
            emit_psib_bd(0)
            for g in range(4):
                if DBG_L34['gate']:
                    emit_some_gate(4 * (g + 1))
                emit_group(0, g)
            if DBG_L34['chain']:
                emit_chain(0)
            if DBG_L34['far']:
                emit_far(0)
            emit_psib_bd(1)
            for g in range(4):
                if DBG_L34['gate']:
                    emit_some_gate(16 + 4 * (g + 1))
                emit_group(1, g)
            if DBG_L34['chain']:
                emit_chain(1)
            if DBG_L34['far']:
                emit_far(1)
        if DBG_L34['gate']:
            emit_some_gate(NH)

        if DBG_L34['gate']:
            p.op("vector", "tensor_reduce", out=mfin[:],
                 in_=(mcol[:] if not is4 else m5col[:]),
                 axis=AX.X, op=ALU.max, waits=[f"gdone{NH - 1}"], label="mfl")
            p.dma("scalar", stMx.nxt(0), out=mout, in_=mfin[:], waits=["mfl"])
        if (not DBG_L34['far']) or (not DBG_L34['far_st']):
            p.op("vector", "memset", sstg[:, 0:256], 0.0, label="sz")
            for lch in range(64):
                p.dma("scalar", stS.nxt(lch), out=_ap(sa, lch * 32768, [[256, 128], [1, 256]]),
                      in_=sstg[:, 0:256], waits=["sz"])
        if not DBG_L34['gate']:
            p.op("vector", "memset", mfin[:], 0.0, label="mz")
            p.dma("scalar", stMx.nxt(1), out=mout, in_=mfin[:], waits=["mz"])
        drains = []
        for slots in (stS, stMx) + ((stP4,) if is4 else ()):
            drains += slots.drains()
        p.op("gpsimd", "engine_nop", waits=drains)
        p.emit(block)
    return nc



def _f32(a):
    return np.ascontiguousarray(a, dtype=np.float32)


def l1_inmap(inputs, c):
    u = np.asarray(inputs["L0_attn_u"])
    ab = np.asarray(inputs["L0_attn_ab"])
    wg = np.asarray(inputs["L0_ffn_w_gate"])
    wu = np.asarray(inputs["L0_ffn_w_up"])
    wd = np.asarray(inputs["L0_ffn_w_down"])
    a = ab[:D]
    b = ab[D:]
    return {
        "x": _f32(u[c]),
        "wup_s": _f32(wu[384 * c:384 * (c + 1)]),
        "wg_s": _f32(wg[384 * c:384 * (c + 1)]),
        "wd_s": _f32(wd[128 * c:128 * (c + 1)]),
        "af": _f32(a[CPC * c:CPC * (c + 1), ::-1]),
        "bsl": _f32(b[CPC * c:CPC * (c + 1), ::-1]),
    }


def run_L1(inputs):
    in_maps = [l1_inmap(inputs, c) for c in range(NCORE)]
    nc = build_L1()
    res = run_bass_kernel_spmd(nc, in_maps, core_ids=_CORES)
    return res.results







def build_L5():
    nc = bass.Bass("TRN2", target_bir_lowering=False)
    prodH = nc.dram_tensor("prodH", [II, T], BF16, kind="ExternalInput").ap()
    m58 = nc.dram_tensor("m58", [1, NCORE * 128], F32, kind="ExternalInput").ap()
    wdT = nc.dram_tensor("wdT", [II, D], BF16, kind="ExternalInput").ap()
    sT = nc.dram_tensor("sT", [D, T], F32, kind="ExternalInput").ap()
    xT = nc.dram_tensor("xT", [D, T], F32, kind="ExternalInput").ap()
    yT = nc.dram_tensor("yT", [D, T], F32, kind="ExternalOutput").ap()

    NKD = II // 128
    I16 = mybir.dt.int16

    with ExitStack() as st:
        sb = lambda n, s, d: st.enter_context(nc.sbuf_tensor(n, s, d))
        mrow = sb("mrow", [1, NCORE * 128], F32)
        sc1 = sb("sc1", [1, 2], F32)
        ones1 = sb("ones1", [1, 128], F32)
        sxb = sb("sxb", [128, 2], F32)
        pbuf = sb("pbuf", [128, 2 * 2048], BF16)
        pi16 = sb("pi16", [128, 2 * 2048], I16)
        OQ = sb("OQ", [128, NKD * 2048], BF16)
        WD = sb("WD", [128, NKD * 1024], BF16)
        xbuf = sb("xbuf", [128, 2 * 512], F32)
        sbuf2 = sb("sbuf2", [128, 2 * 512], F32)
        gst = sb("gst", [128, 2 * 512], F32)
        sgst = sb("sgst", [128, 2 * 512], F32)
        yst = sb("yst", [128, 2 * 512], F32)
        ps = st.enter_context(nc.psum_tensor("ps", [128, 8 * 512], F32))
        ldM = Slots(nc, st, "ldM", 1)
        ldP = Slots(nc, st, "ldP", 2)
        ldW5 = Slots(nc, st, "ldW5", 2)
        ldXS = Slots(nc, st, "ldXS", 4)
        stY = Slots(nc, st, "stY", 2)
        block = st.enter_context(nc.Block())
        p = Prog(nc, st)

        p.dma("sync", ldM.nxt(0), out=mrow[:], in_=m58, label="ldm")
        p.op("vector", "tensor_reduce", out=sc1[:, 1:2], in_=mrow[:], axis=AX.X,
             op=ALU.max, waits=["ldm"])
        p.op("vector", "tensor_scalar", out=sc1[:, 0:1], in0=sc1[:, 1:2],
             scalar1=1.0 / QMAX, scalar2=1e-12, op0=ALU.mult, op1=ALU.max)
        p.op("vector", "reciprocal", out=sc1[:, 1:2], in_=sc1[:, 0:1])
        p.op("vector", "memset", ones1[:], 1.0, label="sc_ready")
        p.op("tensor", "matmul", out=ps[:, 0:2], lhsT=ones1[:], rhs=sc1[:],
             start=True, stop=True, waits=["sc_ready"], label="sc_mm")
        p.op("vector", "tensor_copy", out=sxb[:], in_=ps[:, 0:2], waits=["sc_mm"],
             label="sc_bcast")

        for kb in range(NKD):
            p.dma("sync", ldW5.nxt(kb), out=WD[:, kb * 1024:(kb + 1) * 1024],
                  in_=wdT[kb * 128:(kb + 1) * 128, :], label=f"ldw{kb}")

        for th in range(2):
            for kb in range(NKD):
                n = th * NKD + kb
                w = [f"oq{n - 2}"] if n >= 2 else []
                p.dma("sync", ldP.nxt(n), out=pbuf[:, (n % 2) * 2048:(n % 2 + 1) * 2048],
                      in_=prodH[kb * 128:(kb + 1) * 128, th * 2048:(th + 1) * 2048],
                      waits=w, label=f"ldp{n}")
                p.op("vector", "tensor_scalar",
                     out=pi16[:, (n % 2) * 2048:(n % 2 + 1) * 2048],
                     in0=pbuf[:, (n % 2) * 2048:(n % 2 + 1) * 2048],
                     scalar1=sxb[:, 1:2], scalar2=None, op0=ALU.mult,
                     waits=[f"ldp{n}", "sc_bcast"] +
                     ([f"mmlast{th - 1}"] if (kb == 0 and th > 0) else []))
                p.op("vector", "tensor_scalar",
                     out=OQ[:, kb * 2048:(kb + 1) * 2048],
                     in0=pi16[:, (n % 2) * 2048:(n % 2 + 1) * 2048],
                     scalar1=sxb[:, 0:1], scalar2=None, op0=ALU.mult,
                     label=f"oq{n}")
            for db in range(8):
                dpar = db % 2
                for kb in range(NKD):
                    for tt in range(4):
                        bank = dpar * 4 + tt
                        wv = []
                        if kb == 0:
                            wv = [f"oq{th * NKD + NKD - 1}"]
                            prev = (th * 8 + db) - 2
                            if prev >= 0:
                                wv.append(f"sil{prev}_{tt}")
                            elif th == 0 and db == 0 and tt == 0:
                                wv.append("sc_bcast")
                        lastmm = (kb == NKD - 1 and tt == 3)
                        p.op("tensor", "matmul",
                             out=ps[:, bank * 512:(bank + 1) * 512],
                             lhsT=WD[:, kb * 1024 + db * 128:kb * 1024 + (db + 1) * 128],
                             rhs=OQ[:, kb * 2048 + tt * 512:kb * 2048 + (tt + 1) * 512],
                             start=(kb == 0), stop=(kb == NKD - 1),
                             waits=(wv + [f"ldw{kb}"]) if (tt == 0 and db == 0) else wv,
                             label=(f"mm{th * 8 + db}_{tt}" if kb == NKD - 1 else None))
                    if kb == NKD - 1:
                        pass
                for tt in range(4):
                    bank = dpar * 4 + tt
                    n5 = (th * 8 + db) * 4 + tt
                    gidx = th * 8 + db
                    p.op("scalar", "activation", out=gst[:, (n5 % 2) * 512:(n5 % 2 + 1) * 512],
                         in_=ps[:, bank * 512:(bank + 1) * 512], func=AF.Silu,
                         waits=[f"mm{gidx}_{tt}"] +
                         ([f"yv{n5 - 2}"] if n5 >= 2 else []),
                         label=f"sil{gidx}_{tt}")
                    p.dma("sync", ldXS.nxt(2 * n5),
                          out=xbuf[:, (n5 % 2) * 512:(n5 % 2 + 1) * 512],
                          in_=xT[db * 128:(db + 1) * 128,
                                 th * 2048 + tt * 512:th * 2048 + (tt + 1) * 512],
                          waits=([f"yv{n5 - 2}"] if n5 >= 2 else []),
                          label=f"ldx{n5}")
                    p.dma("sync", ldXS.nxt(2 * n5 + 1),
                          out=sbuf2[:, (n5 % 2) * 512:(n5 % 2 + 1) * 512],
                          in_=sT[db * 128:(db + 1) * 128,
                                 th * 2048 + tt * 512:th * 2048 + (tt + 1) * 512],
                          waits=([f"yv{n5 - 2}"] if n5 >= 2 else []),
                          label=f"lds{n5}")
                    p.op("vector", "tensor_tensor",
                         out=sgst[:, (n5 % 2) * 512:(n5 % 2 + 1) * 512],
                         in0=gst[:, (n5 % 2) * 512:(n5 % 2 + 1) * 512],
                         in1=sbuf2[:, (n5 % 2) * 512:(n5 % 2 + 1) * 512],
                         op=ALU.mult,
                         waits=[f"sil{gidx}_{tt}", f"lds{n5}"] +
                         ([f"sty{n5 - 2}"] if n5 >= 2 else []))
                    p.op("vector", "tensor_tensor",
                         out=yst[:, (n5 % 2) * 512:(n5 % 2 + 1) * 512],
                         in0=sgst[:, (n5 % 2) * 512:(n5 % 2 + 1) * 512],
                         in1=xbuf[:, (n5 % 2) * 512:(n5 % 2 + 1) * 512],
                         op=ALU.add, waits=[f"ldx{n5}"], label=f"yv{n5}")
                    p.dma("scalar", stY.nxt(n5),
                          out=yT[db * 128:(db + 1) * 128,
                                 th * 2048 + tt * 512:th * 2048 + (tt + 1) * 512],
                          in_=yst[:, (n5 % 2) * 512:(n5 % 2 + 1) * 512],
                          waits=[f"yv{n5}"], label=f"sty{n5}")
                if db == 7:
                    p.op("gpsimd", "engine_nop",
                         waits=[f"sil{th * 8 + 7}_{t2}" for t2 in range(4)],
                         label=f"mmlast{th}")
        drains = stY.drains()
        p.op("gpsimd", "engine_nop", waits=drains)
        p.emit(block)
    return nc


def l5_prep(inputs, l1res, l3res, l4res):
    u = np.asarray(inputs["L0_attn_u"])
    m58 = np.ascontiguousarray(
        np.stack([np.asarray(r["m5"]).ravel() for r in l4res]).reshape(1, -1))
    wdT = np.ascontiguousarray(
        np.concatenate([np.asarray(r["wdq"]) for r in l1res], axis=0).T)
    s_full = np.empty((B, T, D), np.float32)
    for c in range(NCORE):
        for part, res in ((0, l3res[c]), (1, l4res[c])):
            sa = np.asarray(res["sa"]).reshape(64, 128, 8, 32)
            s_full[:, :, 128 * c + 64 * part:128 * c + 64 * part + 64] = (
                sa.transpose(2, 3, 1, 0).reshape(8, T, 64))
    in_maps = []
    for c in range(NCORE):
        in_maps.append({
            "prodH": np.asarray(l4res[c]["prod"]),
            "m58": _f32(m58),
            "wdT": wdT,
            "sT": _f32(s_full[c].T),
            "xT": _f32(u[c].T),
        })
    return in_maps


def run_L5(inputs, l1res, l3res, l4res):
    nc = build_L5()
    res = run_bass_kernel_spmd(nc, l5_prep(inputs, l1res, l3res, l4res),
                               core_ids=_CORES)
    return res.results


def l34_prep(inputs, l1res, l2res, part, m4s=None):
    h_0 = np.asarray(inputs["L0_attn_h_0"])
    xn_all = np.stack([np.asarray(r["xn"]) for r in l1res])
    mg8 = np.ascontiguousarray(
        np.stack([np.asarray(r["mgate"]).ravel() for r in l2res]).reshape(1, -1))
    if part == 1:
        mu8 = np.ascontiguousarray(
            np.stack([np.asarray(r["mup"]).ravel() for r in l2res]).reshape(1, -1))
        m48 = np.ascontiguousarray(np.stack(m4s).reshape(1, -1))
    in_maps = []
    for c in range(NCORE):
        ch_lo = 128 * c + 64 * part
        arr = xn_all[:, :, ch_lo:ch_lo + 64]
        arr = arr.reshape(8, 32, 128, 64).transpose(2, 0, 1, 3)
        m = {
            "gate": np.asarray(l2res[c]["gate"]),
            "mg8": _f32(mg8),
            "xsl": _f32(arr.reshape(128, 8 * 32 * 64)),
            "cvec": np.asarray(l1res[c]["cvec"]),
            "psiT": np.asarray(l1res[c]["psiT"]),
            "h0r": _f32(h_0[ch_lo:ch_lo + 64].reshape(1, 64)),
        }
        if part == 1:
            m["up"] = np.asarray(l2res[c]["up"])
            m["mu8"] = _f32(mu8)
            m["m48"] = _f32(m48)
        in_maps.append(m)
    return in_maps


def run_L3(inputs, l1res, l2res):
    nc = build_L34(0)
    res = run_bass_kernel_spmd(nc, l34_prep(inputs, l1res, l2res, 0),
                               core_ids=_CORES)
    return res.results


def run_L4(inputs, l1res, l2res, l3res):
    m4s = [np.asarray(r["m4"]).ravel() for r in l3res]
    nc = build_L34(1)
    res = run_bass_kernel_spmd(nc, l34_prep(inputs, l1res, l2res, 1, m4s),
                               core_ids=_CORES)
    return res.results


def kernel(L0_attn_u, L0_attn_ab, L0_attn_h_0, L0_ffn_w_gate, L0_ffn_w_up,
           L0_ffn_w_down):
    inputs = {
        "L0_attn_u": L0_attn_u,
        "L0_attn_ab": L0_attn_ab,
        "L0_attn_h_0": L0_attn_h_0,
        "L0_ffn_w_gate": L0_ffn_w_gate,
        "L0_ffn_w_up": L0_ffn_w_up,
        "L0_ffn_w_down": L0_ffn_w_down,
    }
    l1 = run_L1(inputs)
    l2 = run_L2(l1)
    l3 = run_L3(inputs, l1, l2)
    l4 = run_L4(inputs, l1, l2, l3)
    l5 = run_L5(inputs, l1, l3, l4)
    y = np.stack([np.ascontiguousarray(np.asarray(r["yT"]).T) for r in l5])
    return y.astype(np.float32)


# version 02
# speedup vs baseline: 1.6110x; 1.6110x over previous
import numpy as np
from contextlib import ExitStack
import concourse.bass as bass
import concourse.mybir as mybir
from concourse.bass_utils import run_bass_kernel_spmd

F32 = mybir.dt.float32
BF16 = mybir.dt.bfloat16
I32 = mybir.dt.int32
AF = mybir.ActivationFunctionType
ALU = mybir.AluOpType
AX = mybir.AxisListType

NCORE = 8
B, T, D, II = 8, 4096, 1024, 3072
CPC = 128
QMAX = 32767.0
NTILE = T // 128

_CORES = list(range(NCORE))
ENGS = ("sync", "vector", "scalar", "tensor", "gpsimd")


def _ap(t, offset, pattern):
    return bass.AP(tensor=t.tensor, offset=offset, ap=pattern)


class Slots:

    def __init__(self, nc, st, name, n):
        self.sems = [st.enter_context(nc.semaphore(f"{name}{k}")) for k in range(n)]
        self.cnt = [0] * n
        self.n = n

    def nxt(self, i):
        s = i % self.n
        self.cnt[s] += 16
        return self.sems[s], self.cnt[s]

    def drains(self):
        return [(self.sems[k], self.cnt[k]) for k in range(self.n) if self.cnt[k]]


CHAIN_ENGS = {"vector"}


class Prog:

    def __init__(self, nc, st):
        self.nc = nc
        self.entries = {e: [] for e in ENGS}
        self.sems = {e: st.enter_context(nc.semaphore("ch_" + e)) for e in ENGS}
        self.counts = {e: 0 for e in ENGS}
        self.labels = {}

    def op(self, eng, fname, *a, waits=(), label=None, **kw):
        self.counts[eng] += 1
        if label is not None:
            assert label not in self.labels, label
            self.labels[label] = (eng, self.counts[eng])
        self.entries[eng].append(("op", tuple(waits), fname, a, kw))

    def dma(self, eng, slot, *a, waits=(), label=None, **kw):
        if label is not None:
            assert label not in self.labels, label
            self.labels[label] = ("dma", slot)
        self.entries[eng].append(("dma", tuple(waits), slot, a, kw))

    def wait_of(self, w):
        if isinstance(w, str):
            tgt = self.labels[w]
            if tgt[0] == "dma":
                return tgt[1]
            eng, cnt = tgt
            return self.sems[eng], cnt
        return w

    def emit(self, block):
        for eng in ENGS:
            if not self.entries[eng]:
                continue
            entries = self.entries[eng]
            sem = self.sems[eng]
            wait_of = self.wait_of

            def body(raw, entries=entries, sem=sem, eng=eng):
                chain = 0
                for kind, waits, f_or_s, a, kw in entries:
                    for w in waits:
                        ws, wv = wait_of(w)
                        raw.wait_ge(ws, wv)
                    if kind == "op":
                        if eng in CHAIN_ENGS and chain:
                            raw.wait_ge(sem, chain)
                        inst = getattr(raw, f_or_s)(*a, **kw)
                        inst.then_inc(sem, 1)
                        chain += 1
                    else:
                        ssem, _ = f_or_s
                        raw.dma_start(*a, **kw).then_inc(ssem, 16)
            getattr(block, eng)(body)



def build_L1():
    nc = bass.Bass("TRN2", target_bir_lowering=False)
    x = nc.dram_tensor("x", [T, D], F32, kind="ExternalInput").ap()
    wup_s = nc.dram_tensor("wup_s", [384, D], F32, kind="ExternalInput").ap()
    wg_s = nc.dram_tensor("wg_s", [384, D], F32, kind="ExternalInput").ap()
    wd_s = nc.dram_tensor("wd_s", [128, II], F32, kind="ExternalInput").ap()
    af = nc.dram_tensor("af", [CPC, 64], F32, kind="ExternalInput").ap()
    bsl = nc.dram_tensor("bsl", [CPC, 64], F32, kind="ExternalInput").ap()

    xn = nc.dram_tensor("xn", [T, D], F32, kind="ExternalOutput").ap()
    mx = nc.dram_tensor("mx", [128, 1], F32, kind="ExternalOutput").ap()
    wupq = nc.dram_tensor("wupq", [384, D], BF16, kind="ExternalOutput").ap()
    wgq = nc.dram_tensor("wgq", [384, D], BF16, kind="ExternalOutput").ap()
    wdq = nc.dram_tensor("wdq", [128, II], BF16, kind="ExternalOutput").ap()
    cvec = nc.dram_tensor("cvec", [CPC, 1024], BF16, kind="ExternalOutput").ap()
    psiT = nc.dram_tensor("psiT", [CPC, 64 * 128], BF16, kind="ExternalOutput").ap()

    NB = 3

    with ExitStack() as st:
        sb = lambda n, s, d: st.enter_context(nc.sbuf_tensor(n, s, d))
        xbufA = sb("xbufA", [128, NB * D], F32)
        xbufC = sb("xbufC", [128, NB * D], F32)
        sums = sb("sums", [128, NTILE], F32)
        sumsq = sb("sumsq", [128, NTILE], F32)
        sqscr = sb("sqscr", [128, D], F32)
        m32 = sb("m32", [128, NTILE], F32)
        ex2 = sb("ex2", [128, NTILE], F32)
        msq = sb("msq", [128, NTILE], F32)
        var0 = sb("var0", [128, NTILE], F32)
        std32 = sb("std32", [128, NTILE], F32)
        rstd = sb("rstd", [128, NTILE], F32)
        epst = sb("epst", [128, 1], F32)
        xnout = sb("xnout", [128, NB * D], F32)
        mxcols = sb("mxcols", [128, NTILE], F32)
        mxfin = sb("mxfin", [128, 1], F32)
        wbuf = sb("wbuf", [128, 2 * II], F32)
        wmax = sb("wmax", [128, 2 * (II // 32)], F32)
        wsc = sb("wsc", [128, 2 * (II // 32)], F32)
        winv = sb("winv", [128, 2 * (II // 32)], F32)
        wi32 = sb("wi32", [128, 2 * II], I32)
        wqb = sb("wqb", [128, 2 * II], BF16)
        aft = sb("aft", [CPC, 64], F32)
        bt = sb("bt", [CPC, 64], F32)
        nega = sb("nega", [CPC, 64], F32)
        hbuf = sb("hbuf", [CPC, 192], F32)
        hdot = sb("hdot", [CPC, 64], F32)
        cvt = sb("cvt", [CPC, 1024], BF16)
        thtg = sb("thtg", [128, 2 * 8 * 128], BF16)
        eg = sb("eg", [128, 2 * 8 * 64], BF16)
        thtr = sb("thtr", [128, 8 * 128], BF16)
        egr = sb("egr", [128, 8 * 64], BF16)
        psout = sb("psout", [64, 2 * 512], BF16)
        pspsi = st.enter_context(nc.psum_tensor("pspsi", [64, 2 * 512], F32))
        ldA = Slots(nc, st, "ldA", NB)
        ldC = Slots(nc, st, "ldC", NB)
        ldW = Slots(nc, st, "ldW", 2)
        ldS = Slots(nc, st, "ldS", 2)
        stC = Slots(nc, st, "stC", NB)
        stM = Slots(nc, st, "stM", 1)
        stW = Slots(nc, st, "stW", 2)
        stV = Slots(nc, st, "stV", 1)
        stP = Slots(nc, st, "stP", 2)
        wdm = Slots(nc, st, "wdm", 4)
        block = st.enter_context(nc.Block())

        p = Prog(nc, st)

        wtiles = [(wup_s[r * 128:(r + 1) * 128, :], wupq[r * 128:(r + 1) * 128, :], D)
                  for r in range(3)]
        wtiles += [(wg_s[r * 128:(r + 1) * 128, :], wgq[r * 128:(r + 1) * 128, :], D)
                   for r in range(3)]
        wtiles.append((wd_s, wdq, II))

        for i in range(NTILE):
            w = [f"A_red{i - NB}", f"A_sq{i - NB}"] if i >= NB else []
            p.dma("sync", ldA.nxt(i), out=xbufA[:, (i % NB) * D:(i % NB + 1) * D],
                  in_=x[i * 128:(i + 1) * 128, :], waits=w, label=f"ldA{i}")
        for i in range(NTILE):
            bufA = xbufA[:, (i % NB) * D:(i % NB + 1) * D]
            p.op("vector", "reduce_sum", out=sums[:, i:i + 1], in_=bufA, axis=AX.X,
                 waits=[f"ldA{i}"], label=f"A_red{i}")
            p.op("scalar", "activation", out=sqscr[:], in_=bufA, func=AF.Square,
                 accum_out=sumsq[:, i:i + 1], waits=[f"ldA{i}"], label=f"A_sq{i}")
        p.op("vector", "memset", epst[:], 1e-5)
        p.op("vector", "tensor_scalar", out=m32[:], in0=sums[:], scalar1=1.0 / D,
             scalar2=None, op0=ALU.mult)
        p.op("vector", "tensor_scalar", out=ex2[:], in0=sumsq[:], scalar1=1.0 / D,
             scalar2=None, op0=ALU.mult, waits=[f"A_sq{NTILE - 1}"])
        p.op("vector", "tensor_tensor", out=msq[:], in0=m32[:], in1=m32[:], op=ALU.mult)
        p.op("vector", "scalar_tensor_tensor", out=var0[:], in0=msq[:], scalar=-1.0,
             in1=ex2[:], op0=ALU.mult, op1=ALU.add, label="B_var")
        p.op("scalar", "activation", out=std32[:], in_=var0[:], func=AF.Sqrt,
             bias=epst[:, 0:1], waits=["B_var"], label="B_std")
        p.op("vector", "reciprocal", out=rstd[:], in_=std32[:], waits=["B_std"],
             label="B_rstd")
        for i in range(NTILE):
            w = [f"C_norm{i - NB}"] if i >= NB else []
            p.dma("sync", ldC.nxt(i), out=xbufC[:, (i % NB) * D:(i % NB + 1) * D],
                  in_=x[i * 128:(i + 1) * 128, :], waits=w, label=f"ldC{i}")
        for i in range(NTILE):
            w = [f"ldC{i}"]
            if i >= NB:
                w.append(f"stC{i - NB}")
            p.op("vector", "tensor_scalar",
                 out=xnout[:, (i % NB) * D:(i % NB + 1) * D],
                 in0=xbufC[:, (i % NB) * D:(i % NB + 1) * D],
                 scalar1=m32[:, i:i + 1], scalar2=rstd[:, i:i + 1],
                 op0=ALU.subtract, op1=ALU.mult, waits=w, label=f"C_norm{i}")
            p.op("vector", "tensor_reduce", out=mxcols[:, i:i + 1],
                 in_=xnout[:, (i % NB) * D:(i % NB + 1) * D], axis=AX.X, op=ALU.max,
                 apply_absolute_value=True)
            p.dma("scalar", stC.nxt(i), out=xn[i * 128:(i + 1) * 128, :],
                  in_=xnout[:, (i % NB) * D:(i % NB + 1) * D],
                  waits=[f"C_norm{i}"], label=f"stC{i}")
        p.op("vector", "tensor_reduce", out=mxfin[:], in_=mxcols[:], axis=AX.X,
             op=ALU.max, apply_absolute_value=True, label="C_mx")
        p.dma("scalar", stM.nxt(0), out=mx, in_=mxfin[:], waits=["C_mx"])
        for wi, (win, wout, flen) in enumerate(wtiles):
            half = wi % 2
            hoff = half * II
            hblk = half * (II // 32)
            nblk = flen // 32
            w = [f"W_q{wi - 2}"] if wi >= 2 else []
            p.dma("sync", ldW.nxt(wi), out=wbuf[:, hoff:hoff + flen], in_=win,
                  waits=w, label=f"ldW{wi}")
            wv = wbuf[:, hoff:hoff + flen].rearrange("q (n c) -> q n c", c=32)
            p.op("vector", "tensor_reduce", out=wmax[:, hblk:hblk + nblk], in_=wv,
                 axis=AX.X, op=ALU.max, apply_absolute_value=True,
                 waits=[f"ldW{wi}"])
            p.op("vector", "tensor_scalar", out=wsc[:, hblk:hblk + nblk],
                 in0=wmax[:, hblk:hblk + nblk], scalar1=1.0 / 7.0, scalar2=1e-12,
                 op0=ALU.mult, op1=ALU.max)
            p.op("vector", "reciprocal", out=winv[:, hblk:hblk + nblk],
                 in_=wsc[:, hblk:hblk + nblk])
            wiv = wi32[:, hoff:hoff + flen].rearrange("q (n c) -> q n c", c=32)
            p.op("vector", "tensor_tensor", out=wiv, in0=wv,
                 in1=winv[:, hblk:hblk + nblk].unsqueeze(2).broadcast_to([128, nblk, 32]),
                 op=ALU.mult)
            wq_w = [f"stW{wi - 2}"] if wi >= 2 else []
            wqv = wqb[:, hoff:hoff + flen].rearrange("q (n c) -> q n c", c=32)
            p.op("vector", "tensor_tensor", out=wqv, in0=wiv,
                 in1=wsc[:, hblk:hblk + nblk].unsqueeze(2).broadcast_to([128, nblk, 32]),
                 op=ALU.mult, waits=wq_w, label=f"W_q{wi}")
            p.dma("scalar", stW.nxt(wi), out=wout, in_=wqb[:, hoff:hoff + flen],
                  waits=[f"W_q{wi}"], label=f"stW{wi}")
        p.dma("sync", ldS.nxt(0), out=aft[:], in_=af, label="ldaf")
        p.dma("sync", ldS.nxt(1), out=bt[:], in_=bsl, label="ldb")
        p.op("vector", "tensor_scalar", out=nega[:], in0=aft[:], scalar1=-1.0,
             scalar2=None, op0=ALU.mult, waits=["ldaf"])
        p.op("vector", "memset", hbuf[:, 0:64], 0.0)
        p.op("vector", "memset", hbuf[:, 64:65], 1.0)
        p.op("vector", "memset", hbuf[:, 65:192], 0.0)
        for t in range(1, 128):
            p.op("vector", "tensor_tensor", out=hdot[:], in0=nega[:],
                 in1=hbuf[:, t:t + 64], op=ALU.mult)
            p.op("vector", "reduce_sum", out=hbuf[:, 64 + t:65 + t], in_=hdot[:],
                 axis=AX.X)
        p.op("vector", "memset", cvt[:], 0.0)
        hrev_in = _ap(hbuf[:], 191, [[192, CPC], [-1, 128]])
        p.op("vector", "tensor_copy", out=cvt[:, 0:128], in_=hrev_in)
        p.op("vector", "tensor_copy", out=cvt[:, 319:383], in_=bt[:], waits=["ldb"])
        p.op("vector", "tensor_copy", out=cvt[:, 703:767], in_=bt[:])
        nega_rev = _ap(nega[:], 63, [[64, CPC], [-1, 64]])
        p.op("vector", "tensor_copy", out=cvt[:, 768:832], in_=nega_rev,
             label="cv_done")
        p.dma("scalar", stV.nxt(0), out=cvec, in_=cvt[:], waits=["cv_done"],
              label="stV")
        for gi in range(16):
            ch0 = gi * 8
            gb = (gi % 2) * 8
            w = ["stV"]
            if gi >= 2:
                w.append(f"psi_rv{gi - 2}")
            in_th = _ap(cvec, ch0 * 1024, [[1, 128], [1024, 8], [1, 128]])
            p.dma("gpsimd", wdm.nxt(2 * gi), out=thtg[:, gb * 128:(gb + 8) * 128]
                  .rearrange("q (c i) -> q c i", c=8), in_=in_th, waits=w,
                  label=f"wdm_t{gi}")
            in_e = _ap(cvec, ch0 * 1024 + 768, [[1, 128], [1024, 8], [1, 64]])
            p.dma("gpsimd", wdm.nxt(2 * gi + 1), out=eg[:, gb * 64:(gb + 8) * 64]
                  .rearrange("q (c i) -> q c i", c=8), in_=in_e, waits=w,
                  label=f"wdm_e{gi}")
            th_rev_in = _ap(thtg[:], gb * 128 + 127,
                            [[2 * 8 * 128, 128], [128, 8], [-1, 128]])
            p.op("vector", "tensor_copy",
                 out=thtr[:].rearrange("q (c i) -> q c i", c=8),
                 in_=th_rev_in, waits=[f"wdm_t{gi}"] +
                 ([f"psi_mm{2 * (gi - 1) + 1}"] if gi >= 1 else []))
            e_rev_in = _ap(eg[:], gb * 64 + 63,
                           [[2 * 8 * 64, 128], [64, 8], [-1, 64]])
            p.op("vector", "tensor_copy",
                 out=egr[:].rearrange("q (c i) -> q c i", c=8),
                 in_=e_rev_in, waits=[f"wdm_e{gi}"], label=f"psi_rv{gi}")
            for half in range(2):
                k = 2 * gi + half
                w2 = [f"psi_rv{gi}"]
                if k >= 2:
                    w2.append(f"psi_ev{k - 2}")
                for c4 in range(4):
                    ch = half * 4 + c4
                    p.op("tensor", "matmul",
                         out=pspsi[:, half * 512 + c4 * 128: half * 512 + (c4 + 1) * 128],
                         lhsT=egr[:, ch * 64:(ch + 1) * 64],
                         rhs=thtr[:, ch * 128:(ch + 1) * 128],
                         start=True, stop=True,
                         waits=(w2 if c4 == 0 else []),
                         label=(f"psi_mm{k}" if c4 == 3 else None))
                w3 = [f"psi_mm{k}"]
                if k >= 2:
                    w3.append(f"stP{k - 2}")
                p.op("vector", "tensor_copy", out=psout[:, half * 512:(half + 1) * 512],
                     in_=pspsi[:, half * 512:(half + 1) * 512], waits=w3,
                     label=f"psi_ev{k}")
                for c4 in range(4):
                    ch4 = k * 4 + c4
                    out_ap = _ap(psiT, ch4 * 8192, [[128, 64], [1, 128]])
                    p.dma("scalar", stP.nxt(k), out=out_ap,
                          in_=psout[:, half * 512 + c4 * 128: half * 512 + (c4 + 1) * 128],
                          waits=[f"psi_ev{k}"],
                          label=(f"stP{k}" if c4 == 3 else None))
        drains = []
        for slots in (stC, stM, stW, stV, stP):
            drains += slots.drains()
        p.op("gpsimd", "engine_nop", waits=drains)

        p.emit(block)
    return nc





def build_L2():
    nc = bass.Bass("TRN2", target_bir_lowering=False)
    xnT = nc.dram_tensor("xnT", [D, T], F32, kind="ExternalInput").ap()
    wuT = nc.dram_tensor("wuT", [D, II], BF16, kind="ExternalInput").ap()
    wgT = nc.dram_tensor("wgT", [D, II], BF16, kind="ExternalInput").ap()
    mx8 = nc.dram_tensor("mx8", [1, NCORE * 128], F32, kind="ExternalInput").ap()
    up = nc.dram_tensor("up", [II, T], BF16, kind="ExternalOutput").ap()
    gate = nc.dram_tensor("gate", [II, T], BF16, kind="ExternalOutput").ap()
    mup = nc.dram_tensor("mup", [128, 1], F32, kind="ExternalOutput").ap()
    mgate = nc.dram_tensor("mgate", [128, 1], F32, kind="ExternalOutput").ap()

    NJ = II // 128

    with ExitStack() as st:
        sb = lambda n, s, d: st.enter_context(nc.sbuf_tensor(n, s, d))
        mxt = sb("mxt", [1, NCORE * 128], F32)
        gmax = sb("gmax", [1, 1], F32)
        scl = sb("scl", [1, 2], F32)
        ones1 = sb("ones1", [1, 128], F32)
        sxb = sb("sxb", [128, 2], F32)
        xbuf = sb("xbuf", [128, 2 * T], F32)
        xi32 = sb("xi32", [128, 2 * T], I32)
        xq = sb("xq", [128, 8 * T], BF16)
        wb = sb("wb", [128, 2 * 2 * 1024], BF16)
        stage = sb("stage", [128, 2 * T], BF16)
        mcols = sb("mcols", [128, 2 * NJ], F32)
        mfin = sb("mfin", [128, 2], F32)
        ps = st.enter_context(nc.psum_tensor("ps", [128, 8 * 512], F32))
        ldM = Slots(nc, st, "ldM", 1)
        ldX = Slots(nc, st, "ldX", 2)
        ldW2 = Slots(nc, st, "ldW2", 4)
        stU = Slots(nc, st, "stU", 2)
        stMx = Slots(nc, st, "stMx", 2)
        block = st.enter_context(nc.Block())
        p = Prog(nc, st)

        p.dma("sync", ldM.nxt(0), out=mxt[:], in_=mx8, label="ldM")
        p.op("vector", "tensor_reduce", out=gmax[:], in_=mxt[:], axis=AX.X,
             op=ALU.max, waits=["ldM"])
        p.op("vector", "tensor_scalar", out=scl[:, 0:1], in0=gmax[:],
             scalar1=1.0 / QMAX, scalar2=1e-12, op0=ALU.mult, op1=ALU.max)
        p.op("vector", "reciprocal", out=scl[:, 1:2], in_=scl[:, 0:1])
        p.op("vector", "memset", ones1[:], 1.0, label="sc_ready")
        p.op("tensor", "matmul", out=ps[:, 0:2], lhsT=ones1[:], rhs=scl[:],
             start=True, stop=True, waits=["sc_ready"], label="sc_mm")
        p.op("vector", "tensor_copy", out=sxb[:], in_=ps[:, 0:2], waits=["sc_mm"],
             label="sc_bcast")

        for kb in range(8):
            w = [f"xq{kb - 2}"] if kb >= 2 else []
            p.dma("sync", ldX.nxt(kb), out=xbuf[:, (kb % 2) * T:(kb % 2 + 1) * T],
                  in_=xnT[kb * 128:(kb + 1) * 128, :], waits=w, label=f"ldX{kb}")
            p.op("vector", "tensor_scalar", out=xi32[:, (kb % 2) * T:(kb % 2 + 1) * T],
                 in0=xbuf[:, (kb % 2) * T:(kb % 2 + 1) * T],
                 scalar1=sxb[:, 1:2], scalar2=None, op0=ALU.mult,
                 waits=[f"ldX{kb}", "sc_bcast"])
            p.op("vector", "tensor_scalar", out=xq[:, kb * T:(kb + 1) * T],
                 in0=xi32[:, (kb % 2) * T:(kb % 2 + 1) * T],
                 scalar1=sxb[:, 0:1], scalar2=None, op0=ALU.mult, label=f"xq{kb}")

        for mi, (wsrc, dst, mret) in enumerate(((wuT, up, mup), (wgT, gate, mgate))):
            for j in range(NJ):
                js = j % 2
                woff = (mi * 2 + js) * 1024
                w = [f"mm{mi}_{j - 2}_l"] if j >= 2 else []
                win = _ap(wsrc, j * 128, [[II, 128], [128 * II, 8], [1, 128]])
                p.dma("sync", ldW2.nxt(mi * 2 + js), out=wb[:, woff:woff + 1024]
                      .rearrange("q (k c) -> q k c", k=8), in_=win, waits=w,
                      label=f"ldw{mi}_{j}")
                for k in range(8):
                    for tt in range(8):
                        wv = [f"ldw{mi}_{j}", f"xq{k}"] if tt == 0 else []
                        if k == 0 and j >= 1 and tt < 8:
                            wv.append(f"ev{mi}_{j - 1}_{tt}")
                        if k == 0 and mi == 1 and j == 0:
                            wv.append(f"ev0_{NJ - 1}_{tt}")
                        if k == 0 and mi == 0 and j == 0 and tt == 0:
                            wv.append("sc_bcast")
                        lastmm = (k == 7 and tt == 7)
                        p.op("tensor", "matmul", out=ps[:, tt * 512:(tt + 1) * 512],
                             lhsT=wb[:, woff + k * 128:woff + (k + 1) * 128],
                             rhs=xq[:, k * T + tt * 512:k * T + (tt + 1) * 512],
                             start=(k == 0), stop=(k == 7), waits=wv,
                             label=(f"mm{mi}_{j}_l" if lastmm else
                                    (f"mm{mi}_{j}_{tt}" if k == 7 else None)))
                    if k == 7:
                        pass
                for tt in range(8):
                    wv = [f"mm{mi}_{j}_{tt}" if tt < 7 else f"mm{mi}_{j}_l"]
                    if j >= 2:
                        wv.append(f"st{mi}_{j - 2}")
                        wv.append(f"red{mi}_{j - 2}")
                    elif mi == 1:
                        wv.append(f"st0_{NJ - 2 + j}")
                        wv.append(f"red0_{NJ - 2 + j}")
                    p.op("vector", "tensor_copy",
                         out=stage[:, js * T + tt * 512:js * T + (tt + 1) * 512],
                         in_=ps[:, tt * 512:(tt + 1) * 512], waits=wv,
                         label=f"ev{mi}_{j}_{tt}")

                p.op("vector", "tensor_reduce",
                     out=mcols[:, mi * NJ + j:mi * NJ + j + 1],
                     in_=stage[:, js * T:(js + 1) * T],
                     axis=AX.X, op=ALU.max, apply_absolute_value=True,
                     label=f"red{mi}_{j}")
                p.dma("scalar", stU.nxt(mi * 2 + js),
                      out=dst[j * 128:(j + 1) * 128, :],
                      in_=stage[:, js * T:(js + 1) * T],
                      waits=[f"ev{mi}_{j}_{tt2}" for tt2 in range(8)],
                      label=f"st{mi}_{j}")
            p.op("vector", "tensor_reduce", out=mfin[:, mi:mi + 1],
                 in_=mcols[:, mi * NJ:(mi + 1) * NJ], axis=AX.X, op=ALU.max,
                 label=f"mfin{mi}")
            p.dma("scalar", stMx.nxt(mi), out=mret, in_=mfin[:, mi:mi + 1],
                  waits=[f"mfin{mi}"])

        drains = []
        for slots in (stU, stMx):
            drains += slots.drains()
        p.op("gpsimd", "engine_nop", waits=drains)
        p.emit(block)
    return nc


def l2_prep(l1res, ab_unused=None):
    import ml_dtypes
    wuT = np.ascontiguousarray(
        np.concatenate([np.asarray(r["wupq"]) for r in l1res], axis=0).T)
    wgT = np.ascontiguousarray(
        np.concatenate([np.asarray(r["wgq"]) for r in l1res], axis=0).T)
    mx8 = np.ascontiguousarray(
        np.stack([np.asarray(r["mx"]).ravel() for r in l1res]).reshape(1, -1))
    in_maps = []
    for c in range(NCORE):
        xnT = np.ascontiguousarray(np.asarray(l1res[c]["xn"]).T)
        in_maps.append({"xnT": xnT, "wuT": wuT, "wgT": wgT, "mx8": _f32(mx8)})
    return in_maps


def run_L2(l1res):
    in_maps = l2_prep(l1res)
    nc = build_L2()
    res = run_bass_kernel_spmd(nc, in_maps, core_ids=_CORES)
    return res.results





DBG_L34 = {'gate': True, 'ssm': True, 'chain': True, 'far': True, 'far_stt': True, 'far_st': True, 'far_mm': True, 'far_ev': True}


def build_L34(part):
    is4 = (part == 1)
    nc = bass.Bass("TRN2", target_bir_lowering=False)
    gate = nc.dram_tensor("gate", [II, T], BF16, kind="ExternalInput").ap()
    mg8 = nc.dram_tensor("mg8", [1, NCORE * 128], F32, kind="ExternalInput").ap()
    xsl = nc.dram_tensor("xsl", [128, 8 * 32 * 64], F32, kind="ExternalInput").ap()
    cvec = nc.dram_tensor("cvec", [CPC, 1024], BF16, kind="ExternalInput").ap()
    psiT = nc.dram_tensor("psiT", [CPC, 64 * 128], BF16, kind="ExternalInput").ap()
    h0r = nc.dram_tensor("h0r", [1, 64], F32, kind="ExternalInput").ap()
    if is4:
        up = nc.dram_tensor("up", [II, T], BF16, kind="ExternalInput").ap()
        mu8 = nc.dram_tensor("mu8", [1, NCORE * 128], F32, kind="ExternalInput").ap()
        m48 = nc.dram_tensor("m48", [1, NCORE * 128], F32, kind="ExternalInput").ap()
        prod = nc.dram_tensor("prod", [II, T], BF16, kind="ExternalOutput").ap()
        mout = nc.dram_tensor("m5", [128, 1], F32, kind="ExternalOutput").ap()
    else:
        mout = nc.dram_tensor("m4", [128, 1], F32, kind="ExternalOutput").ap()
    sa = nc.dram_tensor("sa", [64, 128 * 256], F32, kind="ExternalOutput").ap()

    I16 = mybir.dt.int16
    U16 = mybir.dt.uint16
    ch0 = part * 64
    NK = II // 128
    NH = 2 * NK
    HW = 2048

    with ExitStack() as st:
        sb = lambda n, s, d: st.enter_context(nc.sbuf_tensor(n, s, d))
        mrow = sb("mrow", [1, NCORE * 128], F32)
        sc1 = sb("sc1", [1, 8], F32)
        ones1 = sb("ones1", [1, 128], F32)
        sxb = sb("sxb", [128, 8], F32)
        h0row = sb("h0row", [1, 64], F32)
        h0bc = sb("h0bc", [128, 64], F32)
        c65536 = sb("c65536", [128, 1], F32)
        gbuf = sb("gbuf", [128, 2 * HW], BF16)
        gi16 = sb("gi16", [128, 2 * HW], I16)
        sgf = sb("sgf", [128, 2 * HW], F32)
        su16 = sb("su16", [128, 2 * HW], U16)
        p65 = sb("p65", [128, 2 * HW], F32)
        mcol = sb("mcol", [128, NH], F32)
        mfin = sb("mfin", [128, 1], F32)
        if is4:
            ubuf = sb("ubuf", [128, 2 * HW], BF16)
            ui16 = sb("ui16", [128, 2 * HW], I16)
            ti16 = sb("ti16", [128, 2 * HW], I16)
            prodb = sb("prodb", [128, 2 * HW], BF16)
            m5col = sb("m5col", [128, NH], F32)
        xf32 = sb("xf32", [128, 2048], F32)
        XSL = sb("XSL", [128, 8 * 33 * 64], BF16)
        hk = sb("hk", [128, 8 * 128 * 3], BF16)
        tbm = sb("tbm", [128, 8 * 128 * 3], BF16)
        psib = sb("psib", [128, 16 * 128], BF16)
        psibZ = sb("psibZ", [128, 32 * 128], BF16)
        bd = sb("bd", [128, 16 * 128], BF16)
        ubx = sb("ubx", [128, 2 * 256], BF16)
        vinb = sb("vinb", [128, 32 * 256], BF16)
        qbuf = sb("qbuf", [128, 32 * 128], BF16)
        zbuf = sb("zbuf", [128, 32 * 128], BF16)
        t1s = sb("t1s", [128, 2 * 256], F32)
        sstg = sb("sstg", [128, 2 * 256], F32)
        ps_u = st.enter_context(nc.psum_tensor("ps_u", [128, 1024], F32))
        ps_v = st.enter_context(nc.psum_tensor("ps_v", [128, 1024], F32))
        ps_q = st.enter_context(nc.psum_tensor("ps_q", [128, 1024], F32))
        ps_f = st.enter_context(nc.psum_tensor("ps_f", [128, 1024], F32))
        ldM = Slots(nc, st, "ldM", 3)
        ldG = Slots(nc, st, "ldG", 2)
        ldU = Slots(nc, st, "ldU", 2)
        ldX8 = Slots(nc, st, "ldX8", 1)
        ldPS = Slots(nc, st, "ldPS", 2)
        ldH = Slots(nc, st, "ldH", 1)
        wdm = Slots(nc, st, "wdm", 3)
        stS = Slots(nc, st, "stS", 2)
        stMx = Slots(nc, st, "stMx", 1)
        stP4 = Slots(nc, st, "stP4", 2)
        block = st.enter_context(nc.Block())
        p = Prog(nc, st)

        p.dma("sync", ldM.nxt(0), out=mrow[:], in_=mg8, label="ldmg")
        p.op("vector", "memset", sc1[:], 0.0)
        p.op("vector", "tensor_reduce", out=sc1[:, 6:7], in_=mrow[:],
             axis=AX.X, op=ALU.max, waits=["ldmg"], label="redmg")
        p.op("vector", "tensor_scalar", out=sc1[:, 0:1], in0=sc1[:, 6:7],
             scalar1=1.0 / QMAX, scalar2=1e-12, op0=ALU.mult, op1=ALU.max)
        p.op("vector", "reciprocal", out=sc1[:, 1:2], in_=sc1[:, 0:1])
        if is4:
            p.dma("sync", ldM.nxt(1), out=mrow[:], in_=mu8, waits=["redmg"],
                  label="ldmu")
            p.op("vector", "tensor_reduce", out=sc1[:, 6:7], in_=mrow[:],
                 axis=AX.X, op=ALU.max, waits=["ldmu"], label="redmu")
            p.op("vector", "tensor_scalar", out=sc1[:, 4:5], in0=sc1[:, 6:7],
                 scalar1=1.0 / QMAX, scalar2=1e-12, op0=ALU.mult, op1=ALU.max)
            p.op("vector", "reciprocal", out=sc1[:, 5:6], in_=sc1[:, 4:5])
            p.dma("sync", ldM.nxt(2), out=mrow[:], in_=m48, waits=["redmu"],
                  label="ldm4")
            p.op("vector", "tensor_reduce", out=sc1[:, 7:8], in_=mrow[:],
                 axis=AX.X, op=ALU.max, waits=["ldm4"])
            p.op("vector", "tensor_scalar", out=sc1[:, 7:8], in0=sc1[:, 7:8],
                 scalar1=1.0, scalar2=1e-30, op0=ALU.mult, op1=ALU.max)
            p.op("vector", "reciprocal", out=sc1[:, 2:3], in_=sc1[:, 7:8])
            p.op("vector", "tensor_scalar", out=sc1[:, 2:3], in0=sc1[:, 2:3],
                 scalar1=QMAX, scalar2=None, op0=ALU.mult)
            p.op("vector", "tensor_tensor", out=sc1[:, 3:4], in0=sc1[:, 7:8],
                 in1=sc1[:, 0:1], op=ALU.mult)
            p.op("vector", "tensor_tensor", out=sc1[:, 3:4], in0=sc1[:, 3:4],
                 in1=sc1[:, 4:5], op=ALU.mult)
            p.op("vector", "tensor_scalar", out=sc1[:, 3:4], in0=sc1[:, 3:4],
                 scalar1=1.0 / (65536.0 * QMAX), scalar2=None, op0=ALU.mult)
        p.dma("sync", ldH.nxt(0), out=h0row[:], in_=h0r, label="ldh0")
        p.op("vector", "memset", c65536[:], 65536.0)
        p.op("vector", "memset", ones1[:], 1.0, label="sc_ready")
        p.op("tensor", "matmul", out=ps_q[:, 384:392], lhsT=ones1[:], rhs=sc1[:],
             start=True, stop=True, waits=["sc_ready"], label="sc_mm")
        p.op("tensor", "matmul", out=ps_q[:, 392:456], lhsT=ones1[:], rhs=h0row[:],
             start=True, stop=True, waits=["ldh0"], label="h0_mm")
        p.op("vector", "tensor_copy", out=sxb[:], in_=ps_q[:, 384:392],
             waits=["sc_mm", "h0_mm"], label="sc_bcast")
        p.op("vector", "tensor_copy", out=h0bc[:], in_=ps_q[:, 392:456],
             waits=["h0_mm"], label="h0_bcast")

        for b in range(8):
            p.dma("sync", ldX8.nxt(b), out=xf32[:],
                  in_=xsl[:, b * 2048:(b + 1) * 2048],
                  waits=([f"xc{b - 1}"] if b >= 1 else []), label=f"ldx{b}")
            p.op("vector", "tensor_copy",
                 out=XSL[:, (b * 33 + 1) * 64:(b * 33 + 33) * 64],
                 in_=xf32[:], waits=[f"ldx{b}"], label=f"xc{b}")
        zsl = _ap(XSL[:], 0, [[8 * 33 * 64, 128], [33 * 64, 8], [1, 64]])
        p.op("vector", "memset", zsl, 0.0, label="xz")

        def emit_gate_half(n):
            k, h = n // 2, n % 2
            ns = n % 2
            w = [f"gq{n - 2}"] if n >= 2 else []
            p.dma("sync", ldG.nxt(n), out=gbuf[:, ns * HW:(ns + 1) * HW],
                  in_=gate[k * 128:(k + 1) * 128, h * HW:(h + 1) * HW],
                  waits=w, label=f"ldg{n}")
            gv = gbuf[:, ns * HW:(ns + 1) * HW]
            w2 = [f"ldg{n}", "sc_bcast"]
            if n >= 2:
                w2.append(f"sg{n - 2}")
                w2.append(f"p65_{n - 2}")
            p.op("vector", "tensor_scalar", out=gi16[:, ns * HW:(ns + 1) * HW],
                 in0=gv, scalar1=sxb[:, 1:2], scalar2=None, op0=ALU.mult,
                 waits=w2, label=f"gq{n}")
            p.op("scalar", "activation", out=sgf[:, ns * HW:(ns + 1) * HW],
                 in_=gi16[:, ns * HW:(ns + 1) * HW], func=AF.Sigmoid,
                 scale=sxb[:, 0:1], waits=[f"gq{n}"])
            p.op("scalar", "activation", out=su16[:, ns * HW:(ns + 1) * HW],
                 in_=sgf[:, ns * HW:(ns + 1) * HW], func=AF.Copy,
                 scale=c65536[:, 0:1], label=f"sg{n}")
            p.op("gpsimd", "tensor_tensor", out=p65[:, ns * HW:(ns + 1) * HW],
                 in0=su16[:, ns * HW:(ns + 1) * HW],
                 in1=gi16[:, ns * HW:(ns + 1) * HW], op=ALU.mult,
                 waits=[f"sg{n}"], label=f"p65_{n}")
            if not is4:
                p.op("vector", "tensor_reduce", out=mcol[:, n:n + 1],
                     in_=p65[:, ns * HW:(ns + 1) * HW], axis=AX.X, op=ALU.max,
                     apply_absolute_value=True, waits=[f"p65_{n}"],
                     label=f"gdone{n}")
            else:
                wu_ = [f"uq{n - 2}"] if n >= 2 else []
                p.dma("sync", ldU.nxt(n), out=ubuf[:, ns * HW:(ns + 1) * HW],
                      in_=up[k * 128:(k + 1) * 128, h * HW:(h + 1) * HW],
                      waits=wu_, label=f"ldu{n}")
                p.op("vector", "tensor_scalar", out=ti16[:, ns * HW:(ns + 1) * HW],
                     in0=p65[:, ns * HW:(ns + 1) * HW], scalar1=sxb[:, 2:3],
                     scalar2=None, op0=ALU.mult, waits=[f"p65_{n}"],
                     label=f"tq{n}")
                p.op("vector", "tensor_scalar", out=ui16[:, ns * HW:(ns + 1) * HW],
                     in0=ubuf[:, ns * HW:(ns + 1) * HW], scalar1=sxb[:, 5:6],
                     scalar2=None, op0=ALU.mult, waits=[f"ldu{n}"],
                     label=f"uq{n}")
                wpr = [f"stp{n - 2}"] if n >= 2 else []
                p.op("vector", "scalar_tensor_tensor",
                     out=prodb[:, ns * HW:(ns + 1) * HW],
                     in0=ti16[:, ns * HW:(ns + 1) * HW], scalar=sxb[:, 3:4],
                     in1=ui16[:, ns * HW:(ns + 1) * HW], op0=ALU.mult,
                     op1=ALU.mult, waits=wpr, label=f"pr{n}")
                p.op("vector", "tensor_reduce", out=m5col[:, n:n + 1],
                     in_=prodb[:, ns * HW:(ns + 1) * HW], axis=AX.X, op=ALU.max,
                     apply_absolute_value=True, label=f"gdone{n}")
                p.dma("scalar", stP4.nxt(n),
                      out=prod[k * 128:(k + 1) * 128, h * HW:(h + 1) * HW],
                      in_=prodb[:, ns * HW:(ns + 1) * HW], waits=[f"pr{n}"],
                      label=f"stp{n}")

        def emit_group(wv, g):
            gch = wv * 32 + g * 8
            gid = wv * 4 + g
            for mi2, off in ((0, 256), (1, 512), (2, 0)):
                in_w = _ap(cvec, (ch0 + gch) * 1024 + off,
                           [[1, 128], [1024, 8], [1, 128]])
                w = ["xz"] if gid == 0 and mi2 == 0 else []
                if gid >= 1:
                    w.append(f"grp_mm{gid - 1}")
                p.dma("gpsimd", wdm.nxt(3 * gid + mi2),
                      out=hk[:, mi2 * 1024:(mi2 + 1) * 1024]
                      .rearrange("q (c i) -> q c i", c=8), in_=in_w, waits=w,
                      label=f"wd{gid}_{mi2}")
            for mi2 in range(3):
                rev_in = _ap(hk[:], mi2 * 1024 + 127,
                             [[3 * 1024, 128], [128, 8], [-1, 128]])
                w = [f"wd{gid}_{mi2}"]
                if gid >= 1:
                    w.append(f"grp_mm{gid - 1}")
                p.op("vector", "tensor_copy",
                     out=tbm[:, mi2 * 1024:(mi2 + 1) * 1024]
                     .rearrange("q (c i) -> q c i", c=8),
                     in_=rev_in, waits=w, label=f"rv{gid}_{mi2}")
            for ci in range(8):
                lch = gch + ci
                wch = lch % 32
                par = wch % 2
                pr = wch // 2
                cs = wch % 2
                xb0 = _ap(XSL[:], 64 + lch, [[8 * 33 * 64, 128], [33 * 64, 8], [64, 32]])
                xb1 = _ap(XSL[:], lch, [[8 * 33 * 64, 128], [33 * 64, 8], [64, 32]])
                wv_ = ["xz", f"rv{gid}_0"]
                if gid * 8 + ci >= 2:
                    wv_.append(f"uev{gch + ci - 2}")
                p.op("tensor", "matmul", out=ps_u[:, cs * 512:cs * 512 + 256],
                     lhsT=tbm[:, ci * 128:(ci + 1) * 128],
                     rhs=xb0, start=True, stop=False, waits=wv_)
                p.op("tensor", "matmul", out=ps_u[:, cs * 512:cs * 512 + 256],
                     lhsT=tbm[:, 1024 + ci * 128:1024 + (ci + 1) * 128],
                     rhs=xb1, start=False, stop=True, waits=[f"rv{gid}_1"],
                     label=f"umm{lch}")
                p.op("vector", "tensor_copy", out=ubx[:, cs * 256:(cs + 1) * 256],
                     in_=ps_u[:, cs * 512:cs * 512 + 256], waits=[f"umm{lch}"],
                     label=f"uev{lch}")
                wv2 = [f"uev{lch}", f"rv{gid}_2"]
                if gid * 8 + ci >= 2:
                    wv2.append(f"vev{gch + ci - 2}")
                p.op("tensor", "matmul", out=ps_v[:, cs * 512:cs * 512 + 256],
                     lhsT=tbm[:, 2048 + ci * 128:2048 + (ci + 1) * 128],
                     rhs=ubx[:, cs * 256:(cs + 1) * 256], start=True, stop=True,
                     waits=wv2, label=f"vmm{lch}")
                wq = []
                if gid == 0 and ci < 2:
                    wq += ["sc_bcast", "h0_bcast"]
                if pr >= 2 and par == 0:
                    wq.append(f"qev{wv}_{pr - 2}")
                p.op("tensor", "matmul",
                     out=ps_q[64 * par:64 * par + 64, (pr % 2) * 512:(pr % 2) * 512 + 256],
                     lhsT=tbm[:, 2048 + ci * 128 + 64:2048 + (ci + 1) * 128],
                     rhs=ubx[:, cs * 256:(cs + 1) * 256], start=True, stop=True,
                     tile_position=(0, 64 * par), waits=wq, label=f"qmm{lch}")
                p.op("vector", "tensor_copy",
                     out=vinb[:, wch * 256:(wch + 1) * 256],
                     in_=ps_v[:, cs * 512:cs * 512 + 256], waits=[f"vmm{lch}"],
                     label=f"vev{lch}")
                if par == 1:
                    qo = _ap(qbuf[:], pr * 8, [[32 * 128, 128], [1, 8], [128, 32]])
                    p.op("vector", "tensor_copy", out=qo,
                         in_=ps_q[:, (pr % 2) * 512:(pr % 2) * 512 + 256]
                         .rearrange("q (b c) -> q b c", b=8),
                         waits=[f"qmm{lch - 1}", f"qmm{lch}"], label=f"qev{wv}_{pr}")
                if ci == 7:
                    p.op("gpsimd", "engine_nop", waits=[f"vmm{lch}", f"qmm{lch}"],
                         label=f"grp_mm{gid}")

        def emit_psib_bd(wv):
            in_e = _ap(psiT, (ch0 + 32 * wv) * 8192,
                       [[128, 64], [2 * 8192, 16], [1, 128]])
            wld = [f"fmm{wv - 1}_31"] if (wv >= 1 and DBG_L34['far']) else []
            p.dma("sync", ldPS.nxt(2 * wv),
                  out=psib[0:64, :].rearrange("q (r i) -> q r i", r=16), in_=in_e,
                  waits=wld, label=f"ldpe{wv}")
            in_o = _ap(psiT, (ch0 + 32 * wv + 1) * 8192,
                       [[128, 64], [2 * 8192, 16], [1, 128]])
            p.dma("sync", ldPS.nxt(2 * wv + 1),
                  out=psib[64:128, :].rearrange("q (r i) -> q r i", r=16), in_=in_o,
                  waits=wld, label=f"ldpo{wv}")
            wbd = [f"chaindone{wv - 1}"] if (wv >= 1 and DBG_L34['chain']) else []
            p.op("vector", "memset", bd[:], 0.0, waits=wbd)
            bde_o = _ap(bd[:], 0, [[16 * 128, 64], [128, 16], [1, 64]])
            bde_i = _ap(psib[:], 64, [[16 * 128, 64], [128, 16], [1, 64]])
            p.op("vector", "tensor_copy", out=bde_o, in_=bde_i, waits=[f"ldpe{wv}"])
            bdo_o = _ap(bd[:], 64 * 16 * 128 + 64, [[16 * 128, 64], [128, 16], [1, 64]])
            bdo_i = _ap(psib[:], 64 * 16 * 128 + 64, [[16 * 128, 64], [128, 16], [1, 64]])
            p.op("vector", "tensor_copy", out=bdo_o, in_=bdo_i, waits=[f"ldpo{wv}"],
                 label=f"bd_done{wv}")
            wz = ([f"t1_{wv * 32 - 1}"]
                  if (wv >= 1 and DBG_L34['far'] and DBG_L34.get('far_ev') != 'none')
                  else [])
            p.op("vector", "memset", psibZ[:], 0.0, waits=wz)
            ze_o = _ap(psibZ[:], 0, [[32 * 128, 64], [256, 16], [1, 128]])
            ze_i = _ap(psib[:], 0, [[16 * 128, 64], [128, 16], [1, 128]])
            p.op("vector", "tensor_copy", out=ze_o, in_=ze_i, waits=[f"ldpe{wv}"])
            zo_o = _ap(psibZ[:], 64 * 32 * 128 + 128, [[32 * 128, 64], [256, 16], [1, 128]])
            zo_i = _ap(psib[:], 64 * 16 * 128, [[16 * 128, 64], [128, 16], [1, 128]])
            p.op("vector", "tensor_copy", out=zo_o, in_=zo_i, waits=[f"ldpo{wv}"],
                 label=f"psz_done{wv}")

        def emit_chain(wv):
            p.op("vector", "memset", zbuf[:, 0:128], 0.0,
                 waits=[f"bd_done{wv}", f"qev{wv}_15"], label=f"z{wv}_0")
            for c in range(1, 32):
                for pr in range(16):
                    w = []
                    if pr == 0:
                        w.append(f"z{wv}_{c - 1}")
                    p.op("tensor", "matmul", out=ps_q[:, 256 + pr * 8:256 + (pr + 1) * 8],
                         lhsT=bd[:, pr * 128:(pr + 1) * 128],
                         rhs=zbuf[:, (c - 1) * 128 + pr * 8:(c - 1) * 128 + (pr + 1) * 8],
                         start=True, stop=True, waits=w,
                         label=(f"cmm{wv}_{c}" if pr == 15 else None))
                p.op("vector", "tensor_tensor", out=zbuf[:, c * 128:(c + 1) * 128],
                     in0=ps_q[:, 256:384], in1=qbuf[:, (c - 1) * 128:c * 128], op=ALU.add,
                     waits=[f"cmm{wv}_{c}"], label=f"z{wv}_{c}")
            p.op("gpsimd", "engine_nop", waits=[f"z{wv}_31"], label=f"chaindone{wv}")

        def emit_far(wv):
            for wch in range(32):
                lch = wv * 32 + wch
                par = wch % 2
                pr = wch // 2
                cs = wch % 2
                w = [f"z{wv}_31"] if wch == 0 else []
                if wch >= 2 and DBG_L34['far_ev'] != 'none':
                    w.append(f"t1_{lch - 2}")
                rhs_far = _ap(zbuf[:], pr * 8,
                              [[32 * 128, 128], [1, 8], [128, 32]])
                if wch == 0:
                    w.append(f"psz_done{wv}")
                if DBG_L34['far_mm']:
                    p.op("tensor", "matmul", out=ps_f[:, cs * 512:cs * 512 + 256],
                         lhsT=psibZ[:, wch * 128:(wch + 1) * 128],
                         rhs=rhs_far, start=True, stop=True,
                         waits=w, label=f"fmm{wv}_{wch}")
                else:
                    p.op("tensor", "matmul", out=ps_f[:, cs * 512:cs * 512 + 256],
                         lhsT=psibZ[:, wch * 128:(wch + 1) * 128],
                         rhs=ubx[:, 0:256].rearrange("q (b c) -> q b c", b=8),
                         start=True, stop=True, waits=w, label=f"fmm{wv}_{wch}")
                if DBG_L34['far_ev'] == 'none':
                    continue
                if not DBG_L34['far_ev']:
                    p.op("vector", "tensor_copy",
                         out=t1s[:, cs * 256:(cs + 1) * 256],
                         in_=ps_f[:, cs * 512:cs * 512 + 256],
                         waits=[f"fmm{wv}_{wch}"], label=f"t1_{lch}")
                    p.op("vector", "tensor_copy",
                         out=sstg[:, cs * 256:(cs + 1) * 256],
                         in_=t1s[:, cs * 256:(cs + 1) * 256],
                         waits=[f"t1_{lch}"], label=f"sf{lch}")
                    continue
                xb0 = _ap(XSL[:], 64 + lch,
                          [[8 * 33 * 64, 128], [33 * 64, 8], [64, 32]])
                if DBG_L34['far_stt']:
                    p.op("vector", "scalar_tensor_tensor",
                         out=t1s[:, cs * 256:(cs + 1) * 256],
                         in0=xb0, scalar=h0bc[:, lch:lch + 1],
                         in1=ps_f[:, cs * 512:cs * 512 + 256],
                         op0=ALU.mult, op1=ALU.add,
                         waits=[f"fmm{wv}_{wch}", "h0_bcast"], label=f"t1_{lch}")
                else:
                    p.op("vector", "tensor_copy",
                         out=t1s[:, cs * 256:(cs + 1) * 256],
                         in_=ps_f[:, cs * 512:cs * 512 + 256],
                         waits=[f"fmm{wv}_{wch}", "h0_bcast"], label=f"t1_{lch}")
                wst = [f"t1_{lch}"]
                if lch >= 2 and DBG_L34['far_st']:
                    wst.append(f"sst{lch - 2}")
                p.op("vector", "tensor_tensor",
                     out=sstg[:, cs * 256:(cs + 1) * 256],
                     in0=t1s[:, cs * 256:(cs + 1) * 256],
                     in1=vinb[:, wch * 256:(wch + 1) * 256], op=ALU.add,
                     waits=wst, label=f"sf{lch}")
                if DBG_L34['far_st']:
                    out_sa = _ap(sa, lch * 32768, [[256, 128], [1, 256]])
                    p.dma("scalar", stS.nxt(lch), out=out_sa,
                          in_=sstg[:, cs * 256:(cs + 1) * 256], waits=[f"sf{lch}"],
                          label=f"sst{lch}")

        gate_n = [0]

        def emit_some_gate(upto):
            while gate_n[0] < min(NH, upto):
                emit_gate_half(gate_n[0])
                gate_n[0] += 1

        if DBG_L34['ssm']:
            emit_psib_bd(0)
            for g in range(4):
                if DBG_L34['gate']:
                    emit_some_gate(4 * (g + 1))
                emit_group(0, g)
            if DBG_L34['chain']:
                emit_chain(0)
            if DBG_L34['far']:
                emit_far(0)
            emit_psib_bd(1)
            for g in range(4):
                if DBG_L34['gate']:
                    emit_some_gate(16 + 4 * (g + 1))
                emit_group(1, g)
            if DBG_L34['chain']:
                emit_chain(1)
            if DBG_L34['far']:
                emit_far(1)
        if DBG_L34['gate']:
            emit_some_gate(NH)

        if DBG_L34['gate']:
            p.op("vector", "tensor_reduce", out=mfin[:],
                 in_=(mcol[:] if not is4 else m5col[:]),
                 axis=AX.X, op=ALU.max, waits=[f"gdone{NH - 1}"], label="mfl")
            p.dma("scalar", stMx.nxt(0), out=mout, in_=mfin[:], waits=["mfl"])
        if (not DBG_L34['far']) or (not DBG_L34['far_st']):
            p.op("vector", "memset", sstg[:, 0:256], 0.0, label="sz")
            for lch in range(64):
                p.dma("scalar", stS.nxt(lch), out=_ap(sa, lch * 32768, [[256, 128], [1, 256]]),
                      in_=sstg[:, 0:256], waits=["sz"])
        if not DBG_L34['gate']:
            p.op("vector", "memset", mfin[:], 0.0, label="mz")
            p.dma("scalar", stMx.nxt(1), out=mout, in_=mfin[:], waits=["mz"])
        drains = []
        for slots in (stS, stMx) + ((stP4,) if is4 else ()):
            drains += slots.drains()
        p.op("gpsimd", "engine_nop", waits=drains)
        p.emit(block)
    return nc



def _f32(a):
    return np.ascontiguousarray(a, dtype=np.float32)


def l1_inmap(inputs, c):
    u = np.asarray(inputs["L0_attn_u"])
    ab = np.asarray(inputs["L0_attn_ab"])
    wg = np.asarray(inputs["L0_ffn_w_gate"])
    wu = np.asarray(inputs["L0_ffn_w_up"])
    wd = np.asarray(inputs["L0_ffn_w_down"])
    a = ab[:D]
    b = ab[D:]
    return {
        "x": _f32(u[c]),
        "wup_s": _f32(wu[384 * c:384 * (c + 1)]),
        "wg_s": _f32(wg[384 * c:384 * (c + 1)]),
        "wd_s": _f32(wd[128 * c:128 * (c + 1)]),
        "af": _f32(a[CPC * c:CPC * (c + 1), ::-1]),
        "bsl": _f32(b[CPC * c:CPC * (c + 1), ::-1]),
    }


def run_L1(inputs):
    in_maps = [l1_inmap(inputs, c) for c in range(NCORE)]
    nc = build_L1()
    res = run_bass_kernel_spmd(nc, in_maps, core_ids=_CORES)
    return res.results







def build_L5():
    nc = bass.Bass("TRN2", target_bir_lowering=False)
    prodH = nc.dram_tensor("prodH", [II, T], BF16, kind="ExternalInput").ap()
    m58 = nc.dram_tensor("m58", [1, NCORE * 128], F32, kind="ExternalInput").ap()
    wdT = nc.dram_tensor("wdT", [II, D], BF16, kind="ExternalInput").ap()
    sT = nc.dram_tensor("sT", [D, T], F32, kind="ExternalInput").ap()
    xT = nc.dram_tensor("xT", [D, T], F32, kind="ExternalInput").ap()
    yT = nc.dram_tensor("yT", [D, T], F32, kind="ExternalOutput").ap()

    NKD = II // 128
    I16 = mybir.dt.int16

    with ExitStack() as st:
        sb = lambda n, s, d: st.enter_context(nc.sbuf_tensor(n, s, d))
        mrow = sb("mrow", [1, NCORE * 128], F32)
        sc1 = sb("sc1", [1, 2], F32)
        ones1 = sb("ones1", [1, 128], F32)
        sxb = sb("sxb", [128, 2], F32)
        pbuf = sb("pbuf", [128, 2 * 2048], BF16)
        pi16 = sb("pi16", [128, 2 * 2048], I16)
        OQ = sb("OQ", [128, NKD * 2048], BF16)
        WD = sb("WD", [128, NKD * 1024], BF16)
        xbuf = sb("xbuf", [128, 2 * 512], F32)
        sbuf2 = sb("sbuf2", [128, 2 * 512], F32)
        gst = sb("gst", [128, 2 * 512], F32)
        sgst = sb("sgst", [128, 2 * 512], F32)
        yst = sb("yst", [128, 2 * 512], F32)
        ps = st.enter_context(nc.psum_tensor("ps", [128, 8 * 512], F32))
        ldM = Slots(nc, st, "ldM", 1)
        ldP = Slots(nc, st, "ldP", 2)
        ldW5 = Slots(nc, st, "ldW5", 2)
        ldXS = Slots(nc, st, "ldXS", 4)
        stY = Slots(nc, st, "stY", 2)
        block = st.enter_context(nc.Block())
        p = Prog(nc, st)

        p.dma("sync", ldM.nxt(0), out=mrow[:], in_=m58, label="ldm")
        p.op("vector", "tensor_reduce", out=sc1[:, 1:2], in_=mrow[:], axis=AX.X,
             op=ALU.max, waits=["ldm"])
        p.op("vector", "tensor_scalar", out=sc1[:, 0:1], in0=sc1[:, 1:2],
             scalar1=1.0 / QMAX, scalar2=1e-12, op0=ALU.mult, op1=ALU.max)
        p.op("vector", "reciprocal", out=sc1[:, 1:2], in_=sc1[:, 0:1])
        p.op("vector", "memset", ones1[:], 1.0, label="sc_ready")
        p.op("tensor", "matmul", out=ps[:, 0:2], lhsT=ones1[:], rhs=sc1[:],
             start=True, stop=True, waits=["sc_ready"], label="sc_mm")
        p.op("vector", "tensor_copy", out=sxb[:], in_=ps[:, 0:2], waits=["sc_mm"],
             label="sc_bcast")

        for kb in range(NKD):
            p.dma("sync", ldW5.nxt(kb), out=WD[:, kb * 1024:(kb + 1) * 1024],
                  in_=wdT[kb * 128:(kb + 1) * 128, :], label=f"ldw{kb}")

        for th in range(2):
            for kb in range(NKD):
                n = th * NKD + kb
                w = [f"oq{n - 2}"] if n >= 2 else []
                p.dma("sync", ldP.nxt(n), out=pbuf[:, (n % 2) * 2048:(n % 2 + 1) * 2048],
                      in_=prodH[kb * 128:(kb + 1) * 128, th * 2048:(th + 1) * 2048],
                      waits=w, label=f"ldp{n}")
                p.op("vector", "tensor_scalar",
                     out=pi16[:, (n % 2) * 2048:(n % 2 + 1) * 2048],
                     in0=pbuf[:, (n % 2) * 2048:(n % 2 + 1) * 2048],
                     scalar1=sxb[:, 1:2], scalar2=None, op0=ALU.mult,
                     waits=[f"ldp{n}", "sc_bcast"] +
                     ([f"mmlast{th - 1}"] if (kb == 0 and th > 0) else []))
                p.op("vector", "tensor_scalar",
                     out=OQ[:, kb * 2048:(kb + 1) * 2048],
                     in0=pi16[:, (n % 2) * 2048:(n % 2 + 1) * 2048],
                     scalar1=sxb[:, 0:1], scalar2=None, op0=ALU.mult,
                     label=f"oq{n}")
            for db in range(8):
                dpar = db % 2
                for kb in range(NKD):
                    for tt in range(4):
                        bank = dpar * 4 + tt
                        wv = []
                        if kb == 0:
                            wv = [f"oq{th * NKD + NKD - 1}"]
                            prev = (th * 8 + db) - 2
                            if prev >= 0:
                                wv.append(f"sil{prev}_{tt}")
                            elif th == 0 and db == 0 and tt == 0:
                                wv.append("sc_bcast")
                        lastmm = (kb == NKD - 1 and tt == 3)
                        p.op("tensor", "matmul",
                             out=ps[:, bank * 512:(bank + 1) * 512],
                             lhsT=WD[:, kb * 1024 + db * 128:kb * 1024 + (db + 1) * 128],
                             rhs=OQ[:, kb * 2048 + tt * 512:kb * 2048 + (tt + 1) * 512],
                             start=(kb == 0), stop=(kb == NKD - 1),
                             waits=(wv + [f"ldw{kb}"]) if (tt == 0 and db == 0) else wv,
                             label=(f"mm{th * 8 + db}_{tt}" if kb == NKD - 1 else None))
                    if kb == NKD - 1:
                        pass
                for tt in range(4):
                    bank = dpar * 4 + tt
                    n5 = (th * 8 + db) * 4 + tt
                    gidx = th * 8 + db
                    p.op("scalar", "activation", out=gst[:, (n5 % 2) * 512:(n5 % 2 + 1) * 512],
                         in_=ps[:, bank * 512:(bank + 1) * 512], func=AF.Silu,
                         waits=[f"mm{gidx}_{tt}"] +
                         ([f"yv{n5 - 2}"] if n5 >= 2 else []),
                         label=f"sil{gidx}_{tt}")
                    p.dma("sync", ldXS.nxt(2 * n5),
                          out=xbuf[:, (n5 % 2) * 512:(n5 % 2 + 1) * 512],
                          in_=xT[db * 128:(db + 1) * 128,
                                 th * 2048 + tt * 512:th * 2048 + (tt + 1) * 512],
                          waits=([f"yv{n5 - 2}"] if n5 >= 2 else []),
                          label=f"ldx{n5}")
                    p.dma("sync", ldXS.nxt(2 * n5 + 1),
                          out=sbuf2[:, (n5 % 2) * 512:(n5 % 2 + 1) * 512],
                          in_=sT[db * 128:(db + 1) * 128,
                                 th * 2048 + tt * 512:th * 2048 + (tt + 1) * 512],
                          waits=([f"yv{n5 - 2}"] if n5 >= 2 else []),
                          label=f"lds{n5}")
                    p.op("vector", "tensor_tensor",
                         out=sgst[:, (n5 % 2) * 512:(n5 % 2 + 1) * 512],
                         in0=gst[:, (n5 % 2) * 512:(n5 % 2 + 1) * 512],
                         in1=sbuf2[:, (n5 % 2) * 512:(n5 % 2 + 1) * 512],
                         op=ALU.mult,
                         waits=[f"sil{gidx}_{tt}", f"lds{n5}"] +
                         ([f"sty{n5 - 2}"] if n5 >= 2 else []))
                    p.op("vector", "tensor_tensor",
                         out=yst[:, (n5 % 2) * 512:(n5 % 2 + 1) * 512],
                         in0=sgst[:, (n5 % 2) * 512:(n5 % 2 + 1) * 512],
                         in1=xbuf[:, (n5 % 2) * 512:(n5 % 2 + 1) * 512],
                         op=ALU.add, waits=[f"ldx{n5}"], label=f"yv{n5}")
                    p.dma("scalar", stY.nxt(n5),
                          out=yT[db * 128:(db + 1) * 128,
                                 th * 2048 + tt * 512:th * 2048 + (tt + 1) * 512],
                          in_=yst[:, (n5 % 2) * 512:(n5 % 2 + 1) * 512],
                          waits=[f"yv{n5}"], label=f"sty{n5}")
                if db == 7:
                    p.op("gpsimd", "engine_nop",
                         waits=[f"sil{th * 8 + 7}_{t2}" for t2 in range(4)],
                         label=f"mmlast{th}")
        drains = stY.drains()
        p.op("gpsimd", "engine_nop", waits=drains)
        p.emit(block)
    return nc


def l5_prep(inputs, l1res, l3res, l4res):
    u = np.asarray(inputs["L0_attn_u"])
    m58 = np.ascontiguousarray(
        np.stack([np.asarray(r["m5"]).ravel() for r in l4res]).reshape(1, -1))
    wdT = np.ascontiguousarray(
        np.concatenate([np.asarray(r["wdq"]) for r in l1res], axis=0).T)
    s_full = np.empty((B, T, D), np.float32)
    for c in range(NCORE):
        for part, res in ((0, l3res[c]), (1, l4res[c])):
            sa = np.asarray(res["sa"]).reshape(64, 128, 8, 32)
            s_full[:, :, 128 * c + 64 * part:128 * c + 64 * part + 64] = (
                sa.transpose(2, 3, 1, 0).reshape(8, T, 64))
    in_maps = []
    for c in range(NCORE):
        in_maps.append({
            "prodH": np.asarray(l4res[c]["prod"]),
            "m58": _f32(m58),
            "wdT": wdT,
            "sT": _f32(s_full[c].T),
            "xT": _f32(u[c].T),
        })
    return in_maps


def run_L5(inputs, l1res, l3res, l4res):
    nc = build_L5()
    res = run_bass_kernel_spmd(nc, l5_prep(inputs, l1res, l3res, l4res),
                               core_ids=_CORES)
    return res.results


def l34_prep(inputs, l1res, l2res, part, m4s=None):
    h_0 = np.asarray(inputs["L0_attn_h_0"])
    xn_all = np.stack([np.asarray(r["xn"]) for r in l1res])
    mg8 = np.ascontiguousarray(
        np.stack([np.asarray(r["mgate"]).ravel() for r in l2res]).reshape(1, -1))
    if part == 1:
        mu8 = np.ascontiguousarray(
            np.stack([np.asarray(r["mup"]).ravel() for r in l2res]).reshape(1, -1))
        m48 = np.ascontiguousarray(np.stack(m4s).reshape(1, -1))
    in_maps = []
    for c in range(NCORE):
        ch_lo = 128 * c + 64 * part
        arr = xn_all[:, :, ch_lo:ch_lo + 64]
        arr = arr.reshape(8, 32, 128, 64).transpose(2, 0, 1, 3)
        m = {
            "gate": np.asarray(l2res[c]["gate"]),
            "mg8": _f32(mg8),
            "xsl": _f32(arr.reshape(128, 8 * 32 * 64)),
            "cvec": np.asarray(l1res[c]["cvec"]),
            "psiT": np.asarray(l1res[c]["psiT"]),
            "h0r": _f32(h_0[ch_lo:ch_lo + 64].reshape(1, 64)),
        }
        if part == 1:
            m["up"] = np.asarray(l2res[c]["up"])
            m["mu8"] = _f32(mu8)
            m["m48"] = _f32(m48)
        in_maps.append(m)
    return in_maps


def run_L3(inputs, l1res, l2res):
    nc = build_L34(0)
    res = run_bass_kernel_spmd(nc, l34_prep(inputs, l1res, l2res, 0),
                               core_ids=_CORES)
    return res.results


def run_L4(inputs, l1res, l2res, l3res):
    m4s = [np.asarray(r["m4"]).ravel() for r in l3res]
    nc = build_L34(1)
    res = run_bass_kernel_spmd(nc, l34_prep(inputs, l1res, l2res, 1, m4s),
                               core_ids=_CORES)
    return res.results


def kernel(L0_attn_u, L0_attn_ab, L0_attn_h_0, L0_ffn_w_gate, L0_ffn_w_up,
           L0_ffn_w_down):
    inputs = {
        "L0_attn_u": L0_attn_u,
        "L0_attn_ab": L0_attn_ab,
        "L0_attn_h_0": L0_attn_h_0,
        "L0_ffn_w_gate": L0_ffn_w_gate,
        "L0_ffn_w_up": L0_ffn_w_up,
        "L0_ffn_w_down": L0_ffn_w_down,
    }
    l1 = run_L1(inputs)
    l2 = run_L2(l1)
    l3 = run_L3(inputs, l1, l2)
    l4 = run_L4(inputs, l1, l2, l3)
    l5 = run_L5(inputs, l1, l3, l4)
    y = np.stack([np.ascontiguousarray(np.asarray(r["yT"]).T) for r in l5])
    return y.astype(np.float32)


# version 03
# speedup vs baseline: 1.9089x; 1.1849x over previous
import numpy as np
from contextlib import ExitStack
import concourse.bass as bass
import concourse.mybir as mybir
from concourse.bass_utils import run_bass_kernel_spmd

F32 = mybir.dt.float32
BF16 = mybir.dt.bfloat16
I32 = mybir.dt.int32
AF = mybir.ActivationFunctionType
ALU = mybir.AluOpType
AX = mybir.AxisListType

NCORE = 8
B, T, D, II = 8, 4096, 1024, 3072
CPC = 128
QMAX = 32767.0
NTILE = T // 128

_CORES = list(range(NCORE))
ENGS = ("sync", "vector", "scalar", "tensor", "gpsimd")


def _ap(t, offset, pattern):
    return bass.AP(tensor=t.tensor, offset=offset, ap=pattern)


class Slots:

    def __init__(self, nc, st, name, n):
        self.sems = [st.enter_context(nc.semaphore(f"{name}{k}")) for k in range(n)]
        self.cnt = [0] * n
        self.n = n

    def nxt(self, i):
        s = i % self.n
        self.cnt[s] += 16
        return self.sems[s], self.cnt[s]

    def drains(self):
        return [(self.sems[k], self.cnt[k]) for k in range(self.n) if self.cnt[k]]


CHAIN_ENGS = {"vector"}


class Prog:

    def __init__(self, nc, st):
        self.nc = nc
        self.entries = {e: [] for e in ENGS}
        self.sems = {e: st.enter_context(nc.semaphore("ch_" + e)) for e in ENGS}
        self.counts = {e: 0 for e in ENGS}
        self.labels = {}

    def op(self, eng, fname, *a, waits=(), label=None, **kw):
        self.counts[eng] += 1
        if label is not None:
            assert label not in self.labels, label
            self.labels[label] = (eng, self.counts[eng])
        self.entries[eng].append(("op", tuple(waits), fname, a, kw))

    def dma(self, eng, slot, *a, waits=(), label=None, **kw):
        if label is not None:
            assert label not in self.labels, label
            self.labels[label] = ("dma", slot)
        self.entries[eng].append(("dma", tuple(waits), slot, a, kw))

    def wait_of(self, w):
        if isinstance(w, str):
            tgt = self.labels[w]
            if tgt[0] == "dma":
                return tgt[1]
            eng, cnt = tgt
            return self.sems[eng], cnt
        return w

    def emit(self, block):
        for eng in ENGS:
            if not self.entries[eng]:
                continue
            entries = self.entries[eng]
            sem = self.sems[eng]
            wait_of = self.wait_of

            def body(raw, entries=entries, sem=sem, eng=eng):
                chain = 0
                for kind, waits, f_or_s, a, kw in entries:
                    for w in waits:
                        ws, wv = wait_of(w)
                        raw.wait_ge(ws, wv)
                    if kind == "op":
                        if eng in CHAIN_ENGS and chain:
                            raw.wait_ge(sem, chain)
                        inst = getattr(raw, f_or_s)(*a, **kw)
                        inst.then_inc(sem, 1)
                        chain += 1
                    else:
                        ssem, _ = f_or_s
                        raw.dma_start(*a, **kw).then_inc(ssem, 16)
            getattr(block, eng)(body)



def build_L1():
    nc = bass.Bass("TRN2", target_bir_lowering=False)
    x = nc.dram_tensor("x", [T, D], F32, kind="ExternalInput").ap()
    wup_s = nc.dram_tensor("wup_s", [384, D], F32, kind="ExternalInput").ap()
    wg_s = nc.dram_tensor("wg_s", [384, D], F32, kind="ExternalInput").ap()
    wd_s = nc.dram_tensor("wd_s", [128, II], F32, kind="ExternalInput").ap()
    af = nc.dram_tensor("af", [CPC, 64], F32, kind="ExternalInput").ap()
    bsl = nc.dram_tensor("bsl", [CPC, 64], F32, kind="ExternalInput").ap()

    xn = nc.dram_tensor("xn", [T, D], F32, kind="ExternalOutput").ap()
    mx = nc.dram_tensor("mx", [128, 1], F32, kind="ExternalOutput").ap()
    wupq = nc.dram_tensor("wupq", [384, D], BF16, kind="ExternalOutput").ap()
    wgq = nc.dram_tensor("wgq", [384, D], BF16, kind="ExternalOutput").ap()
    wdq = nc.dram_tensor("wdq", [128, II], BF16, kind="ExternalOutput").ap()
    cvec = nc.dram_tensor("cvec", [CPC, 1024], BF16, kind="ExternalOutput").ap()
    psiT = nc.dram_tensor("psiT", [CPC, 64 * 128], BF16, kind="ExternalOutput").ap()

    NB = 3

    with ExitStack() as st:
        sb = lambda n, s, d: st.enter_context(nc.sbuf_tensor(n, s, d))
        xbufA = sb("xbufA", [128, NB * D], F32)
        xbufC = sb("xbufC", [128, NB * D], F32)
        sums = sb("sums", [128, NTILE], F32)
        sumsq = sb("sumsq", [128, NTILE], F32)
        sqscr = sb("sqscr", [128, D], F32)
        m32 = sb("m32", [128, NTILE], F32)
        ex2 = sb("ex2", [128, NTILE], F32)
        msq = sb("msq", [128, NTILE], F32)
        var0 = sb("var0", [128, NTILE], F32)
        std32 = sb("std32", [128, NTILE], F32)
        rstd = sb("rstd", [128, NTILE], F32)
        epst = sb("epst", [128, 1], F32)
        xnout = sb("xnout", [128, NB * D], F32)
        mxcols = sb("mxcols", [128, NTILE], F32)
        mxfin = sb("mxfin", [128, 1], F32)
        wbuf = sb("wbuf", [128, 2 * II], F32)
        wmax = sb("wmax", [128, 2 * (II // 32)], F32)
        wsc = sb("wsc", [128, 2 * (II // 32)], F32)
        winv = sb("winv", [128, 2 * (II // 32)], F32)
        wi32 = sb("wi32", [128, 2 * II], I32)
        wqb = sb("wqb", [128, 2 * II], BF16)
        aft = sb("aft", [CPC, 64], F32)
        bt = sb("bt", [CPC, 64], F32)
        nega = sb("nega", [CPC, 64], F32)
        hbuf = sb("hbuf", [CPC, 192], F32)
        hdot = sb("hdot", [CPC, 64], F32)
        cvt = sb("cvt", [CPC, 1024], BF16)
        thtg = sb("thtg", [128, 2 * 8 * 128], BF16)
        eg = sb("eg", [128, 2 * 8 * 64], BF16)
        thtr = sb("thtr", [128, 8 * 128], BF16)
        egr = sb("egr", [128, 8 * 64], BF16)
        psout = sb("psout", [64, 2 * 512], BF16)
        pspsi = st.enter_context(nc.psum_tensor("pspsi", [64, 2 * 512], F32))
        ldA = Slots(nc, st, "ldA", NB)
        ldC = Slots(nc, st, "ldC", NB)
        ldW = Slots(nc, st, "ldW", 2)
        ldS = Slots(nc, st, "ldS", 2)
        stC = Slots(nc, st, "stC", NB)
        stM = Slots(nc, st, "stM", 1)
        stW = Slots(nc, st, "stW", 2)
        stV = Slots(nc, st, "stV", 1)
        stP = Slots(nc, st, "stP", 2)
        wdm = Slots(nc, st, "wdm", 4)
        block = st.enter_context(nc.Block())

        p = Prog(nc, st)

        wtiles = [(wup_s[r * 128:(r + 1) * 128, :], wupq[r * 128:(r + 1) * 128, :], D)
                  for r in range(3)]
        wtiles += [(wg_s[r * 128:(r + 1) * 128, :], wgq[r * 128:(r + 1) * 128, :], D)
                   for r in range(3)]
        wtiles.append((wd_s, wdq, II))

        for i in range(NTILE):
            w = [f"A_red{i - NB}", f"A_sq{i - NB}"] if i >= NB else []
            p.dma("sync", ldA.nxt(i), out=xbufA[:, (i % NB) * D:(i % NB + 1) * D],
                  in_=x[i * 128:(i + 1) * 128, :], waits=w, label=f"ldA{i}")
        for i in range(NTILE):
            bufA = xbufA[:, (i % NB) * D:(i % NB + 1) * D]
            p.op("vector", "reduce_sum", out=sums[:, i:i + 1], in_=bufA, axis=AX.X,
                 waits=[f"ldA{i}"], label=f"A_red{i}")
            p.op("scalar", "activation", out=sqscr[:], in_=bufA, func=AF.Square,
                 accum_out=sumsq[:, i:i + 1], waits=[f"ldA{i}"], label=f"A_sq{i}")
        p.op("vector", "memset", epst[:], 1e-5)
        p.op("vector", "tensor_scalar", out=m32[:], in0=sums[:], scalar1=1.0 / D,
             scalar2=None, op0=ALU.mult)
        p.op("vector", "tensor_scalar", out=ex2[:], in0=sumsq[:], scalar1=1.0 / D,
             scalar2=None, op0=ALU.mult, waits=[f"A_sq{NTILE - 1}"])
        p.op("vector", "tensor_tensor", out=msq[:], in0=m32[:], in1=m32[:], op=ALU.mult)
        p.op("vector", "scalar_tensor_tensor", out=var0[:], in0=msq[:], scalar=-1.0,
             in1=ex2[:], op0=ALU.mult, op1=ALU.add, label="B_var")
        p.op("scalar", "activation", out=std32[:], in_=var0[:], func=AF.Sqrt,
             bias=epst[:, 0:1], waits=["B_var"], label="B_std")
        p.op("vector", "reciprocal", out=rstd[:], in_=std32[:], waits=["B_std"],
             label="B_rstd")
        for i in range(NTILE):
            w = [f"C_norm{i - NB}"] if i >= NB else []
            p.dma("sync", ldC.nxt(i), out=xbufC[:, (i % NB) * D:(i % NB + 1) * D],
                  in_=x[i * 128:(i + 1) * 128, :], waits=w, label=f"ldC{i}")
        for i in range(NTILE):
            w = [f"ldC{i}"]
            if i >= NB:
                w.append(f"stC{i - NB}")
            p.op("vector", "tensor_scalar",
                 out=xnout[:, (i % NB) * D:(i % NB + 1) * D],
                 in0=xbufC[:, (i % NB) * D:(i % NB + 1) * D],
                 scalar1=m32[:, i:i + 1], scalar2=rstd[:, i:i + 1],
                 op0=ALU.subtract, op1=ALU.mult, waits=w, label=f"C_norm{i}")
            p.op("vector", "tensor_reduce", out=mxcols[:, i:i + 1],
                 in_=xnout[:, (i % NB) * D:(i % NB + 1) * D], axis=AX.X, op=ALU.max,
                 apply_absolute_value=True)
            p.dma("scalar", stC.nxt(i), out=xn[i * 128:(i + 1) * 128, :],
                  in_=xnout[:, (i % NB) * D:(i % NB + 1) * D],
                  waits=[f"C_norm{i}"], label=f"stC{i}")
        p.op("vector", "tensor_reduce", out=mxfin[:], in_=mxcols[:], axis=AX.X,
             op=ALU.max, apply_absolute_value=True, label="C_mx")
        p.dma("scalar", stM.nxt(0), out=mx, in_=mxfin[:], waits=["C_mx"])
        for wi, (win, wout, flen) in enumerate(wtiles):
            half = wi % 2
            hoff = half * II
            hblk = half * (II // 32)
            nblk = flen // 32
            w = [f"W_q{wi - 2}"] if wi >= 2 else []
            p.dma("sync", ldW.nxt(wi), out=wbuf[:, hoff:hoff + flen], in_=win,
                  waits=w, label=f"ldW{wi}")
            wv = wbuf[:, hoff:hoff + flen].rearrange("q (n c) -> q n c", c=32)
            p.op("vector", "tensor_reduce", out=wmax[:, hblk:hblk + nblk], in_=wv,
                 axis=AX.X, op=ALU.max, apply_absolute_value=True,
                 waits=[f"ldW{wi}"])
            p.op("vector", "tensor_scalar", out=wsc[:, hblk:hblk + nblk],
                 in0=wmax[:, hblk:hblk + nblk], scalar1=1.0 / 7.0, scalar2=1e-12,
                 op0=ALU.mult, op1=ALU.max)
            p.op("vector", "reciprocal", out=winv[:, hblk:hblk + nblk],
                 in_=wsc[:, hblk:hblk + nblk])
            wiv = wi32[:, hoff:hoff + flen].rearrange("q (n c) -> q n c", c=32)
            p.op("vector", "tensor_tensor", out=wiv, in0=wv,
                 in1=winv[:, hblk:hblk + nblk].unsqueeze(2).broadcast_to([128, nblk, 32]),
                 op=ALU.mult)
            wq_w = [f"stW{wi - 2}"] if wi >= 2 else []
            wqv = wqb[:, hoff:hoff + flen].rearrange("q (n c) -> q n c", c=32)
            p.op("vector", "tensor_tensor", out=wqv, in0=wiv,
                 in1=wsc[:, hblk:hblk + nblk].unsqueeze(2).broadcast_to([128, nblk, 32]),
                 op=ALU.mult, waits=wq_w, label=f"W_q{wi}")
            p.dma("scalar", stW.nxt(wi), out=wout, in_=wqb[:, hoff:hoff + flen],
                  waits=[f"W_q{wi}"], label=f"stW{wi}")
        p.dma("sync", ldS.nxt(0), out=aft[:], in_=af, label="ldaf")
        p.dma("sync", ldS.nxt(1), out=bt[:], in_=bsl, label="ldb")
        p.op("vector", "tensor_scalar", out=nega[:], in0=aft[:], scalar1=-1.0,
             scalar2=None, op0=ALU.mult, waits=["ldaf"])
        p.op("vector", "memset", hbuf[:, 0:64], 0.0)
        p.op("vector", "memset", hbuf[:, 64:65], 1.0)
        p.op("vector", "memset", hbuf[:, 65:192], 0.0)
        for t in range(1, 128):
            p.op("vector", "tensor_tensor", out=hdot[:], in0=nega[:],
                 in1=hbuf[:, t:t + 64], op=ALU.mult)
            p.op("vector", "reduce_sum", out=hbuf[:, 64 + t:65 + t], in_=hdot[:],
                 axis=AX.X)
        p.op("vector", "memset", cvt[:], 0.0)
        hrev_in = _ap(hbuf[:], 191, [[192, CPC], [-1, 128]])
        p.op("vector", "tensor_copy", out=cvt[:, 0:128], in_=hrev_in)
        p.op("vector", "tensor_copy", out=cvt[:, 319:383], in_=bt[:], waits=["ldb"])
        p.op("vector", "tensor_copy", out=cvt[:, 703:767], in_=bt[:])
        nega_rev = _ap(nega[:], 63, [[64, CPC], [-1, 64]])
        p.op("vector", "tensor_copy", out=cvt[:, 768:832], in_=nega_rev,
             label="cv_done")
        p.dma("scalar", stV.nxt(0), out=cvec, in_=cvt[:], waits=["cv_done"],
              label="stV")
        for gi in range(16):
            ch0 = gi * 8
            gb = (gi % 2) * 8
            w = ["stV"]
            if gi >= 2:
                w.append(f"psi_rv{gi - 2}")
            in_th = _ap(cvec, ch0 * 1024, [[1, 128], [1024, 8], [1, 128]])
            p.dma("gpsimd", wdm.nxt(2 * gi), out=thtg[:, gb * 128:(gb + 8) * 128]
                  .rearrange("q (c i) -> q c i", c=8), in_=in_th, waits=w,
                  label=f"wdm_t{gi}")
            in_e = _ap(cvec, ch0 * 1024 + 768, [[1, 128], [1024, 8], [1, 64]])
            p.dma("gpsimd", wdm.nxt(2 * gi + 1), out=eg[:, gb * 64:(gb + 8) * 64]
                  .rearrange("q (c i) -> q c i", c=8), in_=in_e, waits=w,
                  label=f"wdm_e{gi}")
            th_rev_in = _ap(thtg[:], gb * 128 + 127,
                            [[2 * 8 * 128, 128], [128, 8], [-1, 128]])
            p.op("vector", "tensor_copy",
                 out=thtr[:].rearrange("q (c i) -> q c i", c=8),
                 in_=th_rev_in, waits=[f"wdm_t{gi}"] +
                 ([f"psi_mm{2 * (gi - 1) + 1}"] if gi >= 1 else []))
            e_rev_in = _ap(eg[:], gb * 64 + 63,
                           [[2 * 8 * 64, 128], [64, 8], [-1, 64]])
            p.op("vector", "tensor_copy",
                 out=egr[:].rearrange("q (c i) -> q c i", c=8),
                 in_=e_rev_in, waits=[f"wdm_e{gi}"], label=f"psi_rv{gi}")
            for half in range(2):
                k = 2 * gi + half
                w2 = [f"psi_rv{gi}"]
                if k >= 2:
                    w2.append(f"psi_ev{k - 2}")
                for c4 in range(4):
                    ch = half * 4 + c4
                    p.op("tensor", "matmul",
                         out=pspsi[:, half * 512 + c4 * 128: half * 512 + (c4 + 1) * 128],
                         lhsT=egr[:, ch * 64:(ch + 1) * 64],
                         rhs=thtr[:, ch * 128:(ch + 1) * 128],
                         start=True, stop=True,
                         waits=(w2 if c4 == 0 else []),
                         label=(f"psi_mm{k}" if c4 == 3 else None))
                w3 = [f"psi_mm{k}"]
                if k >= 2:
                    w3.append(f"stP{k - 2}")
                p.op("vector", "tensor_copy", out=psout[:, half * 512:(half + 1) * 512],
                     in_=pspsi[:, half * 512:(half + 1) * 512], waits=w3,
                     label=f"psi_ev{k}")
                for c4 in range(4):
                    ch4 = k * 4 + c4
                    out_ap = _ap(psiT, ch4 * 8192, [[128, 64], [1, 128]])
                    p.dma("scalar", stP.nxt(k), out=out_ap,
                          in_=psout[:, half * 512 + c4 * 128: half * 512 + (c4 + 1) * 128],
                          waits=[f"psi_ev{k}"],
                          label=(f"stP{k}" if c4 == 3 else None))
        drains = []
        for slots in (stC, stM, stW, stV, stP):
            drains += slots.drains()
        p.op("gpsimd", "engine_nop", waits=drains)

        p.emit(block)
    return nc





def build_L2():
    nc = bass.Bass("TRN2", target_bir_lowering=False)
    xnT = nc.dram_tensor("xnT", [D, T], F32, kind="ExternalInput").ap()
    wuT = nc.dram_tensor("wuT", [D, II], BF16, kind="ExternalInput").ap()
    wgT = nc.dram_tensor("wgT", [D, II], BF16, kind="ExternalInput").ap()
    mx8 = nc.dram_tensor("mx8", [1, NCORE * 128], F32, kind="ExternalInput").ap()
    up = nc.dram_tensor("up", [II, T], BF16, kind="ExternalOutput").ap()
    gate = nc.dram_tensor("gate", [II, T], BF16, kind="ExternalOutput").ap()
    mup = nc.dram_tensor("mup", [128, 1], F32, kind="ExternalOutput").ap()
    mgate = nc.dram_tensor("mgate", [128, 1], F32, kind="ExternalOutput").ap()

    NJ = II // 128

    with ExitStack() as st:
        sb = lambda n, s, d: st.enter_context(nc.sbuf_tensor(n, s, d))
        mxt = sb("mxt", [1, NCORE * 128], F32)
        gmax = sb("gmax", [1, 1], F32)
        scl = sb("scl", [1, 2], F32)
        ones1 = sb("ones1", [1, 128], F32)
        sxb = sb("sxb", [128, 2], F32)
        xbuf = sb("xbuf", [128, 2 * T], F32)
        xi32 = sb("xi32", [128, 2 * T], I32)
        xq = sb("xq", [128, 8 * T], BF16)
        wb = sb("wb", [128, 2 * 2 * 1024], BF16)
        stage = sb("stage", [128, 2 * T], BF16)
        mcols = sb("mcols", [128, 2 * NJ], F32)
        mfin = sb("mfin", [128, 2], F32)
        ps = st.enter_context(nc.psum_tensor("ps", [128, 8 * 512], F32))
        ldM = Slots(nc, st, "ldM", 1)
        ldX = Slots(nc, st, "ldX", 2)
        ldW2 = Slots(nc, st, "ldW2", 4)
        stU = Slots(nc, st, "stU", 2)
        stMx = Slots(nc, st, "stMx", 2)
        block = st.enter_context(nc.Block())
        p = Prog(nc, st)

        p.dma("sync", ldM.nxt(0), out=mxt[:], in_=mx8, label="ldM")
        p.op("vector", "tensor_reduce", out=gmax[:], in_=mxt[:], axis=AX.X,
             op=ALU.max, waits=["ldM"])
        p.op("vector", "tensor_scalar", out=scl[:, 0:1], in0=gmax[:],
             scalar1=1.0 / QMAX, scalar2=1e-12, op0=ALU.mult, op1=ALU.max)
        p.op("vector", "reciprocal", out=scl[:, 1:2], in_=scl[:, 0:1])
        p.op("vector", "memset", ones1[:], 1.0, label="sc_ready")
        p.op("tensor", "matmul", out=ps[:, 0:2], lhsT=ones1[:], rhs=scl[:],
             start=True, stop=True, waits=["sc_ready"], label="sc_mm")
        p.op("vector", "tensor_copy", out=sxb[:], in_=ps[:, 0:2], waits=["sc_mm"],
             label="sc_bcast")

        for kb in range(8):
            w = [f"xq{kb - 2}"] if kb >= 2 else []
            p.dma("sync", ldX.nxt(kb), out=xbuf[:, (kb % 2) * T:(kb % 2 + 1) * T],
                  in_=xnT[kb * 128:(kb + 1) * 128, :], waits=w, label=f"ldX{kb}")
            p.op("vector", "tensor_scalar", out=xi32[:, (kb % 2) * T:(kb % 2 + 1) * T],
                 in0=xbuf[:, (kb % 2) * T:(kb % 2 + 1) * T],
                 scalar1=sxb[:, 1:2], scalar2=None, op0=ALU.mult,
                 waits=[f"ldX{kb}", "sc_bcast"])
            p.op("vector", "tensor_scalar", out=xq[:, kb * T:(kb + 1) * T],
                 in0=xi32[:, (kb % 2) * T:(kb % 2 + 1) * T],
                 scalar1=sxb[:, 0:1], scalar2=None, op0=ALU.mult, label=f"xq{kb}")

        for mi, (wsrc, dst, mret) in enumerate(((wuT, up, mup), (wgT, gate, mgate))):
            for j in range(NJ):
                js = j % 2
                woff = (mi * 2 + js) * 1024
                w = [f"mm{mi}_{j - 2}_l"] if j >= 2 else []
                win = _ap(wsrc, j * 128, [[II, 128], [128 * II, 8], [1, 128]])
                p.dma("sync", ldW2.nxt(mi * 2 + js), out=wb[:, woff:woff + 1024]
                      .rearrange("q (k c) -> q k c", k=8), in_=win, waits=w,
                      label=f"ldw{mi}_{j}")
                for k in range(8):
                    for tt in range(8):
                        wv = [f"ldw{mi}_{j}", f"xq{k}"] if tt == 0 else []
                        if k == 0 and j >= 1 and tt < 8:
                            wv.append(f"ev{mi}_{j - 1}_{tt}")
                        if k == 0 and mi == 1 and j == 0:
                            wv.append(f"ev0_{NJ - 1}_{tt}")
                        if k == 0 and mi == 0 and j == 0 and tt == 0:
                            wv.append("sc_bcast")
                        lastmm = (k == 7 and tt == 7)
                        p.op("tensor", "matmul", out=ps[:, tt * 512:(tt + 1) * 512],
                             lhsT=wb[:, woff + k * 128:woff + (k + 1) * 128],
                             rhs=xq[:, k * T + tt * 512:k * T + (tt + 1) * 512],
                             start=(k == 0), stop=(k == 7), waits=wv,
                             label=(f"mm{mi}_{j}_l" if lastmm else
                                    (f"mm{mi}_{j}_{tt}" if k == 7 else None)))
                    if k == 7:
                        pass
                for tt in range(8):
                    wv = [f"mm{mi}_{j}_{tt}" if tt < 7 else f"mm{mi}_{j}_l"]
                    if j >= 2:
                        wv.append(f"st{mi}_{j - 2}")
                        wv.append(f"red{mi}_{j - 2}")
                    elif mi == 1:
                        wv.append(f"st0_{NJ - 2 + j}")
                        wv.append(f"red0_{NJ - 2 + j}")
                    p.op("vector", "tensor_copy",
                         out=stage[:, js * T + tt * 512:js * T + (tt + 1) * 512],
                         in_=ps[:, tt * 512:(tt + 1) * 512], waits=wv,
                         label=f"ev{mi}_{j}_{tt}")

                p.op("vector", "tensor_reduce",
                     out=mcols[:, mi * NJ + j:mi * NJ + j + 1],
                     in_=stage[:, js * T:(js + 1) * T],
                     axis=AX.X, op=ALU.max, apply_absolute_value=True,
                     label=f"red{mi}_{j}")
                p.dma("scalar", stU.nxt(mi * 2 + js),
                      out=dst[j * 128:(j + 1) * 128, :],
                      in_=stage[:, js * T:(js + 1) * T],
                      waits=[f"ev{mi}_{j}_{tt2}" for tt2 in range(8)],
                      label=f"st{mi}_{j}")
            p.op("vector", "tensor_reduce", out=mfin[:, mi:mi + 1],
                 in_=mcols[:, mi * NJ:(mi + 1) * NJ], axis=AX.X, op=ALU.max,
                 label=f"mfin{mi}")
            p.dma("scalar", stMx.nxt(mi), out=mret, in_=mfin[:, mi:mi + 1],
                  waits=[f"mfin{mi}"])

        drains = []
        for slots in (stU, stMx):
            drains += slots.drains()
        p.op("gpsimd", "engine_nop", waits=drains)
        p.emit(block)
    return nc


def l2_prep(l1res, ab_unused=None):
    import ml_dtypes
    wuT = np.ascontiguousarray(
        np.concatenate([np.asarray(r["wupq"]) for r in l1res], axis=0).T)
    wgT = np.ascontiguousarray(
        np.concatenate([np.asarray(r["wgq"]) for r in l1res], axis=0).T)
    mx8 = np.ascontiguousarray(
        np.stack([np.asarray(r["mx"]).ravel() for r in l1res]).reshape(1, -1))
    in_maps = []
    for c in range(NCORE):
        xnT = np.ascontiguousarray(np.asarray(l1res[c]["xn"]).T)
        in_maps.append({"xnT": xnT, "wuT": wuT, "wgT": wgT, "mx8": _f32(mx8)})
    return in_maps


def run_L2(l1res):
    in_maps = l2_prep(l1res)
    nc = build_L2()
    res = run_bass_kernel_spmd(nc, in_maps, core_ids=_CORES)
    return res.results





DBG_L34 = {'gate': True, 'ssm': True, 'chain': True, 'far': True, 'far_stt': True, 'far_st': True, 'far_mm': True, 'far_ev': True}


def build_L34(part):
    is4 = (part == 1)
    nc = bass.Bass("TRN2", target_bir_lowering=False)
    gate = nc.dram_tensor("gate", [II, T], BF16, kind="ExternalInput").ap()
    mg8 = nc.dram_tensor("mg8", [1, NCORE * 128], F32, kind="ExternalInput").ap()
    xsl = nc.dram_tensor("xsl", [128, 8 * 32 * 64], F32, kind="ExternalInput").ap()
    cvec = nc.dram_tensor("cvec", [CPC, 1024], BF16, kind="ExternalInput").ap()
    psiT = nc.dram_tensor("psiT", [CPC, 64 * 128], BF16, kind="ExternalInput").ap()
    h0r = nc.dram_tensor("h0r", [1, 64], F32, kind="ExternalInput").ap()
    if is4:
        up = nc.dram_tensor("up", [II, T], BF16, kind="ExternalInput").ap()
        mu8 = nc.dram_tensor("mu8", [1, NCORE * 128], F32, kind="ExternalInput").ap()
        m48 = nc.dram_tensor("m48", [1, NCORE * 128], F32, kind="ExternalInput").ap()
        prod = nc.dram_tensor("prod", [II, T], BF16, kind="ExternalOutput").ap()
        mout = nc.dram_tensor("m5", [128, 1], F32, kind="ExternalOutput").ap()
    else:
        mout = nc.dram_tensor("m4", [128, 1], F32, kind="ExternalOutput").ap()
    sa = nc.dram_tensor("sa", [64, 128 * 256], F32, kind="ExternalOutput").ap()
    vind = nc.dram_tensor("vind", [64, 128 * 256], BF16, kind="Internal").ap()

    I16 = mybir.dt.int16
    U16 = mybir.dt.uint16
    ch0 = part * 64
    NK = II // 128
    NH = 2 * NK
    HW = 2048

    with ExitStack() as st:
        sb = lambda n, s, d: st.enter_context(nc.sbuf_tensor(n, s, d))
        mrow = sb("mrow", [1, NCORE * 128], F32)
        sc1 = sb("sc1", [1, 8], F32)
        ones1 = sb("ones1", [1, 128], F32)
        sxb = sb("sxb", [128, 8], F32)
        h0row = sb("h0row", [1, 64], F32)
        h0bc = sb("h0bc", [128, 64], F32)
        c65536 = sb("c65536", [128, 1], F32)
        gbuf = sb("gbuf", [128, 2 * HW], BF16)
        gi16 = sb("gi16", [128, 2 * HW], I16)
        sgf = sb("sgf", [128, 2 * HW], F32)
        su16 = sb("su16", [128, 2 * HW], U16)
        p65 = sb("p65", [128, 2 * HW], BF16)
        mcol = sb("mcol", [128, NH], F32)
        mfin = sb("mfin", [128, 1], F32)
        if is4:
            ubuf = sb("ubuf", [128, 2 * HW], BF16)
            ui16 = sb("ui16", [128, 2 * HW], I16)
            ti16 = sb("ti16", [128, 2 * HW], I16)
            prodb = sb("prodb", [128, 2 * HW], BF16)
            m5col = sb("m5col", [128, NH], F32)
        xf32 = sb("xf32", [128, 2048], F32)
        XSL = sb("XSL", [128, 8 * 33 * 64], BF16)
        hk = sb("hk", [128, 8 * 128 * 3], BF16)
        tbm = sb("tbm", [128, 8 * 128 * 3], BF16)
        psib = sb("psib", [128, 2 * 16 * 128], BF16)
        psibZ = sb("psibZ", [128, 32 * 128], BF16)
        bd = sb("bd", [128, 2 * 16 * 128], BF16)
        ubx = sb("ubx", [128, 2 * 256], BF16)
        vstg = sb("vstg", [128, 4 * 256], BF16)
        vld = sb("vld", [128, 4 * 256], BF16)
        qbuf = sb("qbuf", [128, 2 * 32 * 128], BF16)
        zbuf = sb("zbuf", [128, 2 * 32 * 128], BF16)
        t1s = sb("t1s", [128, 4 * 256], F32)
        sstg = sb("sstg", [128, 2 * 256], F32)
        ps_u = st.enter_context(nc.psum_tensor("ps_u", [128, 1024], F32))
        ps_v = st.enter_context(nc.psum_tensor("ps_v", [128, 1024], F32))
        ps_q = st.enter_context(nc.psum_tensor("ps_q", [128, 1024], F32))
        ps_f = st.enter_context(nc.psum_tensor("ps_f", [128, 1024], F32))
        ldM = Slots(nc, st, "ldM", 3)
        ldG = Slots(nc, st, "ldG", 2)
        ldU = Slots(nc, st, "ldU", 2)
        ldX8 = Slots(nc, st, "ldX8", 1)
        ldPS = Slots(nc, st, "ldPS", 2)
        ldH = Slots(nc, st, "ldH", 1)
        wdm = Slots(nc, st, "wdm", 3)
        stS = Slots(nc, st, "stS", 2)
        stMx = Slots(nc, st, "stMx", 1)
        stP4 = Slots(nc, st, "stP4", 2)
        stVin = Slots(nc, st, "stVin", 4)
        ldVin = Slots(nc, st, "ldVin", 4)
        block = st.enter_context(nc.Block())
        p = Prog(nc, st)

        p.dma("sync", ldM.nxt(0), out=mrow[:], in_=mg8, label="ldmg")
        p.op("vector", "memset", sc1[:], 0.0)
        p.op("vector", "tensor_reduce", out=sc1[:, 6:7], in_=mrow[:],
             axis=AX.X, op=ALU.max, waits=["ldmg"], label="redmg")
        p.op("vector", "tensor_scalar", out=sc1[:, 0:1], in0=sc1[:, 6:7],
             scalar1=1.0 / QMAX, scalar2=1e-12, op0=ALU.mult, op1=ALU.max)
        p.op("vector", "reciprocal", out=sc1[:, 1:2], in_=sc1[:, 0:1])
        if is4:
            p.dma("sync", ldM.nxt(1), out=mrow[:], in_=mu8, waits=["redmg"],
                  label="ldmu")
            p.op("vector", "tensor_reduce", out=sc1[:, 6:7], in_=mrow[:],
                 axis=AX.X, op=ALU.max, waits=["ldmu"], label="redmu")
            p.op("vector", "tensor_scalar", out=sc1[:, 4:5], in0=sc1[:, 6:7],
                 scalar1=1.0 / QMAX, scalar2=1e-12, op0=ALU.mult, op1=ALU.max)
            p.op("vector", "reciprocal", out=sc1[:, 5:6], in_=sc1[:, 4:5])
            p.dma("sync", ldM.nxt(2), out=mrow[:], in_=m48, waits=["redmu"],
                  label="ldm4")
            p.op("vector", "tensor_reduce", out=sc1[:, 7:8], in_=mrow[:],
                 axis=AX.X, op=ALU.max, waits=["ldm4"])
            p.op("vector", "tensor_scalar", out=sc1[:, 7:8], in0=sc1[:, 7:8],
                 scalar1=1.0, scalar2=1e-30, op0=ALU.mult, op1=ALU.max)
            p.op("vector", "reciprocal", out=sc1[:, 2:3], in_=sc1[:, 7:8])
            p.op("vector", "tensor_scalar", out=sc1[:, 2:3], in0=sc1[:, 2:3],
                 scalar1=QMAX, scalar2=None, op0=ALU.mult)
            p.op("vector", "tensor_tensor", out=sc1[:, 3:4], in0=sc1[:, 7:8],
                 in1=sc1[:, 0:1], op=ALU.mult)
            p.op("vector", "tensor_tensor", out=sc1[:, 3:4], in0=sc1[:, 3:4],
                 in1=sc1[:, 4:5], op=ALU.mult)
            p.op("vector", "tensor_scalar", out=sc1[:, 3:4], in0=sc1[:, 3:4],
                 scalar1=1.0 / (65536.0 * QMAX), scalar2=None, op0=ALU.mult)
        p.dma("sync", ldH.nxt(0), out=h0row[:], in_=h0r, label="ldh0")
        p.op("vector", "memset", c65536[:], 65536.0)
        p.op("vector", "memset", ones1[:], 1.0, label="sc_ready")
        p.op("tensor", "matmul", out=ps_q[:, 384:392], lhsT=ones1[:], rhs=sc1[:],
             start=True, stop=True, waits=["sc_ready"], label="sc_mm")
        p.op("tensor", "matmul", out=ps_q[:, 392:456], lhsT=ones1[:], rhs=h0row[:],
             start=True, stop=True, waits=["ldh0"], label="h0_mm")
        p.op("vector", "tensor_copy", out=sxb[:], in_=ps_q[:, 384:392],
             waits=["sc_mm", "h0_mm"], label="sc_bcast")
        p.op("vector", "tensor_copy", out=h0bc[:], in_=ps_q[:, 392:456],
             waits=["h0_mm"], label="h0_bcast")

        for b in range(8):
            p.dma("sync", ldX8.nxt(b), out=xf32[:],
                  in_=xsl[:, b * 2048:(b + 1) * 2048],
                  waits=([f"xc{b - 1}"] if b >= 1 else []), label=f"ldx{b}")
            p.op("vector", "tensor_copy",
                 out=XSL[:, (b * 33 + 1) * 64:(b * 33 + 33) * 64],
                 in_=xf32[:], waits=[f"ldx{b}"], label=f"xc{b}")
        zsl = _ap(XSL[:], 0, [[8 * 33 * 64, 128], [33 * 64, 8], [1, 64]])
        p.op("vector", "memset", zsl, 0.0, label="xz")

        def emit_gate_prod(n):
            k, h = n // 2, n % 2
            ns = n % 2
            w = [f"gq{n - 2}"] if n >= 2 else []
            p.dma("sync", ldG.nxt(n), out=gbuf[:, ns * HW:(ns + 1) * HW],
                  in_=gate[k * 128:(k + 1) * 128, h * HW:(h + 1) * HW],
                  waits=w, label=f"ldg{n}")
            gv = gbuf[:, ns * HW:(ns + 1) * HW]
            w2 = [f"ldg{n}", "sc_bcast"]
            if n >= 2:
                w2.append(f"sg{n - 2}")
                w2.append(f"p65_{n - 2}")
            p.op("vector", "tensor_scalar", out=gi16[:, ns * HW:(ns + 1) * HW],
                 in0=gv, scalar1=sxb[:, 1:2], scalar2=None, op0=ALU.mult,
                 waits=w2, label=f"gq{n}")
            p.op("scalar", "activation", out=sgf[:, ns * HW:(ns + 1) * HW],
                 in_=gi16[:, ns * HW:(ns + 1) * HW], func=AF.Sigmoid,
                 scale=sxb[:, 0:1], waits=[f"gq{n}"])
            p.op("scalar", "activation", out=su16[:, ns * HW:(ns + 1) * HW],
                 in_=sgf[:, ns * HW:(ns + 1) * HW], func=AF.Copy,
                 scale=c65536[:, 0:1], label=f"sg{n}")
            wp = [f"sg{n}"]
            if n >= 2:
                wp.append(f"tq{n - 2}" if is4 else f"gdone{n - 2}")
            p.op("gpsimd", "tensor_tensor", out=p65[:, ns * HW:(ns + 1) * HW],
                 in0=su16[:, ns * HW:(ns + 1) * HW],
                 in1=gi16[:, ns * HW:(ns + 1) * HW], op=ALU.mult,
                 waits=wp, label=f"p65_{n}")
            if is4:
                wu_ = [f"uq{n - 2}"] if n >= 2 else []
                p.dma("sync", ldU.nxt(n), out=ubuf[:, ns * HW:(ns + 1) * HW],
                      in_=up[k * 128:(k + 1) * 128, h * HW:(h + 1) * HW],
                      waits=wu_, label=f"ldu{n}")

        def emit_gate_cons(n):
            k, h = n // 2, n % 2
            ns = n % 2
            if not is4:
                p.op("vector", "tensor_reduce", out=mcol[:, n:n + 1],
                     in_=p65[:, ns * HW:(ns + 1) * HW], axis=AX.X, op=ALU.max,
                     apply_absolute_value=True, waits=[f"p65_{n}"],
                     label=f"gdone{n}")
                return
            p.op("vector", "tensor_scalar", out=ti16[:, ns * HW:(ns + 1) * HW],
                 in0=p65[:, ns * HW:(ns + 1) * HW], scalar1=sxb[:, 2:3],
                 scalar2=None, op0=ALU.mult, waits=[f"p65_{n}"],
                 label=f"tq{n}")
            p.op("vector", "tensor_scalar", out=ui16[:, ns * HW:(ns + 1) * HW],
                 in0=ubuf[:, ns * HW:(ns + 1) * HW], scalar1=sxb[:, 5:6],
                 scalar2=None, op0=ALU.mult, waits=[f"ldu{n}"],
                 label=f"uq{n}")
            wpr = [f"stp{n - 2}"] if n >= 2 else []
            p.op("vector", "scalar_tensor_tensor",
                 out=prodb[:, ns * HW:(ns + 1) * HW],
                 in0=ti16[:, ns * HW:(ns + 1) * HW], scalar=sxb[:, 3:4],
                 in1=ui16[:, ns * HW:(ns + 1) * HW], op0=ALU.mult,
                 op1=ALU.mult, waits=wpr, label=f"pr{n}")
            p.op("vector", "tensor_reduce", out=m5col[:, n:n + 1],
                 in_=prodb[:, ns * HW:(ns + 1) * HW], axis=AX.X, op=ALU.max,
                 apply_absolute_value=True, label=f"gdone{n}")
            p.dma("scalar", stP4.nxt(n),
                  out=prod[k * 128:(k + 1) * 128, h * HW:(h + 1) * HW],
                  in_=prodb[:, ns * HW:(ns + 1) * HW], waits=[f"pr{n}"],
                  label=f"stp{n}")

        _gate_state = {"prod": 0, "cons": 0}

        def emit_gate_half(n):
            del n
            if _gate_state["prod"] < NH:
                emit_gate_prod(_gate_state["prod"])
                _gate_state["prod"] += 1
            if _gate_state["prod"] - _gate_state["cons"] > 2 and _gate_state["cons"] < NH:
                emit_gate_cons(_gate_state["cons"])
                _gate_state["cons"] += 1

        def flush_gate():
            while _gate_state["prod"] < NH:
                emit_gate_prod(_gate_state["prod"])
                _gate_state["prod"] += 1
            while _gate_state["cons"] < NH:
                emit_gate_cons(_gate_state["cons"])
                _gate_state["cons"] += 1

        def emit_group(wv, g):
            gch = wv * 32 + g * 8
            gid = wv * 4 + g
            for mi2, off in ((0, 256), (1, 512), (2, 0)):
                in_w = _ap(cvec, (ch0 + gch) * 1024 + off,
                           [[1, 128], [1024, 8], [1, 128]])
                w = ["xz"] if gid == 0 and mi2 == 0 else []
                if gid >= 1:
                    w.append(f"rv{gid - 1}_{mi2}")
                p.dma("gpsimd", wdm.nxt(3 * gid + mi2),
                      out=hk[:, mi2 * 1024:(mi2 + 1) * 1024]
                      .rearrange("q (c i) -> q c i", c=8), in_=in_w, waits=w,
                      label=f"wd{gid}_{mi2}")
            for mi2 in range(3):
                rev_in = _ap(hk[:], mi2 * 1024 + 127,
                             [[3 * 1024, 128], [128, 8], [-1, 128]])
                w = [f"wd{gid}_{mi2}"]
                if gid >= 1:
                    w.append(f"grp_mm{gid - 1}")
                p.op("vector", "tensor_copy",
                     out=tbm[:, mi2 * 1024:(mi2 + 1) * 1024]
                     .rearrange("q (c i) -> q c i", c=8),
                     in_=rev_in, waits=w, label=f"rv{gid}_{mi2}")

            def emit_u(ci):
                lch = gch + ci
                cs = lch % 2
                xb0 = _ap(XSL[:], 64 + lch, [[8 * 33 * 64, 128], [33 * 64, 8], [64, 32]])
                xb1 = _ap(XSL[:], lch, [[8 * 33 * 64, 128], [33 * 64, 8], [64, 32]])
                wv_ = ["xz", f"rv{gid}_0"]
                if lch >= 2:
                    wv_.append(f"uev{lch - 2}")
                p.op("tensor", "matmul", out=ps_u[:, cs * 512:cs * 512 + 256],
                     lhsT=tbm[:, ci * 128:(ci + 1) * 128],
                     rhs=xb0, start=True, stop=False, waits=wv_)
                p.op("tensor", "matmul", out=ps_u[:, cs * 512:cs * 512 + 256],
                     lhsT=tbm[:, 1024 + ci * 128:1024 + (ci + 1) * 128],
                     rhs=xb1, start=False, stop=True, waits=[f"rv{gid}_1"],
                     label=f"umm{lch}")
                w_e = [f"umm{lch}"]
                if lch >= 2:
                    w_e += [f"vmm{lch - 2}", f"qmm{lch - 2}"]
                p.op("vector", "tensor_copy", out=ubx[:, cs * 256:(cs + 1) * 256],
                     in_=ps_u[:, cs * 512:cs * 512 + 256], waits=w_e,
                     label=f"uev{lch}")

            def emit_v(ci):
                lch = gch + ci
                cs = lch % 2
                wv2 = [f"uev{lch}", f"rv{gid}_2"]
                if lch >= 2:
                    wv2.append(f"vev{lch - 2}")
                p.op("tensor", "matmul", out=ps_v[:, cs * 512:cs * 512 + 256],
                     lhsT=tbm[:, 2048 + ci * 128:2048 + (ci + 1) * 128],
                     rhs=ubx[:, cs * 256:(cs + 1) * 256], start=True, stop=True,
                     waits=wv2, label=f"vmm{lch}")
                w_v = [f"vmm{lch}"]
                if lch >= 4:
                    w_v.append(f"stv{lch - 4}")
                p.op("vector", "tensor_copy",
                     out=vstg[:, (lch % 4) * 256:(lch % 4 + 1) * 256],
                     in_=ps_v[:, cs * 512:cs * 512 + 256], waits=w_v,
                     label=f"vev{lch}")
                p.dma("scalar", stVin.nxt(lch),
                      out=_ap(vind, lch * 32768, [[256, 128], [1, 256]]),
                      in_=vstg[:, (lch % 4) * 256:(lch % 4 + 1) * 256],
                      waits=[f"vev{lch}"], label=f"stv{lch}")

            def emit_q(ci):
                lch = gch + ci
                wch = lch % 32
                par = wch % 2
                pr = wch // 2
                cs = lch % 2
                wq = []
                if lch < 2:
                    wq += ["sc_bcast", "h0_bcast"]
                if pr >= 2 and par == 0:
                    wq.append(f"qev{wv}_{pr - 2}")
                p.op("tensor", "matmul",
                     out=ps_q[64 * par:64 * par + 64, (pr % 2) * 512:(pr % 2) * 512 + 256],
                     lhsT=tbm[:, 2048 + ci * 128 + 64:2048 + (ci + 1) * 128],
                     rhs=ubx[:, cs * 256:(cs + 1) * 256], start=True, stop=True,
                     tile_position=(0, 64 * par), waits=wq, label=f"qmm{lch}")
                if par == 1:
                    qo = _ap(qbuf[:], wv * 4096 + pr * 8,
                             [[2 * 32 * 128, 128], [1, 8], [128, 32]])
                    p.op("vector", "tensor_copy", out=qo,
                         in_=ps_q[:, (pr % 2) * 512:(pr % 2) * 512 + 256]
                         .rearrange("q (b c) -> q b c", b=8),
                         waits=[f"qmm{lch - 1}", f"qmm{lch}"], label=f"qev{wv}_{pr}")

            for step in range(10):
                if step < 8:
                    emit_u(step)
                if 1 <= step <= 8:
                    emit_v(step - 1)
                if 2 <= step <= 9:
                    emit_q(step - 2)
            p.op("gpsimd", "engine_nop",
                 waits=[f"vmm{gch + 7}", f"qmm{gch + 7}"], label=f"grp_mm{gid}")

        def emit_psib_bd(wv):
            in_e = _ap(psiT, (ch0 + 32 * wv) * 8192,
                       [[128, 64], [2 * 8192, 16], [1, 128]])
            p.dma("sync", ldPS.nxt(2 * wv),
                  out=psib[0:64, wv * 2048:(wv + 1) * 2048]
                  .rearrange("q (r i) -> q r i", r=16), in_=in_e,
                  label=f"ldpe{wv}")
            in_o = _ap(psiT, (ch0 + 32 * wv + 1) * 8192,
                       [[128, 64], [2 * 8192, 16], [1, 128]])
            p.dma("sync", ldPS.nxt(2 * wv + 1),
                  out=psib[64:128, wv * 2048:(wv + 1) * 2048]
                  .rearrange("q (r i) -> q r i", r=16), in_=in_o,
                  label=f"ldpo{wv}")
            p.op("vector", "memset", bd[:, wv * 2048:(wv + 1) * 2048], 0.0)
            bde_o = _ap(bd[:], wv * 2048, [[2 * 16 * 128, 64], [128, 16], [1, 64]])
            bde_i = _ap(psib[:], wv * 2048 + 64,
                        [[2 * 16 * 128, 64], [128, 16], [1, 64]])
            p.op("vector", "tensor_copy", out=bde_o, in_=bde_i, waits=[f"ldpe{wv}"])
            bdo_o = _ap(bd[:], 64 * 2 * 16 * 128 + wv * 2048 + 64,
                        [[2 * 16 * 128, 64], [128, 16], [1, 64]])
            bdo_i = _ap(psib[:], 64 * 2 * 16 * 128 + wv * 2048 + 64,
                        [[2 * 16 * 128, 64], [128, 16], [1, 64]])
            p.op("vector", "tensor_copy", out=bdo_o, in_=bdo_i, waits=[f"ldpo{wv}"],
                 label=f"bd_done{wv}")

        def emit_chain_init(wv):
            p.op("vector", "memset", zbuf[:, wv * 4096:wv * 4096 + 128], 0.0,
                 waits=[f"bd_done{wv}", f"qev{wv}_15"], label=f"z{wv}_0")

        def emit_chain_step(wv, c):
            for pr in range(16):
                w = []
                if pr == 0:
                    w.append(f"z{wv}_{c - 1}")
                cb = 256 if wv == 0 else 768
                p.op("tensor", "matmul",
                     out=ps_q[:, cb + pr * 8:cb + (pr + 1) * 8],
                     lhsT=bd[:, wv * 2048 + pr * 128:wv * 2048 + (pr + 1) * 128],
                     rhs=zbuf[:, wv * 4096 + (c - 1) * 128 + pr * 8:
                              wv * 4096 + (c - 1) * 128 + (pr + 1) * 8],
                     start=True, stop=True, waits=w,
                     label=(f"cmm{wv}_{c}" if pr == 15 else None))
            p.op("vector", "tensor_tensor",
                 out=zbuf[:, wv * 4096 + c * 128:wv * 4096 + (c + 1) * 128],
                 in0=ps_q[:, (256 if wv == 0 else 768):(384 if wv == 0 else 896)],
                 in1=qbuf[:, wv * 4096 + (c - 1) * 128:wv * 4096 + c * 128],
                 op=ALU.add, waits=[f"cmm{wv}_{c}"], label=f"z{wv}_{c}")

        def emit_far(wv):
            wz = [f"fmm{wv - 1}_31"] if wv >= 1 else []
            p.op("vector", "memset", psibZ[:], 0.0, waits=wz)
            ze_o = _ap(psibZ[:], 0, [[32 * 128, 64], [256, 16], [1, 128]])
            ze_i = _ap(psib[:], wv * 2048, [[2 * 16 * 128, 64], [128, 16], [1, 128]])
            p.op("vector", "tensor_copy", out=ze_o, in_=ze_i)
            zo_o = _ap(psibZ[:], 64 * 32 * 128 + 128, [[32 * 128, 64], [256, 16], [1, 128]])
            zo_i = _ap(psib[:], 64 * 2 * 16 * 128 + wv * 2048,
                       [[2 * 16 * 128, 64], [128, 16], [1, 128]])
            p.op("vector", "tensor_copy", out=zo_o, in_=zo_i, label=f"psz_done{wv}")
            banks = [ps_u[:, 0:256], ps_u[:, 512:768], ps_v[:, 0:256], ps_v[:, 512:768]]
            for wch in range(32):
                lch = wv * 32 + wch
                cs = wch % 4
                pr = wch // 2
                w = []
                if wch == 0:
                    w += [f"z{wv}_31", f"psz_done{wv}"]
                if wch >= 4:
                    w.append(f"t1_{lch - 4}")
                rhs_far = _ap(zbuf[:], wv * 4096 + pr * 8,
                              [[2 * 32 * 128, 128], [1, 8], [128, 32]])
                p.op("tensor", "matmul", out=banks[cs],
                     lhsT=psibZ[:, wch * 128:(wch + 1) * 128],
                     rhs=rhs_far, start=True, stop=True,
                     waits=w, label=f"fmm{wv}_{wch}")
                p.dma("sync", ldVin.nxt(lch),
                      out=vld[:, (lch % 4) * 256:(lch % 4 + 1) * 256],
                      in_=_ap(vind, lch * 32768, [[256, 128], [1, 256]]),
                      waits=[f"stv{lch}"] + ([f"sf{lch - 4}"] if lch >= 4 else []),
                      label=f"ldv{lch}")
                xb0 = _ap(XSL[:], 64 + lch,
                          [[8 * 33 * 64, 128], [33 * 64, 8], [64, 32]])
                p.op("vector", "scalar_tensor_tensor",
                     out=t1s[:, (lch % 4) * 256:(lch % 4 + 1) * 256],
                     in0=xb0, scalar=h0bc[:, lch:lch + 1],
                     in1=banks[cs],
                     op0=ALU.mult, op1=ALU.add,
                     waits=[f"fmm{wv}_{wch}", "h0_bcast"] +
                     ([f"sf{lch - 4}"] if lch >= 4 else []), label=f"t1_{lch}")
                p.op("vector", "tensor_tensor",
                     out=sstg[:, (lch % 2) * 256:(lch % 2 + 1) * 256],
                     in0=t1s[:, (lch % 4) * 256:(lch % 4 + 1) * 256],
                     in1=vld[:, (lch % 4) * 256:(lch % 4 + 1) * 256], op=ALU.add,
                     waits=[f"t1_{lch}", f"ldv{lch}"] +
                     ([f"sst{lch - 2}"] if lch >= 2 else []), label=f"sf{lch}")
                out_sa = _ap(sa, lch * 32768, [[256, 128], [1, 256]])
                p.dma("scalar", stS.nxt(lch), out=out_sa,
                      in_=sstg[:, (lch % 2) * 256:(lch % 2 + 1) * 256],
                      waits=[f"sf{lch}"], label=f"sst{lch}")

        gate_n = [0]

        def emit_some_gate(upto):
            while _gate_state["prod"] < min(NH, upto) or \
                  (_gate_state["cons"] + 2 < min(NH, upto)):
                emit_gate_half(0)

        emit_psib_bd(0)
        emit_psib_bd(1)
        for gid in range(8):
            emit_some_gate(3 * (gid + 1))
            emit_group(gid // 4, gid % 4)
        emit_chain_init(0)
        emit_chain_init(1)
        for c in range(1, 32):
            emit_chain_step(0, c)
            emit_chain_step(1, c)
            if c % 2 == 0:
                emit_some_gate(24 + c // 2)
        emit_far(0)
        emit_some_gate(44)
        emit_far(1)
        emit_some_gate(NH)
        flush_gate()

        if DBG_L34['gate']:
            p.op("vector", "tensor_reduce", out=mfin[:],
                 in_=(mcol[:] if not is4 else m5col[:]),
                 axis=AX.X, op=ALU.max, waits=[f"gdone{NH - 1}"], label="mfl")
            p.dma("scalar", stMx.nxt(0), out=mout, in_=mfin[:], waits=["mfl"])
        if (not DBG_L34['far']) or (not DBG_L34['far_st']):
            p.op("vector", "memset", sstg[:, 0:256], 0.0, label="sz")
            for lch in range(64):
                p.dma("scalar", stS.nxt(lch), out=_ap(sa, lch * 32768, [[256, 128], [1, 256]]),
                      in_=sstg[:, 0:256], waits=["sz"])
        if not DBG_L34['gate']:
            p.op("vector", "memset", mfin[:], 0.0, label="mz")
            p.dma("scalar", stMx.nxt(1), out=mout, in_=mfin[:], waits=["mz"])
        drains = []
        for slots in (stS, stMx) + ((stP4,) if is4 else ()):
            drains += slots.drains()
        p.op("gpsimd", "engine_nop", waits=drains)
        p.emit(block)
    return nc



def _f32(a):
    return np.ascontiguousarray(a, dtype=np.float32)


def l1_inmap(inputs, c):
    u = np.asarray(inputs["L0_attn_u"])
    ab = np.asarray(inputs["L0_attn_ab"])
    wg = np.asarray(inputs["L0_ffn_w_gate"])
    wu = np.asarray(inputs["L0_ffn_w_up"])
    wd = np.asarray(inputs["L0_ffn_w_down"])
    a = ab[:D]
    b = ab[D:]
    return {
        "x": _f32(u[c]),
        "wup_s": _f32(wu[384 * c:384 * (c + 1)]),
        "wg_s": _f32(wg[384 * c:384 * (c + 1)]),
        "wd_s": _f32(wd[128 * c:128 * (c + 1)]),
        "af": _f32(a[CPC * c:CPC * (c + 1), ::-1]),
        "bsl": _f32(b[CPC * c:CPC * (c + 1), ::-1]),
    }


def run_L1(inputs):
    in_maps = [l1_inmap(inputs, c) for c in range(NCORE)]
    nc = build_L1()
    res = run_bass_kernel_spmd(nc, in_maps, core_ids=_CORES)
    return res.results







def build_L5():
    nc = bass.Bass("TRN2", target_bir_lowering=False)
    prodH = nc.dram_tensor("prodH", [II, T], BF16, kind="ExternalInput").ap()
    m58 = nc.dram_tensor("m58", [1, NCORE * 128], F32, kind="ExternalInput").ap()
    wdT = nc.dram_tensor("wdT", [II, D], BF16, kind="ExternalInput").ap()
    sT = nc.dram_tensor("sT", [D, T], F32, kind="ExternalInput").ap()
    xT = nc.dram_tensor("xT", [D, T], F32, kind="ExternalInput").ap()
    yT = nc.dram_tensor("yT", [D, T], F32, kind="ExternalOutput").ap()

    NKD = II // 128
    I16 = mybir.dt.int16

    with ExitStack() as st:
        sb = lambda n, s, d: st.enter_context(nc.sbuf_tensor(n, s, d))
        mrow = sb("mrow", [1, NCORE * 128], F32)
        sc1 = sb("sc1", [1, 2], F32)
        ones1 = sb("ones1", [1, 128], F32)
        sxb = sb("sxb", [128, 2], F32)
        pbuf = sb("pbuf", [128, 2 * 2048], BF16)
        pi16 = sb("pi16", [128, 2 * 2048], I16)
        OQ = sb("OQ", [128, NKD * 2048], BF16)
        WD = sb("WD", [128, NKD * 1024], BF16)
        xbuf = sb("xbuf", [128, 2 * 512], F32)
        sbuf2 = sb("sbuf2", [128, 2 * 512], F32)
        gst = sb("gst", [128, 2 * 512], F32)
        sgst = sb("sgst", [128, 2 * 512], F32)
        yst = sb("yst", [128, 2 * 512], F32)
        ps = st.enter_context(nc.psum_tensor("ps", [128, 8 * 512], F32))
        ldM = Slots(nc, st, "ldM", 1)
        ldP = Slots(nc, st, "ldP", 2)
        ldW5 = Slots(nc, st, "ldW5", 2)
        ldXS = Slots(nc, st, "ldXS", 4)
        stY = Slots(nc, st, "stY", 2)
        block = st.enter_context(nc.Block())
        p = Prog(nc, st)

        p.dma("sync", ldM.nxt(0), out=mrow[:], in_=m58, label="ldm")
        p.op("vector", "tensor_reduce", out=sc1[:, 1:2], in_=mrow[:], axis=AX.X,
             op=ALU.max, waits=["ldm"])
        p.op("vector", "tensor_scalar", out=sc1[:, 0:1], in0=sc1[:, 1:2],
             scalar1=1.0 / QMAX, scalar2=1e-12, op0=ALU.mult, op1=ALU.max)
        p.op("vector", "reciprocal", out=sc1[:, 1:2], in_=sc1[:, 0:1])
        p.op("vector", "memset", ones1[:], 1.0, label="sc_ready")
        p.op("tensor", "matmul", out=ps[:, 0:2], lhsT=ones1[:], rhs=sc1[:],
             start=True, stop=True, waits=["sc_ready"], label="sc_mm")
        p.op("vector", "tensor_copy", out=sxb[:], in_=ps[:, 0:2], waits=["sc_mm"],
             label="sc_bcast")

        for kb in range(NKD):
            p.dma("sync", ldW5.nxt(kb), out=WD[:, kb * 1024:(kb + 1) * 1024],
                  in_=wdT[kb * 128:(kb + 1) * 128, :], label=f"ldw{kb}")

        for th in range(2):
            for kb in range(NKD):
                n = th * NKD + kb
                w = [f"oq{n - 2}"] if n >= 2 else []
                p.dma("sync", ldP.nxt(n), out=pbuf[:, (n % 2) * 2048:(n % 2 + 1) * 2048],
                      in_=prodH[kb * 128:(kb + 1) * 128, th * 2048:(th + 1) * 2048],
                      waits=w, label=f"ldp{n}")
                p.op("vector", "tensor_scalar",
                     out=pi16[:, (n % 2) * 2048:(n % 2 + 1) * 2048],
                     in0=pbuf[:, (n % 2) * 2048:(n % 2 + 1) * 2048],
                     scalar1=sxb[:, 1:2], scalar2=None, op0=ALU.mult,
                     waits=[f"ldp{n}", "sc_bcast"] +
                     ([f"mmlast{th - 1}"] if (kb == 0 and th > 0) else []))
                p.op("vector", "tensor_scalar",
                     out=OQ[:, kb * 2048:(kb + 1) * 2048],
                     in0=pi16[:, (n % 2) * 2048:(n % 2 + 1) * 2048],
                     scalar1=sxb[:, 0:1], scalar2=None, op0=ALU.mult,
                     label=f"oq{n}")
            for db in range(8):
                dpar = db % 2
                for kb in range(NKD):
                    for tt in range(4):
                        bank = dpar * 4 + tt
                        wv = []
                        if kb == 0:
                            wv = [f"oq{th * NKD + NKD - 1}"]
                            prev = (th * 8 + db) - 2
                            if prev >= 0:
                                wv.append(f"sil{prev}_{tt}")
                            elif th == 0 and db == 0 and tt == 0:
                                wv.append("sc_bcast")
                        lastmm = (kb == NKD - 1 and tt == 3)
                        p.op("tensor", "matmul",
                             out=ps[:, bank * 512:(bank + 1) * 512],
                             lhsT=WD[:, kb * 1024 + db * 128:kb * 1024 + (db + 1) * 128],
                             rhs=OQ[:, kb * 2048 + tt * 512:kb * 2048 + (tt + 1) * 512],
                             start=(kb == 0), stop=(kb == NKD - 1),
                             waits=(wv + [f"ldw{kb}"]) if (tt == 0 and db == 0) else wv,
                             label=(f"mm{th * 8 + db}_{tt}" if kb == NKD - 1 else None))
                    if kb == NKD - 1:
                        pass
                for tt in range(4):
                    bank = dpar * 4 + tt
                    n5 = (th * 8 + db) * 4 + tt
                    gidx = th * 8 + db
                    p.op("scalar", "activation", out=gst[:, (n5 % 2) * 512:(n5 % 2 + 1) * 512],
                         in_=ps[:, bank * 512:(bank + 1) * 512], func=AF.Silu,
                         waits=[f"mm{gidx}_{tt}"] +
                         ([f"yv{n5 - 2}"] if n5 >= 2 else []),
                         label=f"sil{gidx}_{tt}")
                    p.dma("sync", ldXS.nxt(2 * n5),
                          out=xbuf[:, (n5 % 2) * 512:(n5 % 2 + 1) * 512],
                          in_=xT[db * 128:(db + 1) * 128,
                                 th * 2048 + tt * 512:th * 2048 + (tt + 1) * 512],
                          waits=([f"yv{n5 - 2}"] if n5 >= 2 else []),
                          label=f"ldx{n5}")
                    p.dma("sync", ldXS.nxt(2 * n5 + 1),
                          out=sbuf2[:, (n5 % 2) * 512:(n5 % 2 + 1) * 512],
                          in_=sT[db * 128:(db + 1) * 128,
                                 th * 2048 + tt * 512:th * 2048 + (tt + 1) * 512],
                          waits=([f"yv{n5 - 2}"] if n5 >= 2 else []),
                          label=f"lds{n5}")
                    p.op("vector", "tensor_tensor",
                         out=sgst[:, (n5 % 2) * 512:(n5 % 2 + 1) * 512],
                         in0=gst[:, (n5 % 2) * 512:(n5 % 2 + 1) * 512],
                         in1=sbuf2[:, (n5 % 2) * 512:(n5 % 2 + 1) * 512],
                         op=ALU.mult,
                         waits=[f"sil{gidx}_{tt}", f"lds{n5}"] +
                         ([f"sty{n5 - 2}"] if n5 >= 2 else []))
                    p.op("vector", "tensor_tensor",
                         out=yst[:, (n5 % 2) * 512:(n5 % 2 + 1) * 512],
                         in0=sgst[:, (n5 % 2) * 512:(n5 % 2 + 1) * 512],
                         in1=xbuf[:, (n5 % 2) * 512:(n5 % 2 + 1) * 512],
                         op=ALU.add, waits=[f"ldx{n5}"], label=f"yv{n5}")
                    p.dma("scalar", stY.nxt(n5),
                          out=yT[db * 128:(db + 1) * 128,
                                 th * 2048 + tt * 512:th * 2048 + (tt + 1) * 512],
                          in_=yst[:, (n5 % 2) * 512:(n5 % 2 + 1) * 512],
                          waits=[f"yv{n5}"], label=f"sty{n5}")
                if db == 7:
                    p.op("gpsimd", "engine_nop",
                         waits=[f"sil{th * 8 + 7}_{t2}" for t2 in range(4)],
                         label=f"mmlast{th}")
        drains = stY.drains()
        p.op("gpsimd", "engine_nop", waits=drains)
        p.emit(block)
    return nc


def l5_prep(inputs, l1res, l3res, l4res):
    u = np.asarray(inputs["L0_attn_u"])
    m58 = np.ascontiguousarray(
        np.stack([np.asarray(r["m5"]).ravel() for r in l4res]).reshape(1, -1))
    wdT = np.ascontiguousarray(
        np.concatenate([np.asarray(r["wdq"]) for r in l1res], axis=0).T)
    s_full = np.empty((B, T, D), np.float32)
    for c in range(NCORE):
        for part, res in ((0, l3res[c]), (1, l4res[c])):
            sa = np.asarray(res["sa"]).reshape(64, 128, 8, 32)
            s_full[:, :, 128 * c + 64 * part:128 * c + 64 * part + 64] = (
                sa.transpose(2, 3, 1, 0).reshape(8, T, 64))
    in_maps = []
    for c in range(NCORE):
        in_maps.append({
            "prodH": np.asarray(l4res[c]["prod"]),
            "m58": _f32(m58),
            "wdT": wdT,
            "sT": _f32(s_full[c].T),
            "xT": _f32(u[c].T),
        })
    return in_maps


def run_L5(inputs, l1res, l3res, l4res):
    nc = build_L5()
    res = run_bass_kernel_spmd(nc, l5_prep(inputs, l1res, l3res, l4res),
                               core_ids=_CORES)
    return res.results


def l34_prep(inputs, l1res, l2res, part, m4s=None):
    h_0 = np.asarray(inputs["L0_attn_h_0"])
    xn_all = np.stack([np.asarray(r["xn"]) for r in l1res])
    mg8 = np.ascontiguousarray(
        np.stack([np.asarray(r["mgate"]).ravel() for r in l2res]).reshape(1, -1))
    if part == 1:
        mu8 = np.ascontiguousarray(
            np.stack([np.asarray(r["mup"]).ravel() for r in l2res]).reshape(1, -1))
        m48 = np.ascontiguousarray(np.stack(m4s).reshape(1, -1))
    in_maps = []
    for c in range(NCORE):
        ch_lo = 128 * c + 64 * part
        arr = xn_all[:, :, ch_lo:ch_lo + 64]
        arr = arr.reshape(8, 32, 128, 64).transpose(2, 0, 1, 3)
        m = {
            "gate": np.asarray(l2res[c]["gate"]),
            "mg8": _f32(mg8),
            "xsl": _f32(arr.reshape(128, 8 * 32 * 64)),
            "cvec": np.asarray(l1res[c]["cvec"]),
            "psiT": np.asarray(l1res[c]["psiT"]),
            "h0r": _f32(h_0[ch_lo:ch_lo + 64].reshape(1, 64)),
        }
        if part == 1:
            m["up"] = np.asarray(l2res[c]["up"])
            m["mu8"] = _f32(mu8)
            m["m48"] = _f32(m48)
        in_maps.append(m)
    return in_maps


def run_L3(inputs, l1res, l2res):
    nc = build_L34(0)
    res = run_bass_kernel_spmd(nc, l34_prep(inputs, l1res, l2res, 0),
                               core_ids=_CORES)
    return res.results


def run_L4(inputs, l1res, l2res, l3res):
    m4s = [np.asarray(r["m4"]).ravel() for r in l3res]
    nc = build_L34(1)
    res = run_bass_kernel_spmd(nc, l34_prep(inputs, l1res, l2res, 1, m4s),
                               core_ids=_CORES)
    return res.results


def kernel(L0_attn_u, L0_attn_ab, L0_attn_h_0, L0_ffn_w_gate, L0_ffn_w_up,
           L0_ffn_w_down):
    inputs = {
        "L0_attn_u": L0_attn_u,
        "L0_attn_ab": L0_attn_ab,
        "L0_attn_h_0": L0_attn_h_0,
        "L0_ffn_w_gate": L0_ffn_w_gate,
        "L0_ffn_w_up": L0_ffn_w_up,
        "L0_ffn_w_down": L0_ffn_w_down,
    }
    l1 = run_L1(inputs)
    l2 = run_L2(l1)
    l3 = run_L3(inputs, l1, l2)
    l4 = run_L4(inputs, l1, l2, l3)
    l5 = run_L5(inputs, l1, l3, l4)
    y = np.stack([np.ascontiguousarray(np.asarray(r["yT"]).T) for r in l5])
    return y.astype(np.float32)
